# version 02
# baseline (speedup 1.0000x reference)
from contextlib import ExitStack

import os
import numpy as np
import ml_dtypes

import concourse.bass as bass
import concourse.mybir as mybir
from concourse.bass_utils import run_bass_kernel_spmd

F32 = mybir.dt.float32
BF16 = mybir.dt.bfloat16
I32 = mybir.dt.int32
AF = mybir.ActivationFunctionType
ALU = mybir.AluOpType
AXX = mybir.AxisListType.X

D = 1024
NH = 4
CH = D // 128
HID = 3072
HC = HID // 128
PAD = 64
NTAP = 93
QS = 1.0 / 127.0

ENGS = ("sync", "gpsimd", "vector", "scalar", "tensor")


DMA_QUEUES = {"dl": "sync", "dg": "gpsimd", "ds": "scalar"}
DMA_POOL = 8


def _freeze(fn):
    import types
    if fn.__closure__:
        fn = types.FunctionType(
            fn.__code__, fn.__globals__, fn.__name__, fn.__defaults__,
            tuple(types.CellType(c.cell_contents) for c in fn.__closure__))
    return fn


class Prog:

    def __init__(self, nc, es):
        self.nc = nc
        self.recs = {e: [] for e in ENGS}
        self.sems = {}
        self.cnt = {}
        for name in ("v", "s", "t", "g"):
            self.sems[name] = es.enter_context(nc.semaphore("sem_" + name))
            self.cnt[name] = 0
        for q in DMA_QUEUES:
            for i in range(DMA_POOL):
                key = f"{q}{i}"
                self.sems[key] = es.enter_context(nc.semaphore("sem_" + key))
                self.cnt[key] = 0
        self.qidx = {q: 0 for q in DMA_QUEUES}
        self.lastw = {}
        self.waited = {name: set() for name in ("v", "s", "t", "g")}

    def emit(self, eng, fn, inc=None):
        fn = _freeze(fn)
        if inc in DMA_QUEUES:
            q = inc
            key = f"{q}{self.qidx[q] % DMA_POOL}"
            self.qidx[q] += 1
            prev = self.cnt[key]
            if prev > 0:
                self.wait(DMA_QUEUES[q], (key, prev))
            self.cnt[key] = prev + 16
            tok = (key, self.cnt[key])
            self.recs[eng].append(("dma", fn, key))
            return tok
        tok = None
        pval = 0
        if inc is not None:
            self.cnt[inc] += 1
            pval = self.cnt[inc]
            tok = (inc, pval)
        self.recs[eng].append(("op", fn, inc, pval))
        return tok

    def wait(self, eng, tok):
        if tok is None:
            return
        sem, val = tok
        if val <= 0:
            return
        key = (eng, sem)
        if self.lastw.get(key, -1) >= val:
            return
        self.lastw[key] = val
        if sem in self.waited:
            self.waited[sem].add(val)
        self.recs[eng].append(("wait", sem, val))

    def now(self, sem):
        return (sem, self.cnt[sem])

    def wait_queue(self, eng, q):
        for i in range(DMA_POOL):
            key = f"{q}{i}"
            if self.cnt[key] > 0:
                self.wait(eng, (key, self.cnt[key]))

    def barrier(self, bop=None):
        if os.environ.get("BARRIER_OFF") == "1":
            return
        for sem in ("s", "t"):
            self.wait("vector", (sem, self.cnt[sem]))
        for q in DMA_QUEUES:
            self.wait_queue("vector", q)
        btok = self.emit("vector", self._bop, "v")
        for eng in ("sync", "gpsimd", "scalar", "tensor"):
            self.wait(eng, btok)

    def finalize_maps(self):
        keep = {}
        for sem in ("v", "s", "t", "g"):
            kept = set()
            total = self.cnt[sem]
            for w in self.waited[sem]:
                if 0 < w <= total:
                    kept.add(w)
            if total > 0:
                kept.add(total)
            kept = sorted(kept)
            m = {v: i + 1 for i, v in enumerate(kept)}
            keep[sem] = (kept, m)
        return keep


class Bank:
    def __init__(self, prog, handle):
        self.p = prog
        self.h = handle
        self.readers = []
        self.wtok = None

    def begin_write(self):
        for tok in self.readers:
            self.p.wait("tensor", tok)
        self.readers = []

    def end_write(self, tok):
        self.wtok = tok

    def read_from(self, eng):
        self.p.wait(eng, self.wtok)

    def add_reader(self, tok):
        self.readers.append(tok)


def flat(ap):
    try:
        return ap.rearrange("p a b -> p (a b)")
    except ValueError:
        return ap


class _Stop(Exception):
    pass


_LAST = {}


def build(L=2048, taps=(), until=None):
    TH = min(1024, L // 2)
    NHALF = L // TH
    NC = TH // 128
    NT = TH // 512
    FT = 256
    NFT = L // FT

    nc = bass.Bass("TRN2", target_bir_lowering=False,
                   detect_race_conditions=False)
    es = ExitStack()
    P = Prog(nc, es)
    _LAST["nc"] = nc
    _LAST["P"] = P
    _LAST["tapd"] = tapd = {}
    tapset = set(taps)

    dr = lambda n, sh, dt, k: nc.dram_tensor(n, sh, dt, kind=k)
    x_d = dr("x", [L, D], F32, "ExternalInput")
    wq_d = dr("wqT", [CH, 128, D], F32, "ExternalInput")
    wk_d = dr("wkT", [CH, 128, D], F32, "ExternalInput")
    wv_d = dr("wvT", [CH, 128, D], F32, "ExternalInput")
    wo_d = dr("woT", [CH, 128, D], F32, "ExternalInput")
    wgb_d = dr("wgbT", [CH, 128, 24], F32, "ExternalInput")
    diag_d = dr("diag_all", [CH, NTAP, 128, 128], F32, "ExternalInput")
    gbx_d = dr("gb_exp", [128, 20], F32, "ExternalInput")
    ltemp_d = dr("log_temp", [4, 1], F32, "ExternalInput")
    epsp_d = dr("eps_param", [4, 1], F32, "ExternalInput")
    nw_d = dr("nw", [128, 2], F32, "ExternalInput")
    wg_d = dr("wgT", [CH, 128, HID], I32, "ExternalInput")
    wu_d = dr("wuT", [CH, 128, HID], I32, "ExternalInput")
    wd_d = dr("wdT", [HC, 128, D], I32, "ExternalInput")
    out_d = dr("out", [L, D], F32, "ExternalOutput")
    xnew_d = dr("xnew", [L, D], F32, "Internal")
    bsc_d = dr("bscr", [4, 3], F32, "Internal")

    def mktap(name, shape, dt=F32):
        if name in tapset:
            tapd[name] = dr("tap_" + name, shape, dt, "ExternalOutput")

    mktap("ln1f", [NHALF, CH, 128, TH], BF16)
    mktap("qf", [NHALF, CH, 128, TH], BF16)
    mktap("kf", [NHALF, CH, 128, TH], BF16)
    mktap("vf", [NHALF, CH, 128, TH], BF16)
    mktap("ktok", [NHALF, NC, 128, D], BF16)
    mktap("vtok", [NHALF, NC, 128, D], BF16)
    mktap("khatf", [NHALF, CH, 128, TH], BF16)
    mktap("psc", [NHALF, NC, 128, 40], F32)
    mktap("accmix", [NHALF, NC, 128, D], BF16)
    mktap("accfin", [NHALF, NC, 128, D], BF16)
    mktap("of", [NHALF, CH, 128, TH], BF16)
    mktap("xnew", [L, D], F32)
    mktap("ln2f", [NFT, CH, 128, FT], BF16)
    mktap("hbuf", [NFT, HC, 128, FT], BF16)

    eyeb_d = nc.inline_tensor(np.eye(128, dtype=ml_dtypes.bfloat16), "eyeb_c")
    eyef_d = nc.inline_tensor(np.eye(128, dtype=np.float32), "eyef_c")
    negm_d = nc.inline_tensor(-np.tril(np.ones((128, 128), dtype=np.float32), -1),
                              "negm_c")
    ecst_np = np.zeros((128, 2), dtype=np.float32)
    ecst_np[:, 0] = 1e-5
    ecst_np[:, 1] = 1e-6
    ecst_d = nc.inline_tensor(ecst_np, "ecst_c")
    cmT_d = nc.inline_tensor(np.triu(np.ones((128, 128), dtype=np.float32), 0),
                             "cmT_c")

    _side = "left" if os.environ.get("ALL_LEFT") == "1" else "right"
    sbr = lambda n, sh, dt: es.enter_context(nc.sbuf_tensor(n, sh, dt, side=_side))
    eyeb = sbr("eyeb", [128, 128], BF16)
    eyef = sbr("eyef", [128, 128], F32)
    negm = sbr("negm", [128, 128], F32)
    cmT = sbr("cmT", [128, 128], F32)
    gbx = sbr("gbx", [128, 20], F32)
    nwt = sbr("nwt", [128, 2], F32)
    scb = sbr("scb", [128, 12], F32)
    tiny = sbr("tiny", [128, 8], F32)
    stat = sbr("stat", [128, 64], F32)
    scrap = sbr("scrap", [128, 1024], F32)
    ecst = sbr("ecst", [128, 2], F32)
    P._bop = lambda h: h.tensor_copy(scrap[0:1, 0:1], scrap[0:1, 1:2])
    P.vgap = lambda: P.wait("vector", P.now("v"))
    P.sgap = lambda: P.wait("scalar", P.now("s"))
    wgb = sbr("wgb", [128, CH, 24], BF16)
    att = ExitStack()
    sbar = lambda n, sh, dt: att.enter_context(nc.sbuf_tensor(n, sh, dt, side=_side))
    wo = sbar("wo", [128, CH, D], BF16)
    cq = sbar("cq", [128, CH, PAD], BF16)
    ck = sbar("ck", [128, CH, PAD], BF16)
    cv = sbar("cv", [128, CH, PAD], BF16)
    cvp = sbar("cvp", [128, CH, PAD], BF16)
    S32 = sbar("S32", [128, 2 * NH, 256], F32)
    Sbf = sbar("Sbf", [128, 2 * NH, 256], BF16)

    pb = [Bank(P, es.enter_context(nc.psum_tensor(f"pb{i}", [128, 512], F32)))
          for i in range(6)]
    pbT = Bank(P, es.enter_context(nc.psum_tensor("pbT", [128, 1024], BF16)))
    pbS = pbT
    pb7 = Bank(P, es.enter_context(nc.psum_tensor("pb7", [128, 512], F32)))

    lds = []
    _consts = ((eyeb, eyeb_d), (eyef, eyef_d), (negm, negm_d),
               (cmT, cmT_d), (gbx, gbx_d), (ecst, ecst_d))
    if os.environ.get("NO_CONSTS") == "1":
        _consts = ((eyeb, eyeb_d),)
    for dst, src in _consts:
        lds.append(P.emit("sync", lambda h, d=dst, s=src: h.dma_start(
            out=d[:, :], in_=s.ap()[:, :]), "dl"))
    if os.environ.get("NO_CONSTS") != "1":
        lds.append(P.emit("sync", lambda h: h.dma_start(
            out=nwt[:, :], in_=nw_d.ap()[:, :]), "dl"))
    if os.environ.get("NO_LTEMP") != "1":
        lds.append(P.emit("sync", lambda h: h.dma_start(
            out=tiny[0:4, 0:1], in_=ltemp_d.ap()[:, :]), "dl"))
        lds.append(P.emit("sync", lambda h: h.dma_start(
            out=tiny[0:4, 1:2], in_=epsp_d.ap()[:, :]), "dl"))
    if os.environ.get("NO_WOWGB") != "1":
        t_wo = P.emit("gpsimd", lambda h: h.dma_start(
            out=wo[:, :, :], in_=wo_d.ap().transpose([1, 0, 2])), "dg")
        t_wgb = P.emit("gpsimd", lambda h: h.dma_start(
            out=wgb[:, :, :], in_=wgb_d.ap().transpose([1, 0, 2])), "dg")
    else:
        t_wo = t_wgb = None

    for tok in lds:
        P.wait("scalar", tok)
        P.wait("vector", tok)
    P.emit("vector", lambda h: h.memset(scrap[:, :], 0.0), "v")
    if os.environ.get("NO_TINY") != "1":
        P.emit("scalar", lambda h: h.activation(tiny[0:4, 2:3], tiny[0:4, 0:1], AF.Exp), "s")
        P.wait("vector", P.now("s"))
        P.emit("vector", lambda h: h.tensor_scalar(
            tiny[0:4, 5:6], tiny[0:4, 1:2], 0.0, 0.2, ALU.max, ALU.min), "v")
        P.emit("vector", lambda h: h.reciprocal(tiny[0:4, 4:5], tiny[0:4, 2:3]), "v")
        P.emit("vector", lambda h: h.tensor_scalar(
            tiny[0:4, 6:7], tiny[0:4, 5:6], -5.0, 1.0, ALU.mult, ALU.add), "v")
    P.wait("gpsimd", P.now("v"))
    if os.environ.get("NO_SCB") != "1":
        P.emit("gpsimd", lambda h: h.dma_start(out=bsc_d.ap()[:, :], in_=tiny[0:4, 4:7]), "dg")
        P.wait_queue("gpsimd", "dg")
        t_scb = P.emit("gpsimd", lambda h: h.dma_start(
            out=scb[:, 0:12],
            in_=bsc_d.ap().rearrange("h c -> (h c)").unsqueeze(0).broadcast_to([128, 12])),
            "dg")
        P.wait("vector", t_scb)
        P.wait("scalar", t_scb)
    P.wait("vector", t_wo)
    if os.environ.get("NO_FOLD") != "1":
        for kc in range(CH):
            P.emit("vector", lambda h, kc=kc: h.tensor_scalar(
                wo[:, kc, :], wo[:, kc, :], nwt[:, kc % 2:kc % 2 + 1], None, ALU.mult), "v")
    t_wofold = P.now("v")
    if os.environ.get("NO_MS") != "1":
        P.emit("vector", lambda h: h.memset(flat(S32[:, :, :]), 0.0), "v")
        P.emit("vector", lambda h: h.memset(flat(Sbf[:, :, :]), 0.0), "v")
    t_sinit = P.now("v")
    P.wait("tensor", t_wgb)

    def emit_ln_chunk(t_x, xsl, lnt_dst, t_lnt_free):
        P.wait("vector", t_x)
        P.wait("scalar", t_x)
        P.emit("vector", lambda h: h.tensor_reduce(
            stat[:, 0:1], xsl, AXX, ALU.add), "v")
        P.wait("scalar", t_lnt_free)
        t_sq = P.emit("scalar", lambda h: h.activation(
            lnt_dst, xsl, AF.Square, accum_out=stat[:, 1:2]), "s")
        P.wait("vector", t_sq)
        P.vgap()
        P.emit("vector", lambda h: h.tensor_scalar(
            stat[:, 0:1], stat[:, 0:1], 1.0 / D, None, ALU.mult), "v")
        P.emit("vector", lambda h: h.tensor_scalar(
            stat[:, 1:2], stat[:, 1:2], 1.0 / D, None, ALU.mult), "v")
        P.emit("vector", lambda h: h.tensor_tensor(
            stat[:, 2:3], stat[:, 0:1], stat[:, 0:1], ALU.mult), "v")
        P.vgap()
        P.emit("vector", lambda h: h.tensor_tensor(
            stat[:, 2:3], stat[:, 1:2], stat[:, 2:3], ALU.subtract), "v")
        P.wait("scalar", P.now("v"))
        t_rs = P.emit("scalar", lambda h: h.activation(
            stat[:, 2:3], stat[:, 2:3], AF.Sqrt, bias=ecst[:, 0:1]), "s")
        P.wait("vector", t_rs)
        P.emit("vector", lambda h: h.reciprocal(stat[:, 3:4], stat[:, 2:3]), "v")
        P.vgap()
        P.wait("vector", t_lnt_free)
        return P.emit("vector", lambda h: h.tensor_scalar(
            lnt_dst, xsl, stat[:, 0:1], stat[:, 3:4], ALU.subtract, ALU.mult), "v")

    def ckpt(name):
        if until == name:
            raise _Stop

    ckpt("prologue")
    t_scan_prev = t_sinit
    for half in range(NHALF):
        hs = ExitStack()
        sbl = lambda n, sh, dt: hs.enter_context(
            nc.sbuf_tensor(f"{n}_{half}", sh, dt, side="left"))
        ACC = sbl("ACC", [128, NC, D], BF16)
        vtok = sbl("vtok", [128, NC, D], BF16)
        ktok = sbl("ktok", [128, NC, D], BF16)
        khatf = sbl("khatf", [128, CH, TH], BF16)
        psc = sbl("psc", [128, NC, 48], F32)
        qx = ExitStack()
        sbq = lambda n, sh, dt: qx.enter_context(
            nc.sbuf_tensor(f"{n}_{half}", sh, dt, side="left"))
        qf = sbq("qf", [128, CH, PAD + TH], BF16)
        kf = sbq("kf", [128, CH, PAD + TH], BF16)
        vf = sbq("vf", [128, CH, PAD + TH], BF16)

        ax = ExitStack()
        sba = lambda n, sh, dt: ax.enter_context(
            nc.sbuf_tensor(f"{n}_{half}", sh, dt, side="left"))
        xst = sba("xst", [128, 2, D], F32)
        lnt = sba("lnt", [128, D], BF16)
        ln1f = sba("ln1f", [128, CH, TH], BF16)
        war = sba("war", [128, CH, D], BF16)

        t_lnt_free = None
        for tc in range(0 if os.environ.get("LN_OFF") == "1" else NC):
            t0 = half * TH + tc * 128
            t_x = P.emit("sync", lambda h, tc=tc, t0=t0: h.dma_start(
                out=xst[:, tc % 2, :], in_=x_d.ap()[t0:t0 + 128, :]), "dl")
            t_ln = emit_ln_chunk(t_x, xst[:, tc % 2, :], lnt[:, :], t_lnt_free)
            P.wait("tensor", t_ln)
            pbT.begin_write()
            for cc in range(CH):
                tk = P.emit("tensor", lambda h, cc=cc: h.transpose(
                    pbT.h[:, cc * 128:(cc + 1) * 128],
                    lnt[:, cc * 128:(cc + 1) * 128], eyeb[:, :]),
                    "t" if cc == CH - 1 else None)
            pbT.end_write(tk)
            t_lnt_free = tk
            pbT.read_from("scalar")
            tev = P.emit("scalar", lambda h, tc=tc: h.activation(
                flat(ln1f[:, :, tc * 128:(tc + 1) * 128]), pbT.h[:, :], AF.Copy), "s")
            pbT.add_reader(tev)
            if os.environ.get("NO_GATE") == "1":
                continue
            bg = pb[4]
            bg.begin_write()
            P.wait("tensor", tev)
            for kc in range(CH):
                tk = P.emit("tensor", lambda h, tc=tc, kc=kc: h.matmul(
                    bg.h[:, 0:24], ln1f[:, kc, tc * 128:(tc + 1) * 128],
                    wgb[:, kc, :], start=(kc == 0), stop=(kc == CH - 1)),
                    "t" if kc == CH - 1 else None)
            bg.end_write(tk)
            bg.read_from("vector")
            P.wait("vector", P.now("s"))
            t_gl = P.emit("vector", lambda h: h.tensor_tensor(
                stat[:, 16:36], bg.h[:, 0:20], gbx[:, :], ALU.add), "v")
            bg.read_from("scalar")
            t_bt = P.emit("scalar", lambda h, tc=tc: h.activation(
                psc[:, tc, 20:24], bg.h[:, 20:24], AF.Sigmoid), "s")
            bg.add_reader(t_gl)
            bg.add_reader(t_bt)
            P.wait("scalar", t_gl)
            NPH = 0 if os.environ.get("NO_PROBS") == "1" else NH
            if NPH:
                P.vgap()
            for hh in range(NPH):
                P.emit("vector", lambda h, hh=hh: h.reduce_max(
                    stat[:, 36 + hh:37 + hh], stat[:, 16 + 5 * hh:21 + 5 * hh], AXX), "v")
            for hh in range(NPH):
                P.emit("vector", lambda h, hh=hh: h.tensor_scalar(
                    stat[:, 40 + hh:41 + hh], stat[:, 36 + hh:37 + hh],
                    scb[:, 3 * hh:3 * hh + 1], -1.0, ALU.mult, ALU.mult), "v")
            P.wait("scalar", P.now("v"))
            t_e = None
            for hh in range(NPH):
                t_e = P.emit("scalar", lambda h, hh=hh: h.activation(
                    stat[:, 16 + 5 * hh:21 + 5 * hh], stat[:, 16 + 5 * hh:21 + 5 * hh],
                    AF.Exp, bias=stat[:, 40 + hh:41 + hh],
                    scale=scb[:, 3 * hh:3 * hh + 1]), "s")
            P.wait("vector", t_e if NPH else None)
            for hh in range(NPH):
                P.emit("vector", lambda h, hh=hh: h.tensor_reduce(
                    stat[:, 36 + hh:37 + hh], stat[:, 16 + 5 * hh:21 + 5 * hh], AXX,
                    ALU.add), "v")
            for hh in range(NPH):
                P.emit("vector", lambda h, hh=hh: h.reciprocal(
                    stat[:, 40 + hh:41 + hh], stat[:, 36 + hh:37 + hh]), "v")
            for hh in range(NPH):
                P.emit("vector", lambda h, hh=hh: h.tensor_tensor(
                    stat[:, 40 + hh:41 + hh], stat[:, 40 + hh:41 + hh],
                    scb[:, 3 * hh + 2:3 * hh + 3], ALU.mult), "v")
            for hh in range(NPH):
                P.emit("vector", lambda h, tc=tc, hh=hh: h.tensor_scalar(
                    psc[:, tc, 5 * hh:5 * hh + 5], stat[:, 16 + 5 * hh:21 + 5 * hh],
                    stat[:, 40 + hh:41 + hh], scb[:, 3 * hh + 1:3 * hh + 2],
                    ALU.mult, ALU.add), "v")
            if tc + 2 < NC:
                P.wait("sync", P.now("v"))
                P.wait("sync", P.now("s"))
        t_ln_all = P.now("s")
        ckpt("ln" + str(half))

        if "ln1f" in tapd:
            P.wait("scalar", t_ln_all)
            for cc in range(CH):
                P.emit("scalar", lambda h, cc=cc: h.dma_start(
                    out=tapd["ln1f"].ap()[half, cc], in_=ln1f[:, cc, :]), "ds")

        t_war_free = None
        for wi, (wsrc, dst) in enumerate(((wq_d, qf), (wk_d, kf), (wv_d, vf))):
            P.wait("gpsimd", t_war_free)
            t_w = P.emit("gpsimd", lambda h, wsrc=wsrc: h.dma_start(
                out=war[:, :, :], in_=wsrc.ap().transpose([1, 0, 2])), "dg")
            P.wait("tensor", t_w)
            P.wait("tensor", t_ln_all)
            ckpt(f"pwdma{half}_{wi}")
            for tt in range(int(os.environ.get("TT_LIMIT", NT))):
                for mc in range(int(os.environ.get("MC_LIMIT", CH))):
                    bk = pb[int(os.environ.get('PB_BASE', '0')) + mc % 2] if os.environ.get('PB_ONE') != '1' else pb[0]
                    bk.begin_write()
                    for kc in range(CH):
                        mco = 0 if os.environ.get("LHS0") == "1" else mc
                        tk = P.emit("tensor", lambda h, mco=mco, kc=kc, tt=tt: h.matmul(
                            bk.h[:, 0:512], war[:, kc, mco * 128:(mco + 1) * 128],
                            ln1f[:, kc, tt * 512:tt * 512 + 512],
                            start=(kc == 0), stop=(kc == CH - 1)),
                            "t" if kc == CH - 1 else None)
                    bk.end_write(tk)
                    if os.environ.get("EVAC_V") == "1":
                        bk.read_from("vector")
                        tev = P.emit("vector", lambda h, mc=mc, tt=tt, dst=dst:
                                     h.tensor_copy(
                            dst[:, mc, PAD + tt * 512:PAD + tt * 512 + 512],
                            bk.h[:, 0:512]), "v")
                    else:
                        bk.read_from("scalar")
                        tev = P.emit("scalar", lambda h, mc=mc, tt=tt, dst=dst: h.activation(
                            dst[:, mc, PAD + tt * 512:PAD + tt * 512 + 512],
                            bk.h[:, 0:512], AF.Copy), "s")
                    bk.add_reader(tev)
                    ckpt(f"pwmm{half}_{wi}_{mc}")
            t_war_free = P.now("t")
            ckpt(f"pw{half}_{wi}")
        t_proj = P.now("s")
        ckpt("proj" + str(half))

        if half == 0 and os.environ.get("NO_PADMS") != "1":
            P.wait("vector", t_sinit)
            for buf in (qf, kf, vf):
                P.emit("vector", lambda h, buf=buf: h.memset(
                    flat(buf[:, :, 0:PAD]), 0.0), "v")
        else:
            P.wait("vector", t_proj)
            for buf, car in ((qf, cq), (kf, ck), (vf, cv)):
                P.emit("vector", lambda h, buf=buf, car=car: h.tensor_copy(
                    flat(buf[:, :, 0:PAD]), flat(car[:, :, :])), "v")
        t_pad = P.now("v")
        P.barrier()
        ax.close()

        bx = ExitStack()
        da = bx.enter_context(nc.sbuf_tensor(f"da_{half}", [128, NTAP, 128], BF16,
                                             side="left"))
        fsb = bx.enter_context(nc.sbuf_tensor(f"fs_{half}", [128, TH], BF16,
                                              side="left"))
        P.wait("vector", t_proj)
        if half + 1 < NHALF:
            for buf, car in ((qf, cq), (kf, ck), (vf, cv)):
                P.emit("vector", lambda h, buf=buf, car=car: h.tensor_copy(
                    flat(car[:, :, :]), flat(buf[:, :, TH:TH + PAD])), "v")
        t_stash = P.now("v")

        t_da_use = None
        for cc in range(CH):
            P.wait("gpsimd", t_da_use)
            t_da = P.emit("gpsimd", lambda h, cc=cc: h.dma_start(
                out=da[:, 0:12, :],
                in_=diag_d.ap()[cc, 0:12, :, :].transpose([1, 0, 2])), "dg")
            P.wait("tensor", t_da)
            P.wait("tensor", t_stash)
            P.wait("tensor", t_pad)
            for ti, (buf, f0) in enumerate(((qf, 0), (kf, 4), (vf, 8))):
                b0, b1 = pb[2 * (ti % 2)], pb[2 * (ti % 2) + 1]
                b0.begin_write()
                b1.begin_write()
                for j in range(4):
                    off = PAD - 3 + j
                    for tt in range(NT):
                        bk = b0 if tt % 2 == 0 else b1
                        tk = P.emit("tensor", lambda h, cc=cc, j=j, f0=f0, tt=tt,
                                    off=off, buf=buf, bk=bk: h.matmul(
                            bk.h[:, 0:512], da[:, f0 + j, :],
                            buf[:, cc, off + tt * 512:off + tt * 512 + 512],
                            start=(j == 0), stop=(j == 3)),
                            "t" if j == 3 else None)
                        (b0 if tt % 2 == 0 else b1).end_write(tk)
                for tt in range(NT):
                    bk = b0 if tt % 2 == 0 else b1
                    bk.read_from("scalar")
                    tev = P.emit("scalar", lambda h, cc=cc, tt=tt, buf=buf, bk=bk:
                                 h.activation(
                        buf[:, cc, PAD + tt * 512:PAD + tt * 512 + 512],
                        bk.h[:, 0:512], AF.Silu), "s")
                    bk.add_reader(tev)
            t_da_use = P.now("t")
        t_conv = P.now("s")
        ckpt("conv" + str(half))

        P.wait("vector", t_conv)
        if half > 0:
            P.emit("vector", lambda h: h.tensor_copy(
                flat(vf[:, :, 0:PAD]), flat(cvp[:, :, :])), "v")
        if half + 1 < NHALF:
            P.emit("vector", lambda h: h.tensor_copy(
                flat(cvp[:, :, :]), flat(vf[:, :, TH:TH + PAD])), "v")
        t_vpad = P.now("v")

        for nm, buf in (("qf", qf), ("kf", kf), ("vf", vf)):
            if nm in tapd:
                P.wait("scalar", t_conv)
                P.wait("scalar", t_vpad)
                for cc in range(CH):
                    P.emit("scalar", lambda h, nm=nm, cc=cc, buf=buf: h.dma_start(
                        out=tapd[nm].ap()[half, cc], in_=buf[:, cc, PAD:PAD + TH]), "ds")

        for tname in ("k", "q", "v"):
            src = {"k": kf, "q": qf, "v": vf}[tname]
            t_need = t_vpad if tname == "v" else t_conv
            for tc in range(NC):
                pbT.begin_write()
                P.wait("tensor", t_need)
                for cc in range(CH):
                    tk = P.emit("tensor", lambda h, src=src, cc=cc, tc=tc: h.transpose(
                        pbT.h[:, cc * 128:(cc + 1) * 128],
                        src[:, cc, PAD + tc * 128:PAD + tc * 128 + 128],
                        eyeb[:, :]), "t" if cc == CH - 1 else None)
                pbT.end_write(tk)
                if tname in ("k", "q"):
                    dc = 24 if tname == "k" else 28
                    pbT.read_from("scalar")
                    for hh in range(NH):
                        tr = P.emit("scalar", lambda h, hh=hh, tc=tc, dc=dc:
                                    h.activation(
                            scrap[:, 256 * hh:256 * hh + 256],
                            pbT.h[:, hh * 256:hh * 256 + 256], AF.Square,
                            accum_out=psc[:, tc, dc + hh:dc + hh + 1]), "s")
                        pbT.add_reader(tr)
                    P.sgap()
                    t_s = P.emit("scalar", lambda h, tc=tc, dc=dc: h.activation(
                        psc[:, tc, dc:dc + 4], psc[:, tc, dc:dc + 4], AF.Sqrt,
                        bias=ecst[:, 1:2]), "s")
                    P.wait("vector", t_s)
                    P.emit("vector", lambda h, tc=tc, dc=dc: h.reciprocal(
                        psc[:, tc, dc:dc + 4], psc[:, tc, dc:dc + 4]), "v")
                    if tname == "k":
                        pbT.read_from("vector")
                        P.vgap()
                        for hh in range(NH):
                            tr = P.emit("vector", lambda h, hh=hh, tc=tc: h.tensor_scalar(
                                ktok[:, tc, hh * 256:hh * 256 + 256],
                                pbT.h[:, hh * 256:hh * 256 + 256],
                                psc[:, tc, 24 + hh:25 + hh], None, ALU.mult), "v")
                            pbT.add_reader(tr)
                else:
                    pbT.read_from("scalar")
                    tr = P.emit("scalar", lambda h, tc=tc: h.activation(
                        vtok[:, tc, :], pbT.h[:, :], AF.Copy), "s")
                    pbT.add_reader(tr)
        P.wait("vector", P.now("s"))
        for tc in range(NC):
            P.emit("vector", lambda h, tc=tc: h.tensor_tensor(
                psc[:, tc, 32:36], psc[:, tc, 24:28], psc[:, tc, 20:24], ALU.mult), "v")
            for hh in range(NH):
                P.emit("vector", lambda h, tc=tc, hh=hh: h.tensor_tensor(
                    psc[:, tc, 36 + hh:37 + hh], psc[:, tc, 28 + hh:29 + hh],
                    psc[:, tc, 5 * hh + 3:5 * hh + 4], ALU.mult), "v")
        t_scl = P.now("v")
        P.wait("tensor", t_scl)
        for tc in range(NC):
            for g in range(2):
                pbS.begin_write()
                for cc in range(4 * g, 4 * g + 4):
                    tk = P.emit("tensor", lambda h, cc=cc, tc=tc: h.transpose(
                        pbS.h[:, (cc % 4) * 128:(cc % 4) * 128 + 128],
                        ktok[:, tc, cc * 128:(cc + 1) * 128], eyeb[:, :]),
                        "t" if cc % 4 == 3 else None)
                pbS.end_write(tk)
                pbS.read_from("scalar")
                tr = P.emit("scalar", lambda h, g=g, tc=tc: h.activation(
                    flat(khatf[:, 4 * g:4 * g + 4, tc * 128:tc * 128 + 128]),
                    pbS.h[:, 0:512], AF.Copy), "s")
                pbS.add_reader(tr)
        t_phC = P.now("s")
        ckpt("phC" + str(half))

        for nm, buf in (("ktok", ktok), ("vtok", vtok)):
            if nm in tapd:
                P.wait("scalar", t_phC)
                for tc in range(NC):
                    P.emit("scalar", lambda h, nm=nm, tc=tc, buf=buf: h.dma_start(
                        out=tapd[nm].ap()[half, tc], in_=buf[:, tc, :]), "ds")
        if "khatf" in tapd:
            P.wait("scalar", t_phC)
            for cc in range(CH):
                P.emit("scalar", lambda h, cc=cc: h.dma_start(
                    out=tapd["khatf"].ap()[half, cc], in_=khatf[:, cc, :]), "ds")

        P.wait("vector", t_phC)
        for tc in range(NC):
            for hh in range(NH):
                P.emit("vector", lambda h, tc=tc, hh=hh: h.tensor_scalar(
                    ACC[:, tc, hh * 256:hh * 256 + 256],
                    vtok[:, tc, hh * 256:hh * 256 + 256],
                    psc[:, tc, 5 * hh + 4:5 * hh + 5], None, ALU.mult), "v")
        t_acc0 = P.now("v")

        fir_specs = ((12, 3, 0), (15, 15, 1), (30, 63, 2))
        t_da_use = None
        for cc in range(CH):
            P.wait("gpsimd", t_da_use)
            t_daf = P.emit("gpsimd", lambda h, cc=cc: h.dma_start(
                out=da[:, 12:93, :],
                in_=diag_d.ap()[cc, 12:93, :, :].transpose([1, 0, 2])), "dg")
            hh = cc // 2
            for (lo, ntap, pcol) in fir_specs:
                b0, b1 = pb[0], pb[1]
                b0.begin_write()
                b1.begin_write()
                P.wait("tensor", t_daf)
                P.wait("tensor", t_vpad)
                for j in range(ntap):
                    off = PAD - (ntap - 1) + j
                    for tt in range(NT):
                        bk = b0 if tt % 2 == 0 else b1
                        tk = P.emit("tensor", lambda h, cc=cc, lo=lo, j=j, off=off,
                                    tt=tt, bk=bk, ntap=ntap: h.matmul(
                            bk.h[:, 0:512], da[:, lo + j, :],
                            vf[:, cc, off + tt * 512:off + tt * 512 + 512],
                            start=(j == 0), stop=(j == ntap - 1)),
                            "t" if j == ntap - 1 else None)
                        (b0 if tt % 2 == 0 else b1).end_write(tk)
                for tt in range(NT):
                    bk = b0 if tt % 2 == 0 else b1
                    bk.read_from("scalar")
                    tev = P.emit("scalar", lambda h, tt=tt, bk=bk: h.activation(
                        fsb[:, tt * 512:tt * 512 + 512], bk.h[:, 0:512], AF.Copy), "s")
                    bk.add_reader(tev)
                t_fs = P.now("s")
                P.wait("tensor", t_fs)
                P.wait("tensor", t_acc0)
                for tc in range(NC):
                    pbS.begin_write()
                    tk = P.emit("tensor", lambda h, tc=tc: h.transpose(
                        pbS.h[:, (tc % 4) * 128:(tc % 4) * 128 + 128],
                        fsb[:, tc * 128:tc * 128 + 128], eyeb[:, :]), "t")
                    pbS.end_write(tk)
                    pbS.read_from("vector")
                    tr = P.emit("vector", lambda h, tc=tc, cc=cc, hh=hh, pcol=pcol:
                                h.scalar_tensor_tensor(
                        ACC[:, tc, cc * 128:(cc + 1) * 128],
                        pbS.h[:, (tc % 4) * 128:(tc % 4) * 128 + 128],
                        psc[:, tc, 5 * hh + pcol:5 * hh + pcol + 1],
                        ACC[:, tc, cc * 128:(cc + 1) * 128], ALU.mult, ALU.add), "v")
                    pbS.add_reader(tr)
            t_da_use = P.now("t")
        t_fir = P.now("v")
        P.barrier()
        bx.close()
        ckpt("fir" + str(half))

        if "accmix" in tapd:
            P.wait("scalar", t_fir)
            for tc in range(NC):
                P.emit("scalar", lambda h, tc=tc: h.dma_start(
                    out=tapd["accmix"].ap()[half, tc], in_=ACC[:, tc, :]), "ds")
        if "psc" in tapd:
            P.wait("scalar", t_fir)
            P.wait("scalar", t_scl)
            for tc in range(NC):
                P.emit("scalar", lambda h, tc=tc: h.dma_start(
                    out=tapd["psc"].ap()[half, tc], in_=psc[:, tc, 0:40]), "ds")

        ex = ExitStack()
        sbe = lambda n, sh, dt: ex.enter_context(
            nc.sbuf_tensor(f"{n}_{half}", sh, dt, side="left"))
        dA = sbe("dA", [128, NH, 128], BF16)
        dC = sbe("dC", [128, NH, 128], BF16)
        dB = sbe("dB", [128, NH, 128], BF16)
        dQa = sbe("dQa", [128, NH, 128], BF16)
        dQb = sbe("dQb", [128, NH, 128], BF16)
        du = sbe("du", [128, NH, 256], BF16)
        dua = sbe("dua", [128, NH, 256], BF16)
        datn = sbe("datn", [128, NH, 128], BF16)
        dwf = sbe("dwf", [128, 2 * NH, 128], BF16)
        dvh = sbe("dvh", [128, NH, 256], BF16)
        dkb = sbe("dkb", [128, NH, 256], BF16)

        NLEV = 6
        for tc in range(NC):
            tr0 = PAD + tc * 128
            P.wait("vector", t_fir if tc == 0 else P.now("v"))
            for hh in range(NH):
                P.emit("vector", lambda h, tc=tc, hh=hh: h.tensor_scalar(
                    dvh[:, hh, :], vtok[:, tc, hh * 256:hh * 256 + 256],
                    psc[:, tc, 20 + hh:21 + hh], None, ALU.mult), "v")
                P.emit("vector", lambda h, tc=tc, hh=hh: h.tensor_scalar(
                    dkb[:, hh, :], ktok[:, tc, hh * 256:hh * 256 + 256],
                    psc[:, tc, 20 + hh:21 + hh], None, ALU.mult), "v")
            t_sc = P.now("v")

            pb7.begin_write()
            P.wait("tensor", t_sc)
            for hh in range(NH):
                for kc in range(2):
                    tk = P.emit("tensor", lambda h, hh=hh, kc=kc, tc=tc, tr0=tr0:
                                h.matmul(
                        pb7.h[:, hh * 128:hh * 128 + 128],
                        kf[:, 2 * hh + kc, tr0:tr0 + 128],
                        khatf[:, 2 * hh + kc, tc * 128:tc * 128 + 128],
                        start=(kc == 0), stop=(kc == 1)),
                        "t" if hh == NH - 1 and kc == 1 else None)
            pb7.end_write(tk)
            pb7.read_from("vector")
            for hh in range(NH):
                tr = P.emit("vector", lambda h, hh=hh, tc=tc: h.scalar_tensor_tensor(
                    dA[:, hh, :], pb7.h[:, hh * 128:hh * 128 + 128],
                    psc[:, tc, 32 + hh:33 + hh], negm[:, :], ALU.mult, ALU.mult), "v")
            pb7.add_reader(tr)
            t_A = P.now("v")

            pb[2].begin_write()
            P.wait("tensor", t_A)
            for hh in range(NH):
                for kc in range(2):
                    tk = P.emit("tensor", lambda h, hh=hh, kc=kc, tc=tc, tr0=tr0:
                                h.matmul(
                        pb[2].h[:, hh * 128:hh * 128 + 128],
                        khatf[:, 2 * hh + kc, tc * 128:tc * 128 + 128],
                        qf[:, 2 * hh + kc, tr0:tr0 + 128],
                        start=(kc == 0), stop=(kc == 1)),
                        "t" if hh == NH - 1 and kc == 1 else None)
            pb[2].end_write(tk)
            pb[2].read_from("vector")
            t_atn = P.emit("vector", lambda h: h.tensor_tensor(
                datn[:, :, :],
                pb[2].h[:, 0:512].rearrange("p (a b) -> p a b", a=NH),
                cmT[:, :].unsqueeze(1).broadcast_to([128, NH, 128]), ALU.mult), "v")
            pb[2].add_reader(t_atn)

            pbS.begin_write()
            P.wait("tensor", t_A)
            for hh in range(NH):
                tk = P.emit("tensor", lambda h, hh=hh: h.transpose(
                    pbS.h[:, hh * 128:hh * 128 + 128], dA[:, hh, :], eyeb[:, :]),
                    "t" if hh == NH - 1 else None)
            pbS.end_write(tk)
            pbS.read_from("scalar")
            t_c = P.emit("scalar", lambda h: h.activation(
                flat(dC[:, :, :]), pbS.h[:, 0:512], AF.Copy), "s")
            pbS.add_reader(t_c)
            P.wait("vector", t_c)
            t_q = P.emit("vector", lambda h: h.tensor_tensor(
                dQa[:, :, :], dC[:, :, :],
                eyeb[:, :].unsqueeze(1).broadcast_to([128, NH, 128]), ALU.add), "v")
            P.emit("vector", lambda h: h.tensor_copy(flat(dB[:, :, :]),
                                                     flat(dA[:, :, :])), "v")
            t_b = P.now("v")
            Qcur, Qnxt = dQa, dQb

            for lev in range(NLEV):
                pb7.begin_write()
                P.wait("tensor", t_b)
                for hh in range(NH):
                    tk = P.emit("tensor", lambda h, hh=hh: h.matmul(
                        pb7.h[:, hh * 128:hh * 128 + 128], dB[:, hh, :],
                        dC[:, hh, :], start=True, stop=True),
                        "t" if hh == NH - 1 else None)
                pb7.end_write(tk)
                pb7.read_from("scalar")
                t_c = P.emit("scalar", lambda h: h.activation(
                    flat(dC[:, :, :]), pb7.h[:, 0:512], AF.Copy), "s")
                pb7.add_reader(t_c)
                pbS.begin_write()
                P.wait("tensor", t_c)
                for hh in range(NH):
                    tk = P.emit("tensor", lambda h, hh=hh: h.transpose(
                        pbS.h[:, hh * 128:hh * 128 + 128], dC[:, hh, :],
                        eyeb[:, :]), "t" if hh == NH - 1 else None)
                pbS.end_write(tk)
                pbS.read_from("scalar")
                t_b = P.emit("scalar", lambda h: h.activation(
                    flat(dB[:, :, :]), pbS.h[:, 0:512], AF.Copy), "s")
                pbS.add_reader(t_b)
                pb[3].begin_write()
                P.wait("tensor", t_b)
                P.wait("tensor", t_q)
                for hh in range(NH):
                    P.emit("tensor", lambda h, hh=hh, Q=Qcur: h.matmul(
                        pb[3].h[:, hh * 128:hh * 128 + 128], dB[:, hh, :],
                        Q[:, hh, :], start=True, stop=False), None)
                    tk = P.emit("tensor", lambda h, hh=hh, Q=Qcur: h.matmul(
                        pb[3].h[:, hh * 128:hh * 128 + 128], eyeb[:, :],
                        Q[:, hh, :], start=False, stop=True),
                        "t" if hh == NH - 1 else None)
                pb[3].end_write(tk)
                pb[3].read_from("scalar")
                t_q = P.emit("scalar", lambda h, Q=Qnxt: h.activation(
                    flat(Q[:, :, :]), pb[3].h[:, 0:512], AF.Copy), "s")
                pb[3].add_reader(t_q)
                Qcur, Qnxt = Qnxt, Qcur
            dTT = Qcur
            t_tt = t_q

            pb[2].begin_write()
            pb[4].begin_write()
            P.wait("tensor", t_tt)
            for i in range(8):
                hh, dv = divmod(i, 2)
                bk = pb[2] if i < 4 else pb[4]
                tk = P.emit("tensor", lambda h, hh=hh, dv=dv, bk=bk, i=i, dTT=dTT:
                            h.matmul(
                    bk.h[:, (i % 4) * 128:(i % 4) * 128 + 128],
                    dTT[:, hh, :], dvh[:, hh, dv * 128:dv * 128 + 128],
                    start=True, stop=True), "t" if i in (3, 7) else None)
                if i == 3:
                    pb[2].end_write(tk)
            pb[4].end_write(tk)
            pb[2].read_from("scalar")
            tu1 = P.emit("scalar", lambda h: h.activation(
                flat(du[:, 0:2, :]), pb[2].h[:, 0:512], AF.Copy), "s")
            pb[2].add_reader(tu1)
            pb[4].read_from("scalar")
            tu2 = P.emit("scalar", lambda h: h.activation(
                flat(du[:, 2:4, :]), pb[4].h[:, 0:512], AF.Copy), "s")
            pb[4].add_reader(tu2)

            pb[5].begin_write()
            pb7.begin_write()
            for i in range(8):
                hh, kc = divmod(i, 2)
                bk = pb[5] if i < 4 else pb7
                tk = P.emit("tensor", lambda h, hh=hh, kc=kc, bk=bk, i=i, dTT=dTT:
                            h.matmul(
                    bk.h[:, (i % 4) * 128:(i % 4) * 128 + 128],
                    dkb[:, hh, kc * 128:kc * 128 + 128], dTT[:, hh, :],
                    start=True, stop=True), "t" if i in (3, 7) else None)
                if i == 3:
                    pb[5].end_write(tk)
            pb7.end_write(tk)
            pb[5].read_from("scalar")
            tw1 = P.emit("scalar", lambda h: h.activation(
                flat(dwf[:, 0:4, :]), pb[5].h[:, 0:512], AF.Copy), "s")
            pb[5].add_reader(tw1)
            pb7.read_from("scalar")
            tw2 = P.emit("scalar", lambda h: h.activation(
                flat(dwf[:, 4:8, :]), pb7.h[:, 0:512], AF.Copy), "s")
            pb7.add_reader(tw2)
            t_wf = P.now("s")

            if tc == 0 and half == 0:
                for nm, buf, tk0 in (("tt0", dTT, t_tt), ("u0", du, t_wf),
                                     ("wf0", dwf, t_wf), ("atn0", datn, t_wf)):
                    if nm in tapd:
                        P.wait("scalar", tk0)
                        nslice = buf.shape[1] if hasattr(buf, "shape") else NH
                        for a in range(nslice):
                            P.emit("scalar", lambda h, nm=nm, a=a, buf=buf: h.dma_start(
                                out=tapd[nm].ap()[a], in_=buf[:, a, :]), "ds")

            pb[0].begin_write()
            pb[1].begin_write()
            P.wait("tensor", t_wf)
            P.wait("tensor", t_scan_prev)
            for hh in range(NH):
                bk = pb[0] if hh < 2 else pb[1]
                for kc in range(2):
                    tk = P.emit("tensor", lambda h, hh=hh, kc=kc, bk=bk: h.matmul(
                        bk.h[:, (hh % 2) * 256:(hh % 2) * 256 + 256],
                        dwf[:, 2 * hh + kc, :], Sbf[:, 2 * hh + kc, :],
                        start=(kc == 0), stop=(kc == 1)),
                        "t" if kc == 1 and hh in (1, 3) else None)
                if hh == 1:
                    pb[0].end_write(tk)
            pb[1].end_write(tk)
            pb[0].read_from("vector")
            ta1 = P.emit("vector", lambda h: h.tensor_tensor(
                flat(dua[:, 0:2, :]), flat(du[:, 0:2, :]), pb[0].h[:, 0:512],
                ALU.subtract), "v")
            pb[0].add_reader(ta1)
            pb[1].read_from("vector")
            ta2 = P.emit("vector", lambda h: h.tensor_tensor(
                flat(dua[:, 2:4, :]), flat(du[:, 2:4, :]), pb[1].h[:, 0:512],
                ALU.subtract), "v")
            pb[1].add_reader(ta2)
            t_ua = P.now("v")

            pb[2].begin_write()
            pb[3].begin_write()
            P.wait("tensor", t_ua)
            for hh in range(NH):
                bk = pb[2] if hh < 2 else pb[3]
                for kc in range(2):
                    P.emit("tensor", lambda h, hh=hh, kc=kc, bk=bk, tc=tc, tr0=tr0:
                           h.matmul(
                        bk.h[:, (hh % 2) * 256:(hh % 2) * 256 + 256],
                        qf[:, 2 * hh + kc, tr0:tr0 + 128], Sbf[:, 2 * hh + kc, :],
                        start=(kc == 0), stop=False), None)
                tk = P.emit("tensor", lambda h, hh=hh, bk=bk: h.matmul(
                    bk.h[:, (hh % 2) * 256:(hh % 2) * 256 + 256],
                    datn[:, hh, :], dua[:, hh, :], start=False, stop=True),
                    "t" if hh in (1, 3) else None)
                if hh == 1:
                    pb[2].end_write(tk)
            pb[3].end_write(tk)
            for hh in range(NH):
                bk = pb[2] if hh < 2 else pb[3]
                bk.read_from("vector")
                tr = P.emit("vector", lambda h, hh=hh, bk=bk, tc=tc:
                            h.scalar_tensor_tensor(
                    ACC[:, tc, hh * 256:hh * 256 + 256],
                    bk.h[:, (hh % 2) * 256:(hh % 2) * 256 + 256],
                    psc[:, tc, 36 + hh:37 + hh],
                    ACC[:, tc, hh * 256:hh * 256 + 256], ALU.mult, ALU.add), "v")
                bk.add_reader(tr)

            t_supd = None
            for rnd in range(2):
                h0, h1 = 2 * rnd, 2 * rnd + 1
                pb[4].begin_write()
                pb[5].begin_write()
                P.wait("tensor", t_ua)
                for hh, bk in ((h0, pb[4]), (h1, pb[5])):
                    for kc in range(2):
                        tk = P.emit("tensor", lambda h, hh=hh, kc=kc, bk=bk, tc=tc:
                                    h.matmul(
                            bk.h[:, kc * 256:kc * 256 + 256],
                            ktok[:, tc, hh * 256 + kc * 128:hh * 256 + kc * 128 + 128],
                            dua[:, hh, :], start=True, stop=True),
                            "t" if kc == 1 else None)
                    if hh == h0:
                        pb[4].end_write(tk)
                pb[5].end_write(tk)
                for hh, bk in ((h0, pb[4]), (h1, pb[5])):
                    bk.read_from("vector")
                    ts1 = P.emit("vector", lambda h, hh=hh, bk=bk: h.tensor_tensor(
                        flat(S32[:, 2 * hh:2 * hh + 2, :]),
                        flat(S32[:, 2 * hh:2 * hh + 2, :]),
                        bk.h[:, 0:512], ALU.add), "v")
                    bk.add_reader(ts1)
                    P.wait("scalar", ts1)
                    t_supd = P.emit("scalar", lambda h, hh=hh: h.activation(
                        flat(Sbf[:, 2 * hh:2 * hh + 2, :]),
                        flat(S32[:, 2 * hh:2 * hh + 2, :]), AF.Copy), "s")
            t_scan_prev = t_supd

            ckpt("delta" + str(half) + "_" + str(tc))
        P.barrier()
        ex.close()
        qx.close()
        ckpt("delta" + str(half))

        if "accfin" in tapd:
            P.wait("scalar", P.now("v"))
            for tc in range(NC):
                P.emit("scalar", lambda h, tc=tc: h.dma_start(
                    out=tapd["accfin"].ap()[half, tc], in_=ACC[:, tc, :]), "ds")

        fx = ExitStack()
        sbf_ = lambda n, sh, dt: fx.enter_context(
            nc.sbuf_tensor(f"{n}_{half}", sh, dt, side="left"))
        ot = sbf_("ot", [128, D], BF16)
        of = sbf_("of", [128, CH, TH], BF16)
        wof = sbf_("wof", [128, CH, TH], F32)
        xr = sbf_("xr", [128, D], F32)
        xnt = sbf_("xnt", [128, D], F32)

        t_delta = P.now("v")
        t_ot_free = None
        for tc in range(NC):
            P.wait("vector", t_delta)
            for hh in range(NH):
                P.emit("vector", lambda h, tc=tc, hh=hh: h.tensor_tensor(
                    scrap[:, 256 * hh:256 * hh + 256],
                    ACC[:, tc, hh * 256:hh * 256 + 256],
                    ACC[:, tc, hh * 256:hh * 256 + 256], ALU.mult), "v")
            for hh in range(NH):
                P.emit("vector", lambda h, tc=tc, hh=hh: h.tensor_reduce(
                    psc[:, tc, 40 + hh:41 + hh], scrap[:, 256 * hh:256 * hh + 256],
                    AXX, ALU.add), "v")
            P.wait("scalar", P.now("v"))
            t_r = P.emit("scalar", lambda h, tc=tc: h.activation(
                psc[:, tc, 40:44], psc[:, tc, 40:44], AF.Sqrt,
                scale=1.0 / 256, bias=ecst[:, 0:1]), "s")
            P.wait("vector", t_r)
            P.emit("vector", lambda h, tc=tc: h.reciprocal(
                psc[:, tc, 40:44], psc[:, tc, 40:44]), "v")
            P.vgap()
            P.wait("vector", t_ot_free)
            for hh in range(NH):
                P.emit("vector", lambda h, tc=tc, hh=hh: h.tensor_scalar(
                    ot[:, hh * 256:hh * 256 + 256], ACC[:, tc, hh * 256:hh * 256 + 256],
                    psc[:, tc, 40 + hh:41 + hh], None, ALU.mult), "v")
            t_otk = P.now("v")
            P.wait("tensor", t_otk)
            pbT.begin_write()
            for cc in range(CH):
                tk = P.emit("tensor", lambda h, cc=cc: h.transpose(
                    pbT.h[:, cc * 128:(cc + 1) * 128], ot[:, cc * 128:(cc + 1) * 128],
                    eyeb[:, :]), "t" if cc == CH - 1 else None)
            pbT.end_write(tk)
            t_ot_free = tk
            pbT.read_from("scalar")
            tev = P.emit("scalar", lambda h, tc=tc: h.activation(
                flat(of[:, :, tc * 128:tc * 128 + 128]), pbT.h[:, :], AF.Copy), "s")
            pbT.add_reader(tev)
        t_of = P.now("s")
        ckpt("of" + str(half))

        if "of" in tapd:
            P.wait("scalar", t_of)
            for cc in range(CH):
                P.emit("scalar", lambda h, cc=cc: h.dma_start(
                    out=tapd["of"].ap()[half, cc], in_=of[:, cc, :]), "ds")

        P.wait("tensor", t_of)
        P.wait("tensor", t_wofold if half == 0 else t_of)
        for tt in range(NT):
            for mc in range(CH):
                bk = pb[mc % 2]
                bk.begin_write()
                for kc in range(CH):
                    tk = P.emit("tensor", lambda h, mc=mc, kc=kc, tt=tt: h.matmul(
                        bk.h[:, 0:512], wo[:, kc, mc * 128:(mc + 1) * 128],
                        of[:, kc, tt * 512:tt * 512 + 512],
                        start=(kc == 0), stop=(kc == CH - 1)),
                        "t" if kc == CH - 1 else None)
                bk.end_write(tk)
                bk.read_from("scalar")
                tev = P.emit("scalar", lambda h, mc=mc, tt=tt: h.activation(
                    wof[:, mc, tt * 512:tt * 512 + 512], bk.h[:, 0:512], AF.Copy), "s")
                bk.add_reader(tev)
        t_wof = P.now("s")
        ckpt("wof" + str(half))

        for tc in range(NC):
            t0 = half * TH + tc * 128
            t_x = P.emit("sync", lambda h, t0=t0: h.dma_start(
                out=xr[:, :], in_=x_d.ap()[t0:t0 + 128, :]), "dl")
            pb[4].begin_write()
            pb[5].begin_write()
            P.wait("tensor", t_wof)
            for cc in range(CH):
                bk = pb[4] if cc < 4 else pb[5]
                tk = P.emit("tensor", lambda h, cc=cc, tc=tc, bk=bk: h.transpose(
                    bk.h[:, (cc % 4) * 128:(cc % 4) * 128 + 128],
                    wof[:, cc, tc * 128:tc * 128 + 128], eyef[:, :]),
                    "t" if cc in (3, 7) else None)
                if cc == 3:
                    pb[4].end_write(tk)
            pb[5].end_write(tk)
            P.wait("vector", t_x)
            for g, bk in ((0, pb[4]), (1, pb[5])):
                bk.read_from("vector")
                tr = P.emit("vector", lambda h, g=g, bk=bk: h.tensor_tensor(
                    xnt[:, g * 512:g * 512 + 512], xr[:, g * 512:g * 512 + 512],
                    bk.h[:, 0:512], ALU.add), "v")
                bk.add_reader(tr)
            t_xn = P.now("v")
            P.wait("scalar", t_xn)
            t_st = P.emit("scalar", lambda h, t0=t0: h.dma_start(
                out=xnew_d.ap()[t0:t0 + 128, :], in_=xnt[:, :]), "ds")
            if "xnew" in tapd:
                P.emit("scalar", lambda h, t0=t0: h.dma_start(
                    out=tapd["xnew"].ap()[t0:t0 + 128, :], in_=xnt[:, :]), "ds")
            P.wait("sync", t_st)
            P.wait("vector", t_st)
        P.barrier()
        fx.close()
        hs.close()
        ckpt("resid" + str(half))

    P.barrier()
    att.close()
    ckpt("attn")

    fz = ExitStack()
    sbf2 = lambda n, sh, dt: fz.enter_context(
        nc.sbuf_tensor(n, sh, dt, side="left"))
    wgs = sbf2("wgs", [128, CH, HID], BF16)
    wus = sbf2("wus", [128, CH, HID], BF16)
    wds = sbf2("wds", [128, HC, D], BF16)
    hbuf = sbf2("hbuf", [128, HC, FT], BF16)
    ln2f = sbf2("ln2f", [128, CH, FT], BF16)
    ga = sbf2("ga", [128, FT], BF16)
    lnt2 = sbf2("lnt2", [128, D], BF16)
    xr2 = sbf2("xr2", [128, 2, D], F32)
    dwn = sbf2("dwn", [128, CH, FT], F32)
    ont = sbf2("ont", [128, D], F32)

    t_wgc = {}
    t_wuc = {}
    t_wdc = {}
    for kc in range(CH):
        t_wgc[kc] = P.emit("gpsimd", lambda h, kc=kc: h.dma_start(
            out=wgs[:, kc, :], in_=wg_d.ap()[kc]), "dg")
        t_wuc[kc] = P.emit("gpsimd", lambda h, kc=kc: h.dma_start(
            out=wus[:, kc, :], in_=wu_d.ap()[kc]), "dg")
    for kc in range(HC):
        t_wdc[kc] = P.emit("gpsimd", lambda h, kc=kc: h.dma_start(
            out=wds[:, kc, :], in_=wd_d.ap()[kc]), "dg")

    t_lnt2_free = None
    t_xr2_free = None
    t_tile_prev = None
    for ft in range(NFT):
        f0 = ft * FT
        P.wait("sync", t_xr2_free)
        P.wait("scalar", t_tile_prev)
        P.wait("vector", t_tile_prev)
        for tj in range(2):
            t0 = f0 + tj * 128
            t_x = P.emit("sync", lambda h, t0=t0, tj=tj: h.dma_start(
                out=xr2[:, tj, :], in_=xnew_d.ap()[t0:t0 + 128, :]), "dl")
            t_ln = emit_ln_chunk(t_x, xr2[:, tj, :], lnt2[:, :], t_lnt2_free)
            P.wait("tensor", t_ln)
            pbT.begin_write()
            for cc in range(CH):
                tk = P.emit("tensor", lambda h, cc=cc: h.transpose(
                    pbT.h[:, cc * 128:(cc + 1) * 128],
                    lnt2[:, cc * 128:(cc + 1) * 128], eyeb[:, :]),
                    "t" if cc == CH - 1 else None)
            pbT.end_write(tk)
            t_lnt2_free = tk
            pbT.read_from("scalar")
            tev = P.emit("scalar", lambda h, tj=tj: h.activation(
                flat(ln2f[:, :, tj * 128:tj * 128 + 128]), pbT.h[:, :], AF.Copy), "s")
            pbT.add_reader(tev)
        t_ln2 = P.now("s")
        if "ln2f" in tapd:
            P.wait("scalar", t_ln2)
            for cc in range(CH):
                P.emit("scalar", lambda h, cc=cc, ft=ft: h.dma_start(
                    out=tapd["ln2f"].ap()[ft, cc], in_=ln2f[:, cc, :]), "ds")

        P.wait("tensor", t_ln2)
        for mc in range(HC):
            bg_ = pb[(mc % 2) * 2]
            bu_ = pb[(mc % 2) * 2 + 1]
            bg_.begin_write()
            bu_.begin_write()
            for kc in range(CH):
                P.wait("tensor", t_wgc[kc])
                P.wait("tensor", t_wuc[kc])
                tkg = P.emit("tensor", lambda h, mc=mc, kc=kc, bg_=bg_: h.matmul(
                    bg_.h[:, 0:FT], wgs[:, kc, mc * 128:(mc + 1) * 128],
                    ln2f[:, kc, :], start=(kc == 0), stop=(kc == CH - 1)),
                    "t" if kc == CH - 1 else None)
                tk = P.emit("tensor", lambda h, mc=mc, kc=kc, bu_=bu_: h.matmul(
                    bu_.h[:, 0:FT], wus[:, kc, mc * 128:(mc + 1) * 128],
                    ln2f[:, kc, :], start=(kc == 0), stop=(kc == CH - 1)),
                    "t" if kc == CH - 1 else None)
            bg_.end_write(tkg)
            bu_.end_write(tk)
            bg_.read_from("scalar")
            t_g = P.emit("scalar", lambda h, bg_=bg_: h.activation(
                ga[:, :], bg_.h[:, 0:FT], AF.Silu, scale=QS), "s")
            bg_.add_reader(t_g)
            bu_.read_from("vector")
            P.wait("vector", t_g)
            t_h = P.emit("vector", lambda h, mc=mc, bu_=bu_: h.scalar_tensor_tensor(
                hbuf[:, mc, :], bu_.h[:, 0:FT], QS, ga[:, :], ALU.mult, ALU.mult), "v")
            bu_.add_reader(t_h)
            P.wait("scalar", t_h)
        t_hb = P.now("v")
        ckpt("gup" + str(ft))
        if "hbuf" in tapd:
            P.wait("scalar", t_hb)
            for mc in range(HC):
                P.emit("scalar", lambda h, mc=mc, ft=ft: h.dma_start(
                    out=tapd["hbuf"].ap()[ft, mc], in_=hbuf[:, mc, :]), "ds")

        P.wait("tensor", t_hb)
        for kc in range(HC):
            P.wait("tensor", t_wdc[kc])
        for mc in range(CH):
            bk = pb[2 + mc % 4]
            bk.begin_write()
            for kc in range(HC):
                tk = P.emit("tensor", lambda h, kc=kc, mc=mc, bk=bk: h.matmul(
                    bk.h[:, 0:FT], wds[:, kc, mc * 128:(mc + 1) * 128],
                    hbuf[:, kc, :], start=(kc == 0), stop=(kc == HC - 1)),
                    "t" if kc == HC - 1 else None)
            bk.end_write(tk)
            bk.read_from("scalar")
            tev = P.emit("scalar", lambda h, mc=mc, bk=bk: h.activation(
                dwn[:, mc, :], bk.h[:, 0:FT], AF.Copy), "s")
            bk.add_reader(tev)
        t_dwn = P.now("s")
        ckpt("dwn" + str(ft))

        for tj in range(2):
            t0 = f0 + tj * 128
            pb[0].begin_write()
            pb[1].begin_write()
            P.wait("tensor", t_dwn)
            for cc in range(CH):
                bk = pb[0] if cc < 4 else pb[1]
                tk = P.emit("tensor", lambda h, cc=cc, tj=tj, bk=bk: h.transpose(
                    bk.h[:, (cc % 4) * 128:(cc % 4) * 128 + 128],
                    dwn[:, cc, tj * 128:tj * 128 + 128], eyef[:, :]),
                    "t" if cc in (3, 7) else None)
                if cc == 3:
                    pb[0].end_write(tk)
            pb[1].end_write(tk)
            for g, bk in ((0, pb[0]), (1, pb[1])):
                bk.read_from("vector")
                tr = P.emit("vector", lambda h, g=g, tj=tj, bk=bk:
                            h.scalar_tensor_tensor(
                    ont[:, g * 512:g * 512 + 512], bk.h[:, 0:512], QS,
                    xr2[:, tj, g * 512:g * 512 + 512], ALU.mult, ALU.add), "v")
                bk.add_reader(tr)
            t_o = P.now("v")
            P.wait("scalar", t_o)
            t_ost = P.emit("scalar", lambda h, t0=t0: h.dma_start(
                out=out_d.ap()[t0:t0 + 128, :], in_=ont[:, :]), "ds")
            P.wait("vector", t_ost)
        t_xr2_free = t_o
        t_tile_prev = P.now("t")

    P.wait_queue("sync", "ds")
    P.wait_queue("sync", "dl")
    P.wait_queue("sync", "dg")

    assemble(nc, P)
    return nc, tapd


def assemble(nc, P):
    import bisect as _b
    keep = P.finalize_maps()
    sems = P.sems

    def runner(eng_name):
        def run(h):
            for rec in P.recs[eng_name]:
                kind = rec[0]
                if kind == "dma":
                    _, fn, key = rec
                    fn(h).then_inc(sems[key], 16)
                elif kind == "op":
                    _, fn, inc, pval = rec
                    ins = fn(h)
                    if inc is not None and pval in keep[inc][1]:
                        ins.then_inc(sems[inc], 1)
                else:
                    _, sem, val = rec
                    if sem in keep:
                        kept, m = keep[sem]
                        i = _b.bisect_left(kept, val)
                        if i >= len(kept):
                            i = len(kept) - 1
                        h.wait_ge(sems[sem], m[kept[i]])
                    else:
                        h.wait_ge(sems[sem], val)
        return run

    with nc.Block() as block:
        block.sync(runner("sync"))
        block.gpsimd(runner("gpsimd"))
        block.vector(runner("vector"))
        block.scalar(runner("scalar"))
        block.tensor(runner("tensor"))


def host_prep(inputs, L=2048):
    f32 = np.float32
    Wq = inputs["L0_attn_Wq"]
    Wk = inputs["L0_attn_Wk"]
    Wv = inputs["L0_attn_Wv"]
    Wo = inputs["L0_attn_Wo"]
    Wb = inputs["L0_attn_Wb"]
    gw = inputs["L0_attn_gate_w"].reshape(20, D)
    gb = inputs["L0_attn_gate_b"].reshape(20)
    conv_q = inputs["L0_attn_conv_q"]
    conv_k = inputs["L0_attn_conv_k"]
    conv_v = inputs["L0_attn_conv_v"]
    fir_s = inputs["L0_attn_fir_s"].reshape(D, 3)
    fir_m = inputs["L0_attn_fir_m"].reshape(D, 15)
    fir_l = inputs["L0_attn_fir_l"].reshape(D, 63)

    def wt(w):
        return np.ascontiguousarray(np.asarray(w, f32).T).reshape(CH, 128, D)

    gbw = np.concatenate([np.asarray(gw, f32), np.asarray(Wb, f32)], axis=0)
    wgbT = np.ascontiguousarray(gbw.T).reshape(CH, 128, 24)

    diag = np.zeros((CH, NTAP, 128, 128), dtype=f32)
    idx = np.arange(128)
    filts = [np.asarray(conv_q, f32), np.asarray(conv_k, f32),
             np.asarray(conv_v, f32), np.asarray(fir_s, f32),
             np.asarray(fir_m, f32), np.asarray(fir_l, f32)]
    for cc in range(CH):
        t = 0
        for fm in filts:
            K = fm.shape[1]
            for j in range(K):
                diag[cc, t, idx, idx] = fm[cc * 128:(cc + 1) * 128, j]
                t += 1
    assert t == NTAP

    m = {
        "wqT": wt(Wq), "wkT": wt(Wk), "wvT": wt(Wv), "woT": wt(Wo),
        "wgbT": wgbT, "diag_all": diag,
        "gb_exp": np.ascontiguousarray(np.broadcast_to(gb, (128, 20))).astype(f32),
        "log_temp": np.asarray(inputs["L0_attn_log_temp"], f32).reshape(4, 1),
        "eps_param": np.asarray(inputs["L0_attn_eps_param"], f32).reshape(4, 1),
        "nw": np.ascontiguousarray(
            np.asarray(inputs["L0_attn_norm_w"], f32).reshape(2, 128).T),
        "wgT": np.ascontiguousarray(
            np.asarray(inputs["L0_ffn_w_gate"]).T).reshape(CH, 128, HID),
        "wuT": np.ascontiguousarray(
            np.asarray(inputs["L0_ffn_w_up"]).T).reshape(CH, 128, HID),
        "wdT": np.ascontiguousarray(
            np.asarray(inputs["L0_ffn_w_down"]).T).reshape(HC, 128, D),
    }
    return m


def kernel(**inputs):
    x = np.asarray(inputs["L0_attn_hidden_states"], np.float32)
    B, L, _ = x.shape
    shared = host_prep(inputs, L)
    nc, _ = build(L)
    in_maps = []
    for b in range(B):
        im = dict(shared)
        im["x"] = np.ascontiguousarray(x[b])
        in_maps.append(im)
    res = run_bass_kernel_spmd(nc, in_maps, core_ids=list(range(B)))
    out = np.stack([np.asarray(res.results[b]["out"]) for b in range(B)], axis=0)
    return out.astype(np.float32)


# version 03
# speedup vs baseline: 1.1246x; 1.1246x over previous
from contextlib import ExitStack

import os
import numpy as np
import ml_dtypes

import concourse.bass as bass
import concourse.mybir as mybir
from concourse.bass_utils import run_bass_kernel_spmd

F32 = mybir.dt.float32
BF16 = mybir.dt.bfloat16
I32 = mybir.dt.int32
AF = mybir.ActivationFunctionType
ALU = mybir.AluOpType
AXX = mybir.AxisListType.X

D = 1024
NH = 4
CH = D // 128
HID = 3072
HC = HID // 128
PAD = 64
NTAP = 93
QS = 1.0 / 127.0

ENGS = ("sync", "gpsimd", "vector", "scalar", "tensor")


DMA_QUEUES = {"dl": "sync", "dg": "gpsimd", "ds": "scalar"}
DMA_POOL = 8


def _freeze(fn):
    import types
    if fn.__closure__:
        fn = types.FunctionType(
            fn.__code__, fn.__globals__, fn.__name__, fn.__defaults__,
            tuple(types.CellType(c.cell_contents) for c in fn.__closure__))
    return fn


class Prog:

    def __init__(self, nc, es):
        self.nc = nc
        self.recs = {e: [] for e in ENGS}
        self.sems = {}
        self.cnt = {}
        for name in ("v", "s", "t", "g"):
            self.sems[name] = es.enter_context(nc.semaphore("sem_" + name))
            self.cnt[name] = 0
        for q in DMA_QUEUES:
            for i in range(DMA_POOL):
                key = f"{q}{i}"
                self.sems[key] = es.enter_context(nc.semaphore("sem_" + key))
                self.cnt[key] = 0
        self.qidx = {q: 0 for q in DMA_QUEUES}
        self.lastw = {}
        self.waited = {name: set() for name in ("v", "s", "t", "g")}

    def emit(self, eng, fn, inc=None):
        fn = _freeze(fn)
        if inc in DMA_QUEUES:
            q = inc
            key = f"{q}{self.qidx[q] % DMA_POOL}"
            self.qidx[q] += 1
            prev = self.cnt[key]
            if prev > 0:
                self.wait(DMA_QUEUES[q], (key, prev))
            self.cnt[key] = prev + 16
            tok = (key, self.cnt[key])
            self.recs[eng].append(("dma", fn, key))
            return tok
        tok = None
        pval = 0
        if inc is not None:
            self.cnt[inc] += 1
            pval = self.cnt[inc]
            tok = (inc, pval)
        self.recs[eng].append(("op", fn, inc, pval))
        return tok

    def wait(self, eng, tok):
        if tok is None:
            return
        sem, val = tok
        if val <= 0:
            return
        key = (eng, sem)
        if self.lastw.get(key, -1) >= val:
            return
        self.lastw[key] = val
        if sem in self.waited:
            self.waited[sem].add(val)
        self.recs[eng].append(("wait", sem, val))

    def now(self, sem):
        return (sem, self.cnt[sem])

    def wait_queue(self, eng, q):
        for i in range(DMA_POOL):
            key = f"{q}{i}"
            if self.cnt[key] > 0:
                self.wait(eng, (key, self.cnt[key]))

    def barrier(self, bop=None):
        if os.environ.get("BARRIER_OFF") == "1":
            return
        for sem in ("s", "t"):
            self.wait("vector", (sem, self.cnt[sem]))
        for q in DMA_QUEUES:
            self.wait_queue("vector", q)
        btok = self.emit("vector", self._bop, "v")
        for eng in ("sync", "gpsimd", "scalar", "tensor"):
            self.wait(eng, btok)

    def finalize_maps(self):
        keep = {}
        for sem in ("v", "s", "t", "g"):
            kept = set()
            total = self.cnt[sem]
            for w in self.waited[sem]:
                if 0 < w <= total:
                    kept.add(w)
            if total > 0:
                kept.add(total)
            kept = sorted(kept)
            m = {v: i + 1 for i, v in enumerate(kept)}
            keep[sem] = (kept, m)
        return keep


class Bank:
    def __init__(self, prog, handle):
        self.p = prog
        self.h = handle
        self.readers = []
        self.wtok = None

    def begin_write(self):
        for tok in self.readers:
            self.p.wait("tensor", tok)
        self.readers = []

    def end_write(self, tok):
        self.wtok = tok

    def read_from(self, eng):
        self.p.wait(eng, self.wtok)

    def add_reader(self, tok):
        self.readers.append(tok)


def flat(ap):
    try:
        return ap.rearrange("p a b -> p (a b)")
    except ValueError:
        return ap


class _Stop(Exception):
    pass


_LAST = {}


def build(L=2048, taps=(), until=None):
    TH = min(1024, L // 2)
    NHALF = L // TH
    NC = TH // 128
    NT = TH // 512
    FT = 256
    NFT = L // FT

    nc = bass.Bass("TRN2", target_bir_lowering=False,
                   detect_race_conditions=False)
    es = ExitStack()
    P = Prog(nc, es)
    _LAST["nc"] = nc
    _LAST["P"] = P
    _LAST["tapd"] = tapd = {}
    tapset = set(taps)

    dr = lambda n, sh, dt, k: nc.dram_tensor(n, sh, dt, kind=k)
    x_d = dr("x", [L, D], F32, "ExternalInput")
    wq_d = dr("wqT", [CH, 128, D], F32, "ExternalInput")
    wk_d = dr("wkT", [CH, 128, D], F32, "ExternalInput")
    wv_d = dr("wvT", [CH, 128, D], F32, "ExternalInput")
    wo_d = dr("woT", [CH, 128, D], F32, "ExternalInput")
    wgb_d = dr("wgbT", [CH, 128, 24], F32, "ExternalInput")
    diag_d = dr("diag_all", [CH, NTAP, 128, 128], F32, "ExternalInput")
    gbx_d = dr("gb_exp", [128, 20], F32, "ExternalInput")
    ltemp_d = dr("log_temp", [4, 1], F32, "ExternalInput")
    epsp_d = dr("eps_param", [4, 1], F32, "ExternalInput")
    nw_d = dr("nw", [128, 2], F32, "ExternalInput")
    wg_d = dr("wgT", [CH, 128, HID], I32, "ExternalInput")
    wu_d = dr("wuT", [CH, 128, HID], I32, "ExternalInput")
    wd_d = dr("wdT", [HC, 128, D], I32, "ExternalInput")
    out_d = dr("out", [L, D], F32, "ExternalOutput")
    xnew_d = dr("xnew", [L, D], F32, "Internal")
    bsc_d = dr("bscr", [4, 3], F32, "Internal")

    def mktap(name, shape, dt=F32):
        if name in tapset:
            tapd[name] = dr("tap_" + name, shape, dt, "ExternalOutput")

    mktap("ln1f", [NHALF, CH, 128, TH], BF16)
    mktap("qf", [NHALF, CH, 128, TH], BF16)
    mktap("kf", [NHALF, CH, 128, TH], BF16)
    mktap("vf", [NHALF, CH, 128, TH], BF16)
    mktap("ktok", [NHALF, NC, 128, D], BF16)
    mktap("vtok", [NHALF, NC, 128, D], BF16)
    mktap("khatf", [NHALF, CH, 128, TH], BF16)
    mktap("psc", [NHALF, NC, 128, 40], F32)
    mktap("accmix", [NHALF, NC, 128, D], BF16)
    mktap("accfin", [NHALF, NC, 128, D], BF16)
    mktap("of", [NHALF, CH, 128, TH], BF16)
    mktap("xnew", [L, D], F32)
    mktap("ln2f", [NFT, CH, 128, FT], BF16)
    mktap("hbuf", [NFT, HC, 128, FT], BF16)

    eyeb_d = nc.inline_tensor(np.eye(128, dtype=ml_dtypes.bfloat16), "eyeb_c")
    eyef_d = nc.inline_tensor(np.eye(128, dtype=np.float32), "eyef_c")
    negm_d = nc.inline_tensor(-np.tril(np.ones((128, 128), dtype=np.float32), -1),
                              "negm_c")
    ecst_np = np.zeros((128, 2), dtype=np.float32)
    ecst_np[:, 0] = 1e-5
    ecst_np[:, 1] = 1e-6
    ecst_d = nc.inline_tensor(ecst_np, "ecst_c")
    cmT_d = nc.inline_tensor(np.triu(np.ones((128, 128), dtype=np.float32), 0),
                             "cmT_c")

    _side = "left" if os.environ.get("ALL_LEFT") == "1" else "right"
    sbr = lambda n, sh, dt: es.enter_context(nc.sbuf_tensor(n, sh, dt, side=_side))
    eyeb = sbr("eyeb", [128, 128], BF16)
    eyef = sbr("eyef", [128, 128], F32)
    negm = sbr("negm", [128, 128], F32)
    cmT = sbr("cmT", [128, 128], F32)
    gbx = sbr("gbx", [128, 20], F32)
    nwt = sbr("nwt", [128, 2], F32)
    scb = sbr("scb", [128, 12], F32)
    tiny = sbr("tiny", [128, 8], F32)
    stat = sbr("stat", [128, 64], F32)
    scrap = sbr("scrap", [128, 1024], F32)
    ecst = sbr("ecst", [128, 2], F32)
    P._bop = lambda h: h.tensor_copy(scrap[0:1, 0:1], scrap[0:1, 1:2])
    P.vgap = lambda: P.wait("vector", P.now("v"))
    P.sgap = lambda: P.wait("scalar", P.now("s"))
    wgb = sbr("wgb", [128, CH, 24], BF16)
    att = ExitStack()
    sbar = lambda n, sh, dt: att.enter_context(nc.sbuf_tensor(n, sh, dt, side=_side))
    wo = sbar("wo", [128, CH, D], BF16)
    cq = sbar("cq", [128, CH, PAD], BF16)
    ck = sbar("ck", [128, CH, PAD], BF16)
    cv = sbar("cv", [128, CH, PAD], BF16)
    cvp = sbar("cvp", [128, CH, PAD], BF16)
    S32 = sbar("S32", [128, 2 * NH, 256], F32)
    Sbf = sbar("Sbf", [128, 2 * NH, 256], BF16)

    pb = [Bank(P, es.enter_context(nc.psum_tensor(f"pb{i}", [128, 512], F32)))
          for i in range(6)]
    pbT = Bank(P, es.enter_context(nc.psum_tensor("pbT", [128, 1024], BF16)))
    pbS = pbT
    pb7 = Bank(P, es.enter_context(nc.psum_tensor("pb7", [128, 512], F32)))

    lds = []
    _consts = ((eyeb, eyeb_d), (eyef, eyef_d), (negm, negm_d),
               (cmT, cmT_d), (gbx, gbx_d), (ecst, ecst_d))
    if os.environ.get("NO_CONSTS") == "1":
        _consts = ((eyeb, eyeb_d),)
    for dst, src in _consts:
        lds.append(P.emit("sync", lambda h, d=dst, s=src: h.dma_start(
            out=d[:, :], in_=s.ap()[:, :]), "dl"))
    if os.environ.get("NO_CONSTS") != "1":
        lds.append(P.emit("sync", lambda h: h.dma_start(
            out=nwt[:, :], in_=nw_d.ap()[:, :]), "dl"))
    if os.environ.get("NO_LTEMP") != "1":
        lds.append(P.emit("sync", lambda h: h.dma_start(
            out=tiny[0:4, 0:1], in_=ltemp_d.ap()[:, :]), "dl"))
        lds.append(P.emit("sync", lambda h: h.dma_start(
            out=tiny[0:4, 1:2], in_=epsp_d.ap()[:, :]), "dl"))
    if os.environ.get("NO_WOWGB") != "1":
        t_wo = P.emit("gpsimd", lambda h: h.dma_start(
            out=wo[:, :, :], in_=wo_d.ap().transpose([1, 0, 2])), "dg")
        t_wgb = P.emit("gpsimd", lambda h: h.dma_start(
            out=wgb[:, :, :], in_=wgb_d.ap().transpose([1, 0, 2])), "dg")
    else:
        t_wo = t_wgb = None

    for tok in lds:
        P.wait("scalar", tok)
        P.wait("vector", tok)
    P.emit("vector", lambda h: h.memset(scrap[:, :], 0.0), "v")
    if os.environ.get("NO_TINY") != "1":
        P.emit("scalar", lambda h: h.activation(tiny[0:4, 2:3], tiny[0:4, 0:1], AF.Exp), "s")
        P.wait("vector", P.now("s"))
        P.emit("vector", lambda h: h.tensor_scalar(
            tiny[0:4, 5:6], tiny[0:4, 1:2], 0.0, 0.2, ALU.max, ALU.min), "v")
        P.emit("vector", lambda h: h.reciprocal(tiny[0:4, 4:5], tiny[0:4, 2:3]), "v")
        P.emit("vector", lambda h: h.tensor_scalar(
            tiny[0:4, 6:7], tiny[0:4, 5:6], -5.0, 1.0, ALU.mult, ALU.add), "v")
    P.wait("gpsimd", P.now("v"))
    if os.environ.get("NO_SCB") != "1":
        P.emit("gpsimd", lambda h: h.dma_start(out=bsc_d.ap()[:, :], in_=tiny[0:4, 4:7]), "dg")
        P.wait_queue("gpsimd", "dg")
        t_scb = P.emit("gpsimd", lambda h: h.dma_start(
            out=scb[:, 0:12],
            in_=bsc_d.ap().rearrange("h c -> (h c)").unsqueeze(0).broadcast_to([128, 12])),
            "dg")
        P.wait("vector", t_scb)
        P.wait("scalar", t_scb)
    P.wait("vector", t_wo)
    if os.environ.get("NO_FOLD") != "1":
        for kc in range(CH):
            P.emit("vector", lambda h, kc=kc: h.tensor_scalar(
                wo[:, kc, :], wo[:, kc, :], nwt[:, kc % 2:kc % 2 + 1], None, ALU.mult), "v")
    t_wofold = P.now("v")
    if os.environ.get("NO_MS") != "1":
        P.emit("vector", lambda h: h.memset(flat(S32[:, :, :]), 0.0), "v")
        P.emit("vector", lambda h: h.memset(flat(Sbf[:, :, :]), 0.0), "v")
    t_sinit = P.now("v")
    P.wait("tensor", t_wgb)

    def emit_ln_chunk(t_x, xsl, lnt_dst, t_lnt_free):
        P.wait("vector", t_x)
        P.wait("scalar", t_x)
        P.emit("vector", lambda h: h.tensor_reduce(
            stat[:, 0:1], xsl, AXX, ALU.add), "v")
        P.wait("scalar", t_lnt_free)
        t_sq = P.emit("scalar", lambda h: h.activation(
            lnt_dst, xsl, AF.Square, accum_out=stat[:, 1:2]), "s")
        P.wait("vector", t_sq)
        P.vgap()
        P.emit("vector", lambda h: h.tensor_scalar(
            stat[:, 0:1], stat[:, 0:1], 1.0 / D, None, ALU.mult), "v")
        P.emit("vector", lambda h: h.tensor_scalar(
            stat[:, 1:2], stat[:, 1:2], 1.0 / D, None, ALU.mult), "v")
        P.emit("vector", lambda h: h.tensor_tensor(
            stat[:, 2:3], stat[:, 0:1], stat[:, 0:1], ALU.mult), "v")
        P.vgap()
        P.emit("vector", lambda h: h.tensor_tensor(
            stat[:, 2:3], stat[:, 1:2], stat[:, 2:3], ALU.subtract), "v")
        P.wait("scalar", P.now("v"))
        t_rs = P.emit("scalar", lambda h: h.activation(
            stat[:, 2:3], stat[:, 2:3], AF.Sqrt, bias=ecst[:, 0:1]), "s")
        P.wait("vector", t_rs)
        P.emit("vector", lambda h: h.reciprocal(stat[:, 3:4], stat[:, 2:3]), "v")
        P.vgap()
        P.wait("vector", t_lnt_free)
        return P.emit("vector", lambda h: h.tensor_scalar(
            lnt_dst, xsl, stat[:, 0:1], stat[:, 3:4], ALU.subtract, ALU.mult), "v")

    def ckpt(name):
        if until == name:
            raise _Stop

    ckpt("prologue")
    t_scan_prev = t_sinit
    for half in range(NHALF):
        hs = ExitStack()
        sbl = lambda n, sh, dt: hs.enter_context(
            nc.sbuf_tensor(f"{n}_{half}", sh, dt, side="left"))
        ACC = sbl("ACC", [128, NC, D], BF16)
        vtok = sbl("vtok", [128, NC, D], BF16)
        ktok = sbl("ktok", [128, NC, D], BF16)
        khatf = sbl("khatf", [128, CH, TH], BF16)
        psc = sbl("psc", [128, NC, 48], F32)
        qx = ExitStack()
        sbq = lambda n, sh, dt: qx.enter_context(
            nc.sbuf_tensor(f"{n}_{half}", sh, dt, side="left"))
        qf = sbq("qf", [128, CH, PAD + TH], BF16)
        kf = sbq("kf", [128, CH, PAD + TH], BF16)
        vf = sbq("vf", [128, CH, PAD + TH], BF16)

        ax = ExitStack()
        sba = lambda n, sh, dt: ax.enter_context(
            nc.sbuf_tensor(f"{n}_{half}", sh, dt, side="left"))
        xst = sba("xst", [128, 2, D], F32)
        lnt = sba("lnt", [128, D], BF16)
        ln1f = sba("ln1f", [128, CH, TH], BF16)
        war = sba("war", [128, CH, D], BF16)

        t_lnt_free = None
        for tc in range(0 if os.environ.get("LN_OFF") == "1" else NC):
            t0 = half * TH + tc * 128
            t_x = P.emit("sync", lambda h, tc=tc, t0=t0: h.dma_start(
                out=xst[:, tc % 2, :], in_=x_d.ap()[t0:t0 + 128, :]), "dl")
            t_ln = emit_ln_chunk(t_x, xst[:, tc % 2, :], lnt[:, :], t_lnt_free)
            P.wait("tensor", t_ln)
            pbT.begin_write()
            for cc in range(CH):
                tk = P.emit("tensor", lambda h, cc=cc: h.transpose(
                    pbT.h[:, cc * 128:(cc + 1) * 128],
                    lnt[:, cc * 128:(cc + 1) * 128], eyeb[:, :]),
                    "t" if cc == CH - 1 else None)
            pbT.end_write(tk)
            t_lnt_free = tk
            pbT.read_from("scalar")
            tev = P.emit("scalar", lambda h, tc=tc: h.activation(
                flat(ln1f[:, :, tc * 128:(tc + 1) * 128]), pbT.h[:, :], AF.Copy), "s")
            pbT.add_reader(tev)
            if os.environ.get("NO_GATE") == "1":
                continue
            bg = pb[4]
            bg.begin_write()
            P.wait("tensor", tev)
            for kc in range(CH):
                tk = P.emit("tensor", lambda h, tc=tc, kc=kc: h.matmul(
                    bg.h[:, 0:24], ln1f[:, kc, tc * 128:(tc + 1) * 128],
                    wgb[:, kc, :], start=(kc == 0), stop=(kc == CH - 1)),
                    "t" if kc == CH - 1 else None)
            bg.end_write(tk)
            bg.read_from("vector")
            P.wait("vector", P.now("s"))
            t_gl = P.emit("vector", lambda h: h.tensor_tensor(
                stat[:, 16:36], bg.h[:, 0:20], gbx[:, :], ALU.add), "v")
            bg.read_from("scalar")
            t_bt = P.emit("scalar", lambda h, tc=tc: h.activation(
                psc[:, tc, 20:24], bg.h[:, 20:24], AF.Sigmoid), "s")
            bg.add_reader(t_gl)
            bg.add_reader(t_bt)
            P.wait("scalar", t_gl)
            NPH = 0 if os.environ.get("NO_PROBS") == "1" else NH
            if NPH:
                P.vgap()
            for hh in range(NPH):
                P.emit("vector", lambda h, hh=hh: h.reduce_max(
                    stat[:, 36 + hh:37 + hh], stat[:, 16 + 5 * hh:21 + 5 * hh], AXX), "v")
            for hh in range(NPH):
                P.emit("vector", lambda h, hh=hh: h.tensor_scalar(
                    stat[:, 40 + hh:41 + hh], stat[:, 36 + hh:37 + hh],
                    scb[:, 3 * hh:3 * hh + 1], -1.0, ALU.mult, ALU.mult), "v")
            P.wait("scalar", P.now("v"))
            t_e = None
            for hh in range(NPH):
                t_e = P.emit("scalar", lambda h, hh=hh: h.activation(
                    stat[:, 16 + 5 * hh:21 + 5 * hh], stat[:, 16 + 5 * hh:21 + 5 * hh],
                    AF.Exp, bias=stat[:, 40 + hh:41 + hh],
                    scale=scb[:, 3 * hh:3 * hh + 1]), "s")
            P.wait("vector", t_e if NPH else None)
            for hh in range(NPH):
                P.emit("vector", lambda h, hh=hh: h.tensor_reduce(
                    stat[:, 36 + hh:37 + hh], stat[:, 16 + 5 * hh:21 + 5 * hh], AXX,
                    ALU.add), "v")
            for hh in range(NPH):
                P.emit("vector", lambda h, hh=hh: h.reciprocal(
                    stat[:, 40 + hh:41 + hh], stat[:, 36 + hh:37 + hh]), "v")
            for hh in range(NPH):
                P.emit("vector", lambda h, hh=hh: h.tensor_tensor(
                    stat[:, 40 + hh:41 + hh], stat[:, 40 + hh:41 + hh],
                    scb[:, 3 * hh + 2:3 * hh + 3], ALU.mult), "v")
            for hh in range(NPH):
                P.emit("vector", lambda h, tc=tc, hh=hh: h.tensor_scalar(
                    psc[:, tc, 5 * hh:5 * hh + 5], stat[:, 16 + 5 * hh:21 + 5 * hh],
                    stat[:, 40 + hh:41 + hh], scb[:, 3 * hh + 1:3 * hh + 2],
                    ALU.mult, ALU.add), "v")
            if tc + 2 < NC:
                P.wait("sync", P.now("v"))
                P.wait("sync", P.now("s"))
        t_ln_all = P.now("s")
        ckpt("ln" + str(half))

        if "ln1f" in tapd:
            P.wait("scalar", t_ln_all)
            for cc in range(CH):
                P.emit("scalar", lambda h, cc=cc: h.dma_start(
                    out=tapd["ln1f"].ap()[half, cc], in_=ln1f[:, cc, :]), "ds")

        t_war_free = None
        for wi, (wsrc, dst) in enumerate(((wq_d, qf), (wk_d, kf), (wv_d, vf))):
            P.wait("gpsimd", t_war_free)
            t_w = P.emit("gpsimd", lambda h, wsrc=wsrc: h.dma_start(
                out=war[:, :, :], in_=wsrc.ap().transpose([1, 0, 2])), "dg")
            P.wait("tensor", t_w)
            P.wait("tensor", t_ln_all)
            ckpt(f"pwdma{half}_{wi}")
            for tt in range(int(os.environ.get("TT_LIMIT", NT))):
                for mc in range(int(os.environ.get("MC_LIMIT", CH))):
                    bk = pb[int(os.environ.get('PB_BASE', '0')) + mc % 2] if os.environ.get('PB_ONE') != '1' else pb[0]
                    bk.begin_write()
                    for kc in range(CH):
                        mco = 0 if os.environ.get("LHS0") == "1" else mc
                        tk = P.emit("tensor", lambda h, mco=mco, kc=kc, tt=tt: h.matmul(
                            bk.h[:, 0:512], war[:, kc, mco * 128:(mco + 1) * 128],
                            ln1f[:, kc, tt * 512:tt * 512 + 512],
                            start=(kc == 0), stop=(kc == CH - 1)),
                            "t" if kc == CH - 1 else None)
                    bk.end_write(tk)
                    if os.environ.get("EVAC_V") == "1":
                        bk.read_from("vector")
                        tev = P.emit("vector", lambda h, mc=mc, tt=tt, dst=dst:
                                     h.tensor_copy(
                            dst[:, mc, PAD + tt * 512:PAD + tt * 512 + 512],
                            bk.h[:, 0:512]), "v")
                    else:
                        bk.read_from("scalar")
                        tev = P.emit("scalar", lambda h, mc=mc, tt=tt, dst=dst: h.activation(
                            dst[:, mc, PAD + tt * 512:PAD + tt * 512 + 512],
                            bk.h[:, 0:512], AF.Copy), "s")
                    bk.add_reader(tev)
                    ckpt(f"pwmm{half}_{wi}_{mc}")
            t_war_free = P.now("t")
            ckpt(f"pw{half}_{wi}")
        t_proj = P.now("s")
        ckpt("proj" + str(half))

        if half == 0 and os.environ.get("NO_PADMS") != "1":
            P.wait("vector", t_sinit)
            for buf in (qf, kf, vf):
                P.emit("vector", lambda h, buf=buf: h.memset(
                    flat(buf[:, :, 0:PAD]), 0.0), "v")
        else:
            P.wait("vector", t_proj)
            for buf, car in ((qf, cq), (kf, ck), (vf, cv)):
                P.emit("vector", lambda h, buf=buf, car=car: h.tensor_copy(
                    flat(buf[:, :, 0:PAD]), flat(car[:, :, :])), "v")
        t_pad = P.now("v")
        P.barrier()
        ax.close()

        bx = ExitStack()
        da = bx.enter_context(nc.sbuf_tensor(f"da_{half}", [128, 117, 128], BF16,
                                             side="left"))
        fsb = bx.enter_context(nc.sbuf_tensor(f"fs_{half}", [128, TH], BF16,
                                              side="left"))
        P.wait("vector", t_proj)
        if half + 1 < NHALF:
            for buf, car in ((qf, cq), (kf, ck), (vf, cv)):
                P.emit("vector", lambda h, buf=buf, car=car: h.tensor_copy(
                    flat(car[:, :, :]), flat(buf[:, :, TH:TH + PAD])), "v")
        t_stash = P.now("v")

        t_mmB = {}
        t_daB = {}

        def load_shorts(cc):
            sb = 12 * (cc % 2)
            P.wait("gpsimd", t_mmB.get(cc - 2))
            t_daB[cc] = P.emit("gpsimd", lambda h, cc=cc, sb=sb: h.dma_start(
                out=da[:, sb:sb + 12, :],
                in_=diag_d.ap()[cc, 0:12, :, :].transpose([1, 0, 2])), "dg")

        load_shorts(0)
        load_shorts(1)
        for cc in range(CH):
            sb = 12 * (cc % 2)
            P.wait("tensor", t_daB[cc])
            P.wait("tensor", t_stash)
            P.wait("tensor", t_pad)
            for ti, (buf, f0) in enumerate(((qf, sb + 0), (kf, sb + 4), (vf, sb + 8))):
                b0, b1 = pb[2 * (ti % 2)], pb[2 * (ti % 2) + 1]
                b0.begin_write()
                b1.begin_write()
                for j in range(4):
                    off = PAD - 3 + j
                    for tt in range(NT):
                        bk = b0 if tt % 2 == 0 else b1
                        tk = P.emit("tensor", lambda h, cc=cc, j=j, f0=f0, tt=tt,
                                    off=off, buf=buf, bk=bk: h.matmul(
                            bk.h[:, 0:512], da[:, f0 + j, :],
                            buf[:, cc, off + tt * 512:off + tt * 512 + 512],
                            start=(j == 0), stop=(j == 3)),
                            "t" if j == 3 else None)
                        (b0 if tt % 2 == 0 else b1).end_write(tk)
                for tt in range(NT):
                    bk = b0 if tt % 2 == 0 else b1
                    bk.read_from("scalar")
                    tev = P.emit("scalar", lambda h, cc=cc, tt=tt, buf=buf, bk=bk:
                                 h.activation(
                        buf[:, cc, PAD + tt * 512:PAD + tt * 512 + 512],
                        bk.h[:, 0:512], AF.Silu), "s")
                    bk.add_reader(tev)
            t_mmB[cc] = P.now("t")
            if cc + 2 < CH:
                load_shorts(cc + 2)
        t_conv = P.now("s")
        ckpt("conv" + str(half))

        P.wait("vector", t_conv)
        if half > 0:
            P.emit("vector", lambda h: h.tensor_copy(
                flat(vf[:, :, 0:PAD]), flat(cvp[:, :, :])), "v")
        if half + 1 < NHALF:
            P.emit("vector", lambda h: h.tensor_copy(
                flat(cvp[:, :, :]), flat(vf[:, :, TH:TH + PAD])), "v")
        t_vpad = P.now("v")

        for nm, buf in (("qf", qf), ("kf", kf), ("vf", vf)):
            if nm in tapd:
                P.wait("scalar", t_conv)
                P.wait("scalar", t_vpad)
                for cc in range(CH):
                    P.emit("scalar", lambda h, nm=nm, cc=cc, buf=buf: h.dma_start(
                        out=tapd[nm].ap()[half, cc], in_=buf[:, cc, PAD:PAD + TH]), "ds")

        for tname in ("k", "q", "v"):
            src = {"k": kf, "q": qf, "v": vf}[tname]
            t_need = t_vpad if tname == "v" else t_conv
            for tc in range(NC):
                pbT.begin_write()
                P.wait("tensor", t_need)
                for cc in range(CH):
                    tk = P.emit("tensor", lambda h, src=src, cc=cc, tc=tc: h.transpose(
                        pbT.h[:, cc * 128:(cc + 1) * 128],
                        src[:, cc, PAD + tc * 128:PAD + tc * 128 + 128],
                        eyeb[:, :]), "t" if cc == CH - 1 else None)
                pbT.end_write(tk)
                if tname in ("k", "q"):
                    dc = 24 if tname == "k" else 28
                    pbT.read_from("scalar")
                    for hh in range(NH):
                        tr = P.emit("scalar", lambda h, hh=hh, tc=tc, dc=dc:
                                    h.activation(
                            scrap[:, 256 * hh:256 * hh + 256],
                            pbT.h[:, hh * 256:hh * 256 + 256], AF.Square,
                            accum_out=psc[:, tc, dc + hh:dc + hh + 1]), "s")
                        pbT.add_reader(tr)
                    P.sgap()
                    t_s = P.emit("scalar", lambda h, tc=tc, dc=dc: h.activation(
                        psc[:, tc, dc:dc + 4], psc[:, tc, dc:dc + 4], AF.Sqrt,
                        bias=ecst[:, 1:2]), "s")
                    P.wait("vector", t_s)
                    P.emit("vector", lambda h, tc=tc, dc=dc: h.reciprocal(
                        psc[:, tc, dc:dc + 4], psc[:, tc, dc:dc + 4]), "v")
                    if tname == "k":
                        pbT.read_from("vector")
                        P.vgap()
                        for hh in range(NH):
                            tr = P.emit("vector", lambda h, hh=hh, tc=tc: h.tensor_scalar(
                                ktok[:, tc, hh * 256:hh * 256 + 256],
                                pbT.h[:, hh * 256:hh * 256 + 256],
                                psc[:, tc, 24 + hh:25 + hh], None, ALU.mult), "v")
                            pbT.add_reader(tr)
                else:
                    pbT.read_from("scalar")
                    tr = P.emit("scalar", lambda h, tc=tc: h.activation(
                        vtok[:, tc, :], pbT.h[:, :], AF.Copy), "s")
                    pbT.add_reader(tr)
        P.wait("vector", P.now("s"))
        for tc in range(NC):
            P.emit("vector", lambda h, tc=tc: h.tensor_tensor(
                psc[:, tc, 32:36], psc[:, tc, 24:28], psc[:, tc, 20:24], ALU.mult), "v")
            for hh in range(NH):
                P.emit("vector", lambda h, tc=tc, hh=hh: h.tensor_tensor(
                    psc[:, tc, 36 + hh:37 + hh], psc[:, tc, 28 + hh:29 + hh],
                    psc[:, tc, 5 * hh + 3:5 * hh + 4], ALU.mult), "v")
        t_scl = P.now("v")
        P.wait("tensor", t_scl)
        for tc in range(NC):
            for g in range(2):
                pbS.begin_write()
                for cc in range(4 * g, 4 * g + 4):
                    tk = P.emit("tensor", lambda h, cc=cc, tc=tc: h.transpose(
                        pbS.h[:, (cc % 4) * 128:(cc % 4) * 128 + 128],
                        ktok[:, tc, cc * 128:(cc + 1) * 128], eyeb[:, :]),
                        "t" if cc % 4 == 3 else None)
                pbS.end_write(tk)
                pbS.read_from("scalar")
                tr = P.emit("scalar", lambda h, g=g, tc=tc: h.activation(
                    flat(khatf[:, 4 * g:4 * g + 4, tc * 128:tc * 128 + 128]),
                    pbS.h[:, 0:512], AF.Copy), "s")
                pbS.add_reader(tr)
        t_phC = P.now("s")
        ckpt("phC" + str(half))

        for nm, buf in (("ktok", ktok), ("vtok", vtok)):
            if nm in tapd:
                P.wait("scalar", t_phC)
                for tc in range(NC):
                    P.emit("scalar", lambda h, nm=nm, tc=tc, buf=buf: h.dma_start(
                        out=tapd[nm].ap()[half, tc], in_=buf[:, tc, :]), "ds")
        if "khatf" in tapd:
            P.wait("scalar", t_phC)
            for cc in range(CH):
                P.emit("scalar", lambda h, cc=cc: h.dma_start(
                    out=tapd["khatf"].ap()[half, cc], in_=khatf[:, cc, :]), "ds")

        P.wait("vector", t_phC)
        for tc in range(NC):
            for hh in range(NH):
                P.emit("vector", lambda h, tc=tc, hh=hh: h.tensor_scalar(
                    ACC[:, tc, hh * 256:hh * 256 + 256],
                    vtok[:, tc, hh * 256:hh * 256 + 256],
                    psc[:, tc, 5 * hh + 4:5 * hh + 5], None, ALU.mult), "v")
        t_acc0 = P.now("v")

        t_smm = {}
        t_lmm = {}
        t_dasm = {}
        t_dal = {}

        def load_sm(cc):
            sb = 18 * (cc % 2)
            P.wait("gpsimd", t_smm.get(cc - 2))
            t_dasm[cc] = P.emit("gpsimd", lambda h, cc=cc, sb=sb: h.dma_start(
                out=da[:, sb:sb + 18, :],
                in_=diag_d.ap()[cc, 12:30, :, :].transpose([1, 0, 2])), "dg")

        def load_l(cc):
            P.wait("gpsimd", t_lmm.get(cc - 1))
            t_dal[cc] = P.emit("gpsimd", lambda h, cc=cc: h.dma_start(
                out=da[:, 36:99, :],
                in_=diag_d.ap()[cc, 30:93, :, :].transpose([1, 0, 2])), "dg")

        P.wait("gpsimd", P.now("t"))
        load_sm(0)
        load_l(0)
        load_sm(1)
        for cc in range(CH):
            smb = 18 * (cc % 2)
            hh = cc // 2
            fir_specs = ((smb + 0, 3, 0), (smb + 3, 15, 1), (36, 63, 2))
            for si, (lo, ntap, pcol) in enumerate(fir_specs):
                b0, b1 = pb[0], pb[1]
                b0.begin_write()
                b1.begin_write()
                P.wait("tensor", t_dal[cc] if si == 2 else t_dasm[cc])
                P.wait("tensor", t_vpad)
                for j in range(ntap):
                    off = PAD - (ntap - 1) + j
                    for tt in range(NT):
                        bk = b0 if tt % 2 == 0 else b1
                        tk = P.emit("tensor", lambda h, cc=cc, lo=lo, j=j, off=off,
                                    tt=tt, bk=bk, ntap=ntap: h.matmul(
                            bk.h[:, 0:512], da[:, lo + j, :],
                            vf[:, cc, off + tt * 512:off + tt * 512 + 512],
                            start=(j == 0), stop=(j == ntap - 1)),
                            "t" if j == ntap - 1 else None)
                        (b0 if tt % 2 == 0 else b1).end_write(tk)
                if si == 1:
                    t_smm[cc] = P.now("t")
                    if cc + 2 < CH:
                        load_sm(cc + 2)
                if si == 2:
                    t_lmm[cc] = P.now("t")
                    if cc + 1 < CH:
                        load_l(cc + 1)
                for tt in range(NT):
                    bk = b0 if tt % 2 == 0 else b1
                    bk.read_from("scalar")
                    tev = P.emit("scalar", lambda h, tt=tt, bk=bk: h.activation(
                        fsb[:, tt * 512:tt * 512 + 512], bk.h[:, 0:512], AF.Copy), "s")
                    bk.add_reader(tev)
                t_fs = P.now("s")
                P.wait("tensor", t_fs)
                P.wait("tensor", t_acc0)
                for tc in range(NC):
                    pbS.begin_write()
                    tk = P.emit("tensor", lambda h, tc=tc: h.transpose(
                        pbS.h[:, (tc % 4) * 128:(tc % 4) * 128 + 128],
                        fsb[:, tc * 128:tc * 128 + 128], eyeb[:, :]), "t")
                    pbS.end_write(tk)
                    pbS.read_from("vector")
                    tr = P.emit("vector", lambda h, tc=tc, cc=cc, hh=hh, pcol=pcol:
                                h.scalar_tensor_tensor(
                        ACC[:, tc, cc * 128:(cc + 1) * 128],
                        pbS.h[:, (tc % 4) * 128:(tc % 4) * 128 + 128],
                        psc[:, tc, 5 * hh + pcol:5 * hh + pcol + 1],
                        ACC[:, tc, cc * 128:(cc + 1) * 128], ALU.mult, ALU.add), "v")
                    pbS.add_reader(tr)
        t_fir = P.now("v")
        P.barrier()
        bx.close()
        ckpt("fir" + str(half))

        if "accmix" in tapd:
            P.wait("scalar", t_fir)
            for tc in range(NC):
                P.emit("scalar", lambda h, tc=tc: h.dma_start(
                    out=tapd["accmix"].ap()[half, tc], in_=ACC[:, tc, :]), "ds")
        if "psc" in tapd:
            P.wait("scalar", t_fir)
            P.wait("scalar", t_scl)
            for tc in range(NC):
                P.emit("scalar", lambda h, tc=tc: h.dma_start(
                    out=tapd["psc"].ap()[half, tc], in_=psc[:, tc, 0:40]), "ds")

        ex = ExitStack()
        sbe = lambda n, sh, dt: ex.enter_context(
            nc.sbuf_tensor(f"{n}_{half}", sh, dt, side="left"))
        dA = sbe("dA", [128, NH, 128], BF16)
        dC = sbe("dC", [128, NH, 128], BF16)
        dB = sbe("dB", [128, NH, 128], BF16)
        dQa = sbe("dQa", [128, NH, 128], BF16)
        dQb = sbe("dQb", [128, NH, 128], BF16)
        du = sbe("du", [128, NH, 256], BF16)
        dua = sbe("dua", [128, NH, 256], BF16)
        datn = sbe("datn", [128, NH, 128], BF16)
        dwf = sbe("dwf", [128, 2 * NH, 128], BF16)
        dvh = sbe("dvh", [128, NH, 256], BF16)
        dkb = sbe("dkb", [128, NH, 256], BF16)

        NLEV = 6
        for tc in range(NC):
            tr0 = PAD + tc * 128
            P.wait("vector", t_fir if tc == 0 else P.now("v"))
            for hh in range(NH):
                P.emit("vector", lambda h, tc=tc, hh=hh: h.tensor_scalar(
                    dvh[:, hh, :], vtok[:, tc, hh * 256:hh * 256 + 256],
                    psc[:, tc, 20 + hh:21 + hh], None, ALU.mult), "v")
                P.emit("vector", lambda h, tc=tc, hh=hh: h.tensor_scalar(
                    dkb[:, hh, :], ktok[:, tc, hh * 256:hh * 256 + 256],
                    psc[:, tc, 20 + hh:21 + hh], None, ALU.mult), "v")
            t_sc = P.now("v")

            pb7.begin_write()
            P.wait("tensor", t_sc)
            for hh in range(NH):
                for kc in range(2):
                    tk = P.emit("tensor", lambda h, hh=hh, kc=kc, tc=tc, tr0=tr0:
                                h.matmul(
                        pb7.h[:, hh * 128:hh * 128 + 128],
                        kf[:, 2 * hh + kc, tr0:tr0 + 128],
                        khatf[:, 2 * hh + kc, tc * 128:tc * 128 + 128],
                        start=(kc == 0), stop=(kc == 1)),
                        "t" if hh == NH - 1 and kc == 1 else None)
            pb7.end_write(tk)
            pb7.read_from("vector")
            for hh in range(NH):
                tr = P.emit("vector", lambda h, hh=hh, tc=tc: h.scalar_tensor_tensor(
                    dA[:, hh, :], pb7.h[:, hh * 128:hh * 128 + 128],
                    psc[:, tc, 32 + hh:33 + hh], negm[:, :], ALU.mult, ALU.mult), "v")
            pb7.add_reader(tr)
            t_A = P.now("v")

            pb[2].begin_write()
            P.wait("tensor", t_A)
            for hh in range(NH):
                for kc in range(2):
                    tk = P.emit("tensor", lambda h, hh=hh, kc=kc, tc=tc, tr0=tr0:
                                h.matmul(
                        pb[2].h[:, hh * 128:hh * 128 + 128],
                        khatf[:, 2 * hh + kc, tc * 128:tc * 128 + 128],
                        qf[:, 2 * hh + kc, tr0:tr0 + 128],
                        start=(kc == 0), stop=(kc == 1)),
                        "t" if hh == NH - 1 and kc == 1 else None)
            pb[2].end_write(tk)
            pb[2].read_from("vector")
            t_atn = P.emit("vector", lambda h: h.tensor_tensor(
                datn[:, :, :],
                pb[2].h[:, 0:512].rearrange("p (a b) -> p a b", a=NH),
                cmT[:, :].unsqueeze(1).broadcast_to([128, NH, 128]), ALU.mult), "v")
            pb[2].add_reader(t_atn)

            grpbank = (pb7, pb[3])
            trview = (pbT.h[:, 0:256],
                      pb[2].h[:, :].bitcast(BF16)[:, 0:256])
            trbank = (pbT, pb[2])
            t_c = [None, None]
            t_b = [None, None]
            t_q = [None, None]
            Qcur, Qnxt = dQa, dQb
            for g in range(2):
                bk = trbank[g]
                bk.begin_write()
                P.wait("tensor", t_A)
                for j in range(2):
                    hh = 2 * g + j
                    tk = P.emit("tensor", lambda h, hh=hh, j=j, g=g: h.transpose(
                        trview[g][:, j * 128:j * 128 + 128], dA[:, hh, :],
                        eyeb[:, :]), "t" if j == 1 else None)
                bk.end_write(tk)
            for g in range(2):
                bk = trbank[g]
                bk.read_from("scalar")
                t_c[g] = P.emit("scalar", lambda h, g=g: h.activation(
                    flat(dC[:, 2 * g:2 * g + 2, :]), trview[g][:, 0:256],
                    AF.Copy), "s")
                bk.add_reader(t_c[g])
            for g in range(2):
                P.wait("vector", t_c[g])
                t_q[g] = P.emit("vector", lambda h, g=g: h.tensor_tensor(
                    Qcur[:, 2 * g:2 * g + 2, :], dC[:, 2 * g:2 * g + 2, :],
                    eyeb[:, :].unsqueeze(1).broadcast_to([128, 2, 128]),
                    ALU.add), "v")

            for lev in range(NLEV):
                for g in range(2):
                    bk = grpbank[g]
                    bk.begin_write()
                    P.wait("tensor", t_b[g] if lev > 0 else t_A)
                    P.wait("tensor", t_c[g])
                    for j in range(2):
                        hh = 2 * g + j
                        src = dB if lev > 0 else dA
                        tk = P.emit("tensor", lambda h, hh=hh, j=j, g=g, src=src:
                                    h.matmul(
                            grpbank[g].h[:, j * 128:j * 128 + 128], src[:, hh, :],
                            dC[:, hh, :], start=True, stop=True),
                            "t" if j == 1 else None)
                    bk.end_write(tk)
                for g in range(2):
                    bk = grpbank[g]
                    bk.read_from("scalar")
                    t_c[g] = P.emit("scalar", lambda h, g=g: h.activation(
                        flat(dC[:, 2 * g:2 * g + 2, :]),
                        grpbank[g].h[:, 0:256], AF.Copy), "s")
                    bk.add_reader(t_c[g])
                for g in range(2):
                    bk = trbank[g]
                    bk.begin_write()
                    P.wait("tensor", t_c[g])
                    for j in range(2):
                        hh = 2 * g + j
                        tk = P.emit("tensor", lambda h, hh=hh, j=j, g=g: h.transpose(
                            trview[g][:, j * 128:j * 128 + 128], dC[:, hh, :],
                            eyeb[:, :]), "t" if j == 1 else None)
                    bk.end_write(tk)
                for g in range(2):
                    bk = trbank[g]
                    bk.read_from("scalar")
                    t_b[g] = P.emit("scalar", lambda h, g=g: h.activation(
                        flat(dB[:, 2 * g:2 * g + 2, :]), trview[g][:, 0:256],
                        AF.Copy), "s")
                    bk.add_reader(t_b[g])
                for g in range(2):
                    bk = grpbank[g]
                    bk.begin_write()
                    P.wait("tensor", t_b[g])
                    P.wait("tensor", t_q[g])
                    for j in range(2):
                        hh = 2 * g + j
                        P.emit("tensor", lambda h, hh=hh, j=j, g=g, Q=Qcur: h.matmul(
                            grpbank[g].h[:, j * 128:j * 128 + 128], dB[:, hh, :],
                            Q[:, hh, :], start=True, stop=False), None)
                        tk = P.emit("tensor", lambda h, hh=hh, j=j, g=g, Q=Qcur:
                                    h.matmul(
                            grpbank[g].h[:, j * 128:j * 128 + 128], eyeb[:, :],
                            Q[:, hh, :], start=False, stop=True),
                            "t" if j == 1 else None)
                    bk.end_write(tk)
                for g in range(2):
                    bk = grpbank[g]
                    bk.read_from("scalar")
                    t_q[g] = P.emit("scalar", lambda h, g=g, Q=Qnxt: h.activation(
                        flat(Q[:, 2 * g:2 * g + 2, :]),
                        grpbank[g].h[:, 0:256], AF.Copy), "s")
                    bk.add_reader(t_q[g])
                Qcur, Qnxt = Qnxt, Qcur
            dTT = Qcur
            P.wait("tensor", t_q[0])
            t_tt = t_q[1]

            pb[2].begin_write()
            pb[4].begin_write()
            P.wait("tensor", t_tt)
            for i in range(8):
                hh, dv = divmod(i, 2)
                bk = pb[2] if i < 4 else pb[4]
                tk = P.emit("tensor", lambda h, hh=hh, dv=dv, bk=bk, i=i, dTT=dTT:
                            h.matmul(
                    bk.h[:, (i % 4) * 128:(i % 4) * 128 + 128],
                    dTT[:, hh, :], dvh[:, hh, dv * 128:dv * 128 + 128],
                    start=True, stop=True), "t" if i in (3, 7) else None)
                if i == 3:
                    pb[2].end_write(tk)
            pb[4].end_write(tk)
            pb[2].read_from("scalar")
            tu1 = P.emit("scalar", lambda h: h.activation(
                flat(du[:, 0:2, :]), pb[2].h[:, 0:512], AF.Copy), "s")
            pb[2].add_reader(tu1)
            pb[4].read_from("scalar")
            tu2 = P.emit("scalar", lambda h: h.activation(
                flat(du[:, 2:4, :]), pb[4].h[:, 0:512], AF.Copy), "s")
            pb[4].add_reader(tu2)

            pb[5].begin_write()
            pb7.begin_write()
            for i in range(8):
                hh, kc = divmod(i, 2)
                bk = pb[5] if i < 4 else pb7
                tk = P.emit("tensor", lambda h, hh=hh, kc=kc, bk=bk, i=i, dTT=dTT:
                            h.matmul(
                    bk.h[:, (i % 4) * 128:(i % 4) * 128 + 128],
                    dkb[:, hh, kc * 128:kc * 128 + 128], dTT[:, hh, :],
                    start=True, stop=True), "t" if i in (3, 7) else None)
                if i == 3:
                    pb[5].end_write(tk)
            pb7.end_write(tk)
            pb[5].read_from("scalar")
            tw1 = P.emit("scalar", lambda h: h.activation(
                flat(dwf[:, 0:4, :]), pb[5].h[:, 0:512], AF.Copy), "s")
            pb[5].add_reader(tw1)
            pb7.read_from("scalar")
            tw2 = P.emit("scalar", lambda h: h.activation(
                flat(dwf[:, 4:8, :]), pb7.h[:, 0:512], AF.Copy), "s")
            pb7.add_reader(tw2)
            t_wf = P.now("s")

            if tc == 0 and half == 0:
                for nm, buf, tk0 in (("tt0", dTT, t_tt), ("u0", du, t_wf),
                                     ("wf0", dwf, t_wf), ("atn0", datn, t_wf)):
                    if nm in tapd:
                        P.wait("scalar", tk0)
                        nslice = buf.shape[1] if hasattr(buf, "shape") else NH
                        for a in range(nslice):
                            P.emit("scalar", lambda h, nm=nm, a=a, buf=buf: h.dma_start(
                                out=tapd[nm].ap()[a], in_=buf[:, a, :]), "ds")

            pb[0].begin_write()
            pb[1].begin_write()
            P.wait("tensor", t_wf)
            P.wait("tensor", t_scan_prev)
            for hh in range(NH):
                bk = pb[0] if hh < 2 else pb[1]
                for kc in range(2):
                    tk = P.emit("tensor", lambda h, hh=hh, kc=kc, bk=bk: h.matmul(
                        bk.h[:, (hh % 2) * 256:(hh % 2) * 256 + 256],
                        dwf[:, 2 * hh + kc, :], Sbf[:, 2 * hh + kc, :],
                        start=(kc == 0), stop=(kc == 1)),
                        "t" if kc == 1 and hh in (1, 3) else None)
                if hh == 1:
                    pb[0].end_write(tk)
            pb[1].end_write(tk)
            pb[0].read_from("vector")
            ta1 = P.emit("vector", lambda h: h.tensor_tensor(
                flat(dua[:, 0:2, :]), flat(du[:, 0:2, :]), pb[0].h[:, 0:512],
                ALU.subtract), "v")
            pb[0].add_reader(ta1)
            pb[1].read_from("vector")
            ta2 = P.emit("vector", lambda h: h.tensor_tensor(
                flat(dua[:, 2:4, :]), flat(du[:, 2:4, :]), pb[1].h[:, 0:512],
                ALU.subtract), "v")
            pb[1].add_reader(ta2)
            t_ua = P.now("v")

            pb[2].begin_write()
            pb[3].begin_write()
            P.wait("tensor", t_ua)
            for hh in range(NH):
                bk = pb[2] if hh < 2 else pb[3]
                for kc in range(2):
                    P.emit("tensor", lambda h, hh=hh, kc=kc, bk=bk, tc=tc, tr0=tr0:
                           h.matmul(
                        bk.h[:, (hh % 2) * 256:(hh % 2) * 256 + 256],
                        qf[:, 2 * hh + kc, tr0:tr0 + 128], Sbf[:, 2 * hh + kc, :],
                        start=(kc == 0), stop=False), None)
                tk = P.emit("tensor", lambda h, hh=hh, bk=bk: h.matmul(
                    bk.h[:, (hh % 2) * 256:(hh % 2) * 256 + 256],
                    datn[:, hh, :], dua[:, hh, :], start=False, stop=True),
                    "t" if hh in (1, 3) else None)
                if hh == 1:
                    pb[2].end_write(tk)
            pb[3].end_write(tk)
            for hh in range(NH):
                bk = pb[2] if hh < 2 else pb[3]
                bk.read_from("vector")
                tr = P.emit("vector", lambda h, hh=hh, bk=bk, tc=tc:
                            h.scalar_tensor_tensor(
                    ACC[:, tc, hh * 256:hh * 256 + 256],
                    bk.h[:, (hh % 2) * 256:(hh % 2) * 256 + 256],
                    psc[:, tc, 36 + hh:37 + hh],
                    ACC[:, tc, hh * 256:hh * 256 + 256], ALU.mult, ALU.add), "v")
                bk.add_reader(tr)

            t_supd = None
            for rnd in range(2):
                h0, h1 = 2 * rnd, 2 * rnd + 1
                pb[4].begin_write()
                pb[5].begin_write()
                P.wait("tensor", t_ua)
                for hh, bk in ((h0, pb[4]), (h1, pb[5])):
                    for kc in range(2):
                        tk = P.emit("tensor", lambda h, hh=hh, kc=kc, bk=bk, tc=tc:
                                    h.matmul(
                            bk.h[:, kc * 256:kc * 256 + 256],
                            ktok[:, tc, hh * 256 + kc * 128:hh * 256 + kc * 128 + 128],
                            dua[:, hh, :], start=True, stop=True),
                            "t" if kc == 1 else None)
                    if hh == h0:
                        pb[4].end_write(tk)
                pb[5].end_write(tk)
                for hh, bk in ((h0, pb[4]), (h1, pb[5])):
                    bk.read_from("vector")
                    ts1 = P.emit("vector", lambda h, hh=hh, bk=bk: h.tensor_tensor(
                        flat(S32[:, 2 * hh:2 * hh + 2, :]),
                        flat(S32[:, 2 * hh:2 * hh + 2, :]),
                        bk.h[:, 0:512], ALU.add), "v")
                    bk.add_reader(ts1)
                    P.wait("scalar", ts1)
                    t_supd = P.emit("scalar", lambda h, hh=hh: h.activation(
                        flat(Sbf[:, 2 * hh:2 * hh + 2, :]),
                        flat(S32[:, 2 * hh:2 * hh + 2, :]), AF.Copy), "s")
            t_scan_prev = t_supd

            ckpt("delta" + str(half) + "_" + str(tc))
        P.barrier()
        ex.close()
        qx.close()
        ckpt("delta" + str(half))

        if "accfin" in tapd:
            P.wait("scalar", P.now("v"))
            for tc in range(NC):
                P.emit("scalar", lambda h, tc=tc: h.dma_start(
                    out=tapd["accfin"].ap()[half, tc], in_=ACC[:, tc, :]), "ds")

        fx = ExitStack()
        sbf_ = lambda n, sh, dt: fx.enter_context(
            nc.sbuf_tensor(f"{n}_{half}", sh, dt, side="left"))
        ot = sbf_("ot", [128, D], BF16)
        of = sbf_("of", [128, CH, TH], BF16)
        wof = sbf_("wof", [128, CH, TH], F32)
        xr = sbf_("xr", [128, D], F32)
        xnt = sbf_("xnt", [128, D], F32)

        t_delta = P.now("v")
        t_ot_free = None
        for tc in range(NC):
            P.wait("vector", t_delta)
            for hh in range(NH):
                P.emit("vector", lambda h, tc=tc, hh=hh: h.tensor_tensor(
                    scrap[:, 256 * hh:256 * hh + 256],
                    ACC[:, tc, hh * 256:hh * 256 + 256],
                    ACC[:, tc, hh * 256:hh * 256 + 256], ALU.mult), "v")
            for hh in range(NH):
                P.emit("vector", lambda h, tc=tc, hh=hh: h.tensor_reduce(
                    psc[:, tc, 40 + hh:41 + hh], scrap[:, 256 * hh:256 * hh + 256],
                    AXX, ALU.add), "v")
            P.wait("scalar", P.now("v"))
            t_r = P.emit("scalar", lambda h, tc=tc: h.activation(
                psc[:, tc, 40:44], psc[:, tc, 40:44], AF.Sqrt,
                scale=1.0 / 256, bias=ecst[:, 0:1]), "s")
            P.wait("vector", t_r)
            P.emit("vector", lambda h, tc=tc: h.reciprocal(
                psc[:, tc, 40:44], psc[:, tc, 40:44]), "v")
            P.vgap()
            P.wait("vector", t_ot_free)
            for hh in range(NH):
                P.emit("vector", lambda h, tc=tc, hh=hh: h.tensor_scalar(
                    ot[:, hh * 256:hh * 256 + 256], ACC[:, tc, hh * 256:hh * 256 + 256],
                    psc[:, tc, 40 + hh:41 + hh], None, ALU.mult), "v")
            t_otk = P.now("v")
            P.wait("tensor", t_otk)
            pbT.begin_write()
            for cc in range(CH):
                tk = P.emit("tensor", lambda h, cc=cc: h.transpose(
                    pbT.h[:, cc * 128:(cc + 1) * 128], ot[:, cc * 128:(cc + 1) * 128],
                    eyeb[:, :]), "t" if cc == CH - 1 else None)
            pbT.end_write(tk)
            t_ot_free = tk
            pbT.read_from("scalar")
            tev = P.emit("scalar", lambda h, tc=tc: h.activation(
                flat(of[:, :, tc * 128:tc * 128 + 128]), pbT.h[:, :], AF.Copy), "s")
            pbT.add_reader(tev)
        t_of = P.now("s")
        ckpt("of" + str(half))

        if "of" in tapd:
            P.wait("scalar", t_of)
            for cc in range(CH):
                P.emit("scalar", lambda h, cc=cc: h.dma_start(
                    out=tapd["of"].ap()[half, cc], in_=of[:, cc, :]), "ds")

        P.wait("tensor", t_of)
        P.wait("tensor", t_wofold if half == 0 else t_of)
        for tt in range(NT):
            for mc in range(CH):
                bk = pb[mc % 2]
                bk.begin_write()
                for kc in range(CH):
                    tk = P.emit("tensor", lambda h, mc=mc, kc=kc, tt=tt: h.matmul(
                        bk.h[:, 0:512], wo[:, kc, mc * 128:(mc + 1) * 128],
                        of[:, kc, tt * 512:tt * 512 + 512],
                        start=(kc == 0), stop=(kc == CH - 1)),
                        "t" if kc == CH - 1 else None)
                bk.end_write(tk)
                bk.read_from("scalar")
                tev = P.emit("scalar", lambda h, mc=mc, tt=tt: h.activation(
                    wof[:, mc, tt * 512:tt * 512 + 512], bk.h[:, 0:512], AF.Copy), "s")
                bk.add_reader(tev)
        t_wof = P.now("s")
        ckpt("wof" + str(half))

        for tc in range(NC):
            t0 = half * TH + tc * 128
            t_x = P.emit("sync", lambda h, t0=t0: h.dma_start(
                out=xr[:, :], in_=x_d.ap()[t0:t0 + 128, :]), "dl")
            pb[4].begin_write()
            pb[5].begin_write()
            P.wait("tensor", t_wof)
            for cc in range(CH):
                bk = pb[4] if cc < 4 else pb[5]
                tk = P.emit("tensor", lambda h, cc=cc, tc=tc, bk=bk: h.transpose(
                    bk.h[:, (cc % 4) * 128:(cc % 4) * 128 + 128],
                    wof[:, cc, tc * 128:tc * 128 + 128], eyef[:, :]),
                    "t" if cc in (3, 7) else None)
                if cc == 3:
                    pb[4].end_write(tk)
            pb[5].end_write(tk)
            P.wait("vector", t_x)
            for g, bk in ((0, pb[4]), (1, pb[5])):
                bk.read_from("vector")
                tr = P.emit("vector", lambda h, g=g, bk=bk: h.tensor_tensor(
                    xnt[:, g * 512:g * 512 + 512], xr[:, g * 512:g * 512 + 512],
                    bk.h[:, 0:512], ALU.add), "v")
                bk.add_reader(tr)
            t_xn = P.now("v")
            P.wait("scalar", t_xn)
            t_st = P.emit("scalar", lambda h, t0=t0: h.dma_start(
                out=xnew_d.ap()[t0:t0 + 128, :], in_=xnt[:, :]), "ds")
            if "xnew" in tapd:
                P.emit("scalar", lambda h, t0=t0: h.dma_start(
                    out=tapd["xnew"].ap()[t0:t0 + 128, :], in_=xnt[:, :]), "ds")
            P.wait("sync", t_st)
            P.wait("vector", t_st)
        P.barrier()
        fx.close()
        hs.close()
        ckpt("resid" + str(half))

    P.barrier()
    att.close()
    ckpt("attn")

    fz = ExitStack()
    sbf2 = lambda n, sh, dt: fz.enter_context(
        nc.sbuf_tensor(n, sh, dt, side="left"))
    wgs = sbf2("wgs", [128, CH, HID], BF16)
    wus = sbf2("wus", [128, CH, HID], BF16)
    wds = sbf2("wds", [128, HC, D], BF16)
    hbuf = sbf2("hbuf", [128, HC, FT], BF16)
    ln2f = sbf2("ln2f", [128, CH, FT], BF16)
    ga = sbf2("ga", [128, FT], BF16)
    lnt2 = sbf2("lnt2", [128, D], BF16)
    xr2 = sbf2("xr2", [128, 2, D], F32)
    dwn = sbf2("dwn", [128, CH, FT], F32)
    ont = sbf2("ont", [128, D], F32)

    t_wgc = {}
    t_wuc = {}
    t_wdc = {}
    for kc in range(CH):
        t_wgc[kc] = P.emit("gpsimd", lambda h, kc=kc: h.dma_start(
            out=wgs[:, kc, :], in_=wg_d.ap()[kc]), "dg")
        t_wuc[kc] = P.emit("gpsimd", lambda h, kc=kc: h.dma_start(
            out=wus[:, kc, :], in_=wu_d.ap()[kc]), "dg")
    for kc in range(HC):
        t_wdc[kc] = P.emit("gpsimd", lambda h, kc=kc: h.dma_start(
            out=wds[:, kc, :], in_=wd_d.ap()[kc]), "dg")

    t_lnt2_free = None
    t_xr2_free = None
    t_tile_prev = None
    for ft in range(NFT):
        f0 = ft * FT
        P.wait("sync", t_xr2_free)
        P.wait("scalar", t_tile_prev)
        P.wait("vector", t_tile_prev)
        for tj in range(2):
            t0 = f0 + tj * 128
            t_x = P.emit("sync", lambda h, t0=t0, tj=tj: h.dma_start(
                out=xr2[:, tj, :], in_=xnew_d.ap()[t0:t0 + 128, :]), "dl")
            t_ln = emit_ln_chunk(t_x, xr2[:, tj, :], lnt2[:, :], t_lnt2_free)
            P.wait("tensor", t_ln)
            pbT.begin_write()
            for cc in range(CH):
                tk = P.emit("tensor", lambda h, cc=cc: h.transpose(
                    pbT.h[:, cc * 128:(cc + 1) * 128],
                    lnt2[:, cc * 128:(cc + 1) * 128], eyeb[:, :]),
                    "t" if cc == CH - 1 else None)
            pbT.end_write(tk)
            t_lnt2_free = tk
            pbT.read_from("scalar")
            tev = P.emit("scalar", lambda h, tj=tj: h.activation(
                flat(ln2f[:, :, tj * 128:tj * 128 + 128]), pbT.h[:, :], AF.Copy), "s")
            pbT.add_reader(tev)
        t_ln2 = P.now("s")
        if "ln2f" in tapd:
            P.wait("scalar", t_ln2)
            for cc in range(CH):
                P.emit("scalar", lambda h, cc=cc, ft=ft: h.dma_start(
                    out=tapd["ln2f"].ap()[ft, cc], in_=ln2f[:, cc, :]), "ds")

        P.wait("tensor", t_ln2)
        for mc in range(HC):
            bg_ = pb[(mc % 2) * 2]
            bu_ = pb[(mc % 2) * 2 + 1]
            bg_.begin_write()
            bu_.begin_write()
            for kc in range(CH):
                P.wait("tensor", t_wgc[kc])
                P.wait("tensor", t_wuc[kc])
                tkg = P.emit("tensor", lambda h, mc=mc, kc=kc, bg_=bg_: h.matmul(
                    bg_.h[:, 0:FT], wgs[:, kc, mc * 128:(mc + 1) * 128],
                    ln2f[:, kc, :], start=(kc == 0), stop=(kc == CH - 1)),
                    "t" if kc == CH - 1 else None)
                tk = P.emit("tensor", lambda h, mc=mc, kc=kc, bu_=bu_: h.matmul(
                    bu_.h[:, 0:FT], wus[:, kc, mc * 128:(mc + 1) * 128],
                    ln2f[:, kc, :], start=(kc == 0), stop=(kc == CH - 1)),
                    "t" if kc == CH - 1 else None)
            bg_.end_write(tkg)
            bu_.end_write(tk)
            bg_.read_from("scalar")
            t_g = P.emit("scalar", lambda h, bg_=bg_: h.activation(
                ga[:, :], bg_.h[:, 0:FT], AF.Silu, scale=QS), "s")
            bg_.add_reader(t_g)
            bu_.read_from("vector")
            P.wait("vector", t_g)
            t_h = P.emit("vector", lambda h, mc=mc, bu_=bu_: h.scalar_tensor_tensor(
                hbuf[:, mc, :], bu_.h[:, 0:FT], QS, ga[:, :], ALU.mult, ALU.mult), "v")
            bu_.add_reader(t_h)
            P.wait("scalar", t_h)
        t_hb = P.now("v")
        ckpt("gup" + str(ft))
        if "hbuf" in tapd:
            P.wait("scalar", t_hb)
            for mc in range(HC):
                P.emit("scalar", lambda h, mc=mc, ft=ft: h.dma_start(
                    out=tapd["hbuf"].ap()[ft, mc], in_=hbuf[:, mc, :]), "ds")

        P.wait("tensor", t_hb)
        for kc in range(HC):
            P.wait("tensor", t_wdc[kc])
        for mc in range(CH):
            bk = pb[2 + mc % 4]
            bk.begin_write()
            for kc in range(HC):
                tk = P.emit("tensor", lambda h, kc=kc, mc=mc, bk=bk: h.matmul(
                    bk.h[:, 0:FT], wds[:, kc, mc * 128:(mc + 1) * 128],
                    hbuf[:, kc, :], start=(kc == 0), stop=(kc == HC - 1)),
                    "t" if kc == HC - 1 else None)
            bk.end_write(tk)
            bk.read_from("scalar")
            tev = P.emit("scalar", lambda h, mc=mc, bk=bk: h.activation(
                dwn[:, mc, :], bk.h[:, 0:FT], AF.Copy), "s")
            bk.add_reader(tev)
        t_dwn = P.now("s")
        ckpt("dwn" + str(ft))

        for tj in range(2):
            t0 = f0 + tj * 128
            pb[0].begin_write()
            pb[1].begin_write()
            P.wait("tensor", t_dwn)
            for cc in range(CH):
                bk = pb[0] if cc < 4 else pb[1]
                tk = P.emit("tensor", lambda h, cc=cc, tj=tj, bk=bk: h.transpose(
                    bk.h[:, (cc % 4) * 128:(cc % 4) * 128 + 128],
                    dwn[:, cc, tj * 128:tj * 128 + 128], eyef[:, :]),
                    "t" if cc in (3, 7) else None)
                if cc == 3:
                    pb[0].end_write(tk)
            pb[1].end_write(tk)
            for g, bk in ((0, pb[0]), (1, pb[1])):
                bk.read_from("vector")
                tr = P.emit("vector", lambda h, g=g, tj=tj, bk=bk:
                            h.scalar_tensor_tensor(
                    ont[:, g * 512:g * 512 + 512], bk.h[:, 0:512], QS,
                    xr2[:, tj, g * 512:g * 512 + 512], ALU.mult, ALU.add), "v")
                bk.add_reader(tr)
            t_o = P.now("v")
            P.wait("scalar", t_o)
            t_ost = P.emit("scalar", lambda h, t0=t0: h.dma_start(
                out=out_d.ap()[t0:t0 + 128, :], in_=ont[:, :]), "ds")
            P.wait("vector", t_ost)
        t_xr2_free = t_o
        t_tile_prev = P.now("t")

    P.wait_queue("sync", "ds")
    P.wait_queue("sync", "dl")
    P.wait_queue("sync", "dg")

    assemble(nc, P)
    return nc, tapd


def assemble(nc, P):
    import bisect as _b
    keep = P.finalize_maps()
    sems = P.sems

    def runner(eng_name):
        def run(h):
            for rec in P.recs[eng_name]:
                kind = rec[0]
                if kind == "dma":
                    _, fn, key = rec
                    fn(h).then_inc(sems[key], 16)
                elif kind == "op":
                    _, fn, inc, pval = rec
                    ins = fn(h)
                    if inc is not None and pval in keep[inc][1]:
                        ins.then_inc(sems[inc], 1)
                else:
                    _, sem, val = rec
                    if sem in keep:
                        kept, m = keep[sem]
                        i = _b.bisect_left(kept, val)
                        if i >= len(kept):
                            i = len(kept) - 1
                        h.wait_ge(sems[sem], m[kept[i]])
                    else:
                        h.wait_ge(sems[sem], val)
        return run

    with nc.Block() as block:
        block.sync(runner("sync"))
        block.gpsimd(runner("gpsimd"))
        block.vector(runner("vector"))
        block.scalar(runner("scalar"))
        block.tensor(runner("tensor"))


def host_prep(inputs, L=2048):
    f32 = np.float32
    Wq = inputs["L0_attn_Wq"]
    Wk = inputs["L0_attn_Wk"]
    Wv = inputs["L0_attn_Wv"]
    Wo = inputs["L0_attn_Wo"]
    Wb = inputs["L0_attn_Wb"]
    gw = inputs["L0_attn_gate_w"].reshape(20, D)
    gb = inputs["L0_attn_gate_b"].reshape(20)
    conv_q = inputs["L0_attn_conv_q"]
    conv_k = inputs["L0_attn_conv_k"]
    conv_v = inputs["L0_attn_conv_v"]
    fir_s = inputs["L0_attn_fir_s"].reshape(D, 3)
    fir_m = inputs["L0_attn_fir_m"].reshape(D, 15)
    fir_l = inputs["L0_attn_fir_l"].reshape(D, 63)

    def wt(w):
        return np.ascontiguousarray(np.asarray(w, f32).T).reshape(CH, 128, D)

    gbw = np.concatenate([np.asarray(gw, f32), np.asarray(Wb, f32)], axis=0)
    wgbT = np.ascontiguousarray(gbw.T).reshape(CH, 128, 24)

    diag = np.zeros((CH, NTAP, 128, 128), dtype=f32)
    idx = np.arange(128)
    filts = [np.asarray(conv_q, f32), np.asarray(conv_k, f32),
             np.asarray(conv_v, f32), np.asarray(fir_s, f32),
             np.asarray(fir_m, f32), np.asarray(fir_l, f32)]
    for cc in range(CH):
        t = 0
        for fm in filts:
            K = fm.shape[1]
            for j in range(K):
                diag[cc, t, idx, idx] = fm[cc * 128:(cc + 1) * 128, j]
                t += 1
    assert t == NTAP

    m = {
        "wqT": wt(Wq), "wkT": wt(Wk), "wvT": wt(Wv), "woT": wt(Wo),
        "wgbT": wgbT, "diag_all": diag,
        "gb_exp": np.ascontiguousarray(np.broadcast_to(gb, (128, 20))).astype(f32),
        "log_temp": np.asarray(inputs["L0_attn_log_temp"], f32).reshape(4, 1),
        "eps_param": np.asarray(inputs["L0_attn_eps_param"], f32).reshape(4, 1),
        "nw": np.ascontiguousarray(
            np.asarray(inputs["L0_attn_norm_w"], f32).reshape(2, 128).T),
        "wgT": np.ascontiguousarray(
            np.asarray(inputs["L0_ffn_w_gate"]).T).reshape(CH, 128, HID),
        "wuT": np.ascontiguousarray(
            np.asarray(inputs["L0_ffn_w_up"]).T).reshape(CH, 128, HID),
        "wdT": np.ascontiguousarray(
            np.asarray(inputs["L0_ffn_w_down"]).T).reshape(HC, 128, D),
    }
    return m


def kernel(**inputs):
    x = np.asarray(inputs["L0_attn_hidden_states"], np.float32)
    B, L, _ = x.shape
    shared = host_prep(inputs, L)
    nc, _ = build(L)
    in_maps = []
    for b in range(B):
        im = dict(shared)
        im["x"] = np.ascontiguousarray(x[b])
        in_maps.append(im)
    res = run_bass_kernel_spmd(nc, in_maps, core_ids=list(range(B)))
    out = np.stack([np.asarray(res.results[b]["out"]) for b in range(B)], axis=0)
    return out.astype(np.float32)


# version 04
# speedup vs baseline: 1.1469x; 1.0198x over previous
from contextlib import ExitStack

import os
import numpy as np
import ml_dtypes

import concourse.bass as bass
import concourse.mybir as mybir
from concourse.bass_utils import run_bass_kernel_spmd

F32 = mybir.dt.float32
BF16 = mybir.dt.bfloat16
I32 = mybir.dt.int32
AF = mybir.ActivationFunctionType
ALU = mybir.AluOpType
AXX = mybir.AxisListType.X

D = 1024
NH = 4
CH = D // 128
HID = 3072
HC = HID // 128
PAD = 64
NTAP = 93
QS = 1.0 / 127.0

ENGS = ("sync", "gpsimd", "vector", "scalar", "tensor")


DMA_QUEUES = {"dl": "sync", "dg": "gpsimd", "ds": "scalar"}
DMA_POOL = 8


def _freeze(fn):
    import types
    if fn.__closure__:
        fn = types.FunctionType(
            fn.__code__, fn.__globals__, fn.__name__, fn.__defaults__,
            tuple(types.CellType(c.cell_contents) for c in fn.__closure__))
    return fn


class Prog:

    def __init__(self, nc, es):
        self.nc = nc
        self.recs = {e: [] for e in ENGS}
        self.sems = {}
        self.cnt = {}
        for name in ("v", "s", "t", "g"):
            self.sems[name] = es.enter_context(nc.semaphore("sem_" + name))
            self.cnt[name] = 0
        for q in DMA_QUEUES:
            for i in range(DMA_POOL):
                key = f"{q}{i}"
                self.sems[key] = es.enter_context(nc.semaphore("sem_" + key))
                self.cnt[key] = 0
        self.qidx = {q: 0 for q in DMA_QUEUES}
        self.lastw = {}
        self.waited = {name: set() for name in ("v", "s", "t", "g")}

    def emit(self, eng, fn, inc=None):
        fn = _freeze(fn)
        if inc in DMA_QUEUES:
            q = inc
            key = f"{q}{self.qidx[q] % DMA_POOL}"
            self.qidx[q] += 1
            prev = self.cnt[key]
            if prev > 0:
                self.wait(DMA_QUEUES[q], (key, prev))
            self.cnt[key] = prev + 16
            tok = (key, self.cnt[key])
            self.recs[eng].append(("dma", fn, key))
            return tok
        tok = None
        pval = 0
        if inc is not None:
            self.cnt[inc] += 1
            pval = self.cnt[inc]
            tok = (inc, pval)
        self.recs[eng].append(("op", fn, inc, pval))
        return tok

    def wait(self, eng, tok):
        if tok is None:
            return
        sem, val = tok
        if val <= 0:
            return
        key = (eng, sem)
        if self.lastw.get(key, -1) >= val:
            return
        self.lastw[key] = val
        if sem in self.waited:
            self.waited[sem].add(val)
        self.recs[eng].append(("wait", sem, val))

    def now(self, sem):
        return (sem, self.cnt[sem])

    def wait_queue(self, eng, q):
        for i in range(DMA_POOL):
            key = f"{q}{i}"
            if self.cnt[key] > 0:
                self.wait(eng, (key, self.cnt[key]))

    def barrier(self, bop=None):
        if os.environ.get("BARRIER_OFF") == "1":
            return
        for sem in ("s", "t"):
            self.wait("vector", (sem, self.cnt[sem]))
        for q in DMA_QUEUES:
            self.wait_queue("vector", q)
        btok = self.emit("vector", self._bop, "v")
        for eng in ("sync", "gpsimd", "scalar", "tensor"):
            self.wait(eng, btok)

    def finalize_maps(self):
        keep = {}
        for sem in ("v", "s", "t", "g"):
            kept = set()
            total = self.cnt[sem]
            for w in self.waited[sem]:
                if 0 < w <= total:
                    kept.add(w)
            if total > 0:
                kept.add(total)
            kept = sorted(kept)
            m = {v: i + 1 for i, v in enumerate(kept)}
            keep[sem] = (kept, m)
        return keep


class Bank:
    def __init__(self, prog, handle):
        self.p = prog
        self.h = handle
        self.readers = []
        self.wtok = None

    def begin_write(self):
        for tok in self.readers:
            self.p.wait("tensor", tok)
        self.readers = []

    def end_write(self, tok):
        self.wtok = tok

    def read_from(self, eng):
        self.p.wait(eng, self.wtok)

    def add_reader(self, tok):
        self.readers.append(tok)


def flat(ap):
    try:
        return ap.rearrange("p a b -> p (a b)")
    except ValueError:
        return ap


class _Stop(Exception):
    pass


_LAST = {}


def build(L=2048, taps=(), until=None):
    TH = min(1024, L // 2)
    NHALF = L // TH
    NC = TH // 128
    NT = TH // 512
    FT = 256
    NFT = L // FT

    nc = bass.Bass("TRN2", target_bir_lowering=False,
                   detect_race_conditions=False)
    es = ExitStack()
    P = Prog(nc, es)
    _LAST["nc"] = nc
    _LAST["P"] = P
    _LAST["tapd"] = tapd = {}
    tapset = set(taps)

    dr = lambda n, sh, dt, k: nc.dram_tensor(n, sh, dt, kind=k)
    x_d = dr("x", [L, D], F32, "ExternalInput")
    wq_d = dr("wqT", [CH, 128, D], F32, "ExternalInput")
    wk_d = dr("wkT", [CH, 128, D], F32, "ExternalInput")
    wv_d = dr("wvT", [CH, 128, D], F32, "ExternalInput")
    wo_d = dr("woT", [CH, 128, D], F32, "ExternalInput")
    wgb_d = dr("wgbT", [CH, 128, 24], F32, "ExternalInput")
    diag_d = dr("diag_all", [CH, NTAP, 128, 128], F32, "ExternalInput")
    gbx_d = dr("gb_exp", [128, 20], F32, "ExternalInput")
    ltemp_d = dr("log_temp", [4, 1], F32, "ExternalInput")
    epsp_d = dr("eps_param", [4, 1], F32, "ExternalInput")
    nw_d = dr("nw", [128, 2], F32, "ExternalInput")
    wg_d = dr("wgT", [CH, 128, HID], I32, "ExternalInput")
    wu_d = dr("wuT", [CH, 128, HID], I32, "ExternalInput")
    wd_d = dr("wdT", [HC, 128, D], I32, "ExternalInput")
    out_d = dr("out", [L, D], F32, "ExternalOutput")
    xnew_d = dr("xnew", [L, D], F32, "Internal")
    bsc_d = dr("bscr", [4, 3], F32, "Internal")

    def mktap(name, shape, dt=F32):
        if name in tapset:
            tapd[name] = dr("tap_" + name, shape, dt, "ExternalOutput")

    mktap("ln1f", [NHALF, CH, 128, TH], BF16)
    mktap("qf", [NHALF, CH, 128, TH], BF16)
    mktap("kf", [NHALF, CH, 128, TH], BF16)
    mktap("vf", [NHALF, CH, 128, TH], BF16)
    mktap("ktok", [NHALF, NC, 128, D], BF16)
    mktap("vtok", [NHALF, NC, 128, D], BF16)
    mktap("khatf", [NHALF, CH, 128, TH], BF16)
    mktap("psc", [NHALF, NC, 128, 40], F32)
    mktap("accmix", [NHALF, NC, 128, D], BF16)
    mktap("accfin", [NHALF, NC, 128, D], BF16)
    mktap("of", [NHALF, CH, 128, TH], BF16)
    mktap("xnew", [L, D], F32)
    mktap("ln2f", [NFT, CH, 128, FT], BF16)
    mktap("hbuf", [NFT, HC, 128, FT], BF16)

    eyeb_d = nc.inline_tensor(np.eye(128, dtype=ml_dtypes.bfloat16), "eyeb_c")
    eyef_d = nc.inline_tensor(np.eye(128, dtype=np.float32), "eyef_c")
    negm_d = nc.inline_tensor(-np.tril(np.ones((128, 128), dtype=np.float32), -1),
                              "negm_c")
    ecst_np = np.zeros((128, 2), dtype=np.float32)
    ecst_np[:, 0] = 1e-5
    ecst_np[:, 1] = 1e-6
    ecst_d = nc.inline_tensor(ecst_np, "ecst_c")
    cmT_d = nc.inline_tensor(np.triu(np.ones((128, 128), dtype=np.float32), 0),
                             "cmT_c")

    _side = "left" if os.environ.get("ALL_LEFT") == "1" else "right"
    sbr = lambda n, sh, dt: es.enter_context(nc.sbuf_tensor(n, sh, dt, side=_side))
    eyeb = sbr("eyeb", [128, 128], BF16)
    eyef = sbr("eyef", [128, 128], F32)
    negm = sbr("negm", [128, 128], F32)
    cmT = sbr("cmT", [128, 128], F32)
    gbx = sbr("gbx", [128, 20], F32)
    nwt = sbr("nwt", [128, 2], F32)
    scb = sbr("scb", [128, 12], F32)
    tiny = sbr("tiny", [128, 8], F32)
    stat = sbr("stat", [128, 64], F32)
    scrap = sbr("scrap", [128, 1024], F32)
    ecst = sbr("ecst", [128, 2], F32)
    P._bop = lambda h: h.tensor_copy(scrap[0:1, 0:1], scrap[0:1, 1:2])
    P.vgap = lambda: P.wait("vector", P.now("v"))
    P.sgap = lambda: P.wait("scalar", P.now("s"))
    wgb = sbr("wgb", [128, CH, 24], BF16)
    att = ExitStack()
    sbar = lambda n, sh, dt: att.enter_context(nc.sbuf_tensor(n, sh, dt, side=_side))
    wo = sbar("wo", [128, CH, D], BF16)
    cq = sbar("cq", [128, CH, PAD], BF16)
    ck = sbar("ck", [128, CH, PAD], BF16)
    cv = sbar("cv", [128, CH, PAD], BF16)
    cvp = sbar("cvp", [128, CH, PAD], BF16)
    S32 = sbar("S32", [128, 2 * NH, 256], F32)
    Sbf = sbar("Sbf", [128, 2 * NH, 256], BF16)

    pb = [Bank(P, es.enter_context(nc.psum_tensor(f"pb{i}", [128, 512], F32)))
          for i in range(6)]
    pbT = Bank(P, es.enter_context(nc.psum_tensor("pbT", [128, 1024], BF16)))
    pbS = pbT
    pb7 = Bank(P, es.enter_context(nc.psum_tensor("pb7", [128, 512], F32)))

    lds = []
    _consts = ((eyeb, eyeb_d), (eyef, eyef_d), (negm, negm_d),
               (cmT, cmT_d), (gbx, gbx_d), (ecst, ecst_d))
    if os.environ.get("NO_CONSTS") == "1":
        _consts = ((eyeb, eyeb_d),)
    for dst, src in _consts:
        lds.append(P.emit("sync", lambda h, d=dst, s=src: h.dma_start(
            out=d[:, :], in_=s.ap()[:, :]), "dl"))
    if os.environ.get("NO_CONSTS") != "1":
        lds.append(P.emit("sync", lambda h: h.dma_start(
            out=nwt[:, :], in_=nw_d.ap()[:, :]), "dl"))
    if os.environ.get("NO_LTEMP") != "1":
        lds.append(P.emit("sync", lambda h: h.dma_start(
            out=tiny[0:4, 0:1], in_=ltemp_d.ap()[:, :]), "dl"))
        lds.append(P.emit("sync", lambda h: h.dma_start(
            out=tiny[0:4, 1:2], in_=epsp_d.ap()[:, :]), "dl"))
    if os.environ.get("NO_WOWGB") != "1":
        t_wo = P.emit("gpsimd", lambda h: h.dma_start(
            out=wo[:, :, :], in_=wo_d.ap().transpose([1, 0, 2])), "dg")
        t_wgb = P.emit("gpsimd", lambda h: h.dma_start(
            out=wgb[:, :, :], in_=wgb_d.ap().transpose([1, 0, 2])), "dg")
    else:
        t_wo = t_wgb = None

    for tok in lds:
        P.wait("scalar", tok)
        P.wait("vector", tok)
    P.emit("vector", lambda h: h.memset(scrap[:, :], 0.0), "v")
    if os.environ.get("NO_TINY") != "1":
        P.emit("scalar", lambda h: h.activation(tiny[0:4, 2:3], tiny[0:4, 0:1], AF.Exp), "s")
        P.wait("vector", P.now("s"))
        P.emit("vector", lambda h: h.tensor_scalar(
            tiny[0:4, 5:6], tiny[0:4, 1:2], 0.0, 0.2, ALU.max, ALU.min), "v")
        P.emit("vector", lambda h: h.reciprocal(tiny[0:4, 4:5], tiny[0:4, 2:3]), "v")
        P.emit("vector", lambda h: h.tensor_scalar(
            tiny[0:4, 6:7], tiny[0:4, 5:6], -5.0, 1.0, ALU.mult, ALU.add), "v")
    P.wait("gpsimd", P.now("v"))
    if os.environ.get("NO_SCB") != "1":
        P.emit("gpsimd", lambda h: h.dma_start(out=bsc_d.ap()[:, :], in_=tiny[0:4, 4:7]), "dg")
        P.wait_queue("gpsimd", "dg")
        t_scb = P.emit("gpsimd", lambda h: h.dma_start(
            out=scb[:, 0:12],
            in_=bsc_d.ap().rearrange("h c -> (h c)").unsqueeze(0).broadcast_to([128, 12])),
            "dg")
        P.wait("vector", t_scb)
        P.wait("scalar", t_scb)
    P.wait("vector", t_wo)
    if os.environ.get("NO_FOLD") != "1":
        for kc in range(CH):
            P.emit("vector", lambda h, kc=kc: h.tensor_scalar(
                wo[:, kc, :], wo[:, kc, :], nwt[:, kc % 2:kc % 2 + 1], None, ALU.mult), "v")
    t_wofold = P.now("v")
    if os.environ.get("NO_MS") != "1":
        P.emit("vector", lambda h: h.memset(flat(S32[:, :, :]), 0.0), "v")
        P.emit("vector", lambda h: h.memset(flat(Sbf[:, :, :]), 0.0), "v")
    t_sinit = P.now("v")
    P.wait("tensor", t_wgb)

    def emit_ln_chunk(t_x, xsl, lnt_dst, t_lnt_free):
        P.wait("vector", t_x)
        P.wait("scalar", t_x)
        P.emit("vector", lambda h: h.tensor_reduce(
            stat[:, 0:1], xsl, AXX, ALU.add), "v")
        P.wait("scalar", t_lnt_free)
        t_sq = P.emit("scalar", lambda h: h.activation(
            lnt_dst, xsl, AF.Square, accum_out=stat[:, 1:2]), "s")
        P.wait("vector", t_sq)
        P.vgap()
        P.emit("vector", lambda h: h.tensor_scalar(
            stat[:, 0:1], stat[:, 0:1], 1.0 / D, None, ALU.mult), "v")
        P.emit("vector", lambda h: h.tensor_scalar(
            stat[:, 1:2], stat[:, 1:2], 1.0 / D, None, ALU.mult), "v")
        P.emit("vector", lambda h: h.tensor_tensor(
            stat[:, 2:3], stat[:, 0:1], stat[:, 0:1], ALU.mult), "v")
        P.vgap()
        P.emit("vector", lambda h: h.tensor_tensor(
            stat[:, 2:3], stat[:, 1:2], stat[:, 2:3], ALU.subtract), "v")
        P.wait("scalar", P.now("v"))
        t_rs = P.emit("scalar", lambda h: h.activation(
            stat[:, 2:3], stat[:, 2:3], AF.Sqrt, bias=ecst[:, 0:1]), "s")
        P.wait("vector", t_rs)
        P.emit("vector", lambda h: h.reciprocal(stat[:, 3:4], stat[:, 2:3]), "v")
        P.vgap()
        P.wait("vector", t_lnt_free)
        return P.emit("vector", lambda h: h.tensor_scalar(
            lnt_dst, xsl, stat[:, 0:1], stat[:, 3:4], ALU.subtract, ALU.mult), "v")

    def ckpt(name):
        if until == name:
            raise _Stop

    ckpt("prologue")
    t_scan_prev = t_sinit
    for half in range(NHALF):
        hs = ExitStack()
        sbl = lambda n, sh, dt: hs.enter_context(
            nc.sbuf_tensor(f"{n}_{half}", sh, dt, side="left"))
        ACC = sbl("ACC", [128, NC, D], BF16)
        vtok = sbl("vtok", [128, NC, D], BF16)
        ktok = sbl("ktok", [128, NC, D], BF16)
        khatf = sbl("khatf", [128, CH, TH], BF16)
        psc = sbl("psc", [128, NC, 48], F32)
        qx = ExitStack()
        sbq = lambda n, sh, dt: qx.enter_context(
            nc.sbuf_tensor(f"{n}_{half}", sh, dt, side="left"))
        qf = sbq("qf", [128, CH, PAD + TH], BF16)
        kf = sbq("kf", [128, CH, PAD + TH], BF16)
        vf = sbq("vf", [128, CH, PAD + TH], BF16)

        ax = ExitStack()
        sba = lambda n, sh, dt: ax.enter_context(
            nc.sbuf_tensor(f"{n}_{half}", sh, dt, side="left"))
        xst = sba("xst", [128, 2, D], F32)
        lnt = sba("lnt", [128, D], BF16)
        ln1f = sba("ln1f", [128, CH, TH], BF16)
        war = sba("war", [128, CH, D], BF16)

        t_lnt_free = None
        for tc in range(0 if os.environ.get("LN_OFF") == "1" else NC):
            t0 = half * TH + tc * 128
            t_x = P.emit("sync", lambda h, tc=tc, t0=t0: h.dma_start(
                out=xst[:, tc % 2, :], in_=x_d.ap()[t0:t0 + 128, :]), "dl")
            t_ln = emit_ln_chunk(t_x, xst[:, tc % 2, :], lnt[:, :], t_lnt_free)
            P.wait("tensor", t_ln)
            pbT.begin_write()
            for cc in range(CH):
                tk = P.emit("tensor", lambda h, cc=cc: h.transpose(
                    pbT.h[:, cc * 128:(cc + 1) * 128],
                    lnt[:, cc * 128:(cc + 1) * 128], eyeb[:, :]),
                    "t" if cc == CH - 1 else None)
            pbT.end_write(tk)
            t_lnt_free = tk
            pbT.read_from("scalar")
            tev = P.emit("scalar", lambda h, tc=tc: h.activation(
                flat(ln1f[:, :, tc * 128:(tc + 1) * 128]), pbT.h[:, :], AF.Copy), "s")
            pbT.add_reader(tev)
            if os.environ.get("NO_GATE") == "1":
                continue
            bg = pb[4]
            bg.begin_write()
            P.wait("tensor", tev)
            for kc in range(CH):
                tk = P.emit("tensor", lambda h, tc=tc, kc=kc: h.matmul(
                    bg.h[:, 0:24], ln1f[:, kc, tc * 128:(tc + 1) * 128],
                    wgb[:, kc, :], start=(kc == 0), stop=(kc == CH - 1)),
                    "t" if kc == CH - 1 else None)
            bg.end_write(tk)
            bg.read_from("vector")
            P.wait("vector", P.now("s"))
            t_gl = P.emit("vector", lambda h: h.tensor_tensor(
                stat[:, 16:36], bg.h[:, 0:20], gbx[:, :], ALU.add), "v")
            bg.read_from("scalar")
            t_bt = P.emit("scalar", lambda h, tc=tc: h.activation(
                psc[:, tc, 20:24], bg.h[:, 20:24], AF.Sigmoid), "s")
            bg.add_reader(t_gl)
            bg.add_reader(t_bt)
            P.wait("scalar", t_gl)
            NPH = 0 if os.environ.get("NO_PROBS") == "1" else NH
            if NPH:
                P.vgap()
            for hh in range(NPH):
                P.emit("vector", lambda h, hh=hh: h.reduce_max(
                    stat[:, 36 + hh:37 + hh], stat[:, 16 + 5 * hh:21 + 5 * hh], AXX), "v")
            for hh in range(NPH):
                P.emit("vector", lambda h, hh=hh: h.tensor_scalar(
                    stat[:, 40 + hh:41 + hh], stat[:, 36 + hh:37 + hh],
                    scb[:, 3 * hh:3 * hh + 1], -1.0, ALU.mult, ALU.mult), "v")
            P.wait("scalar", P.now("v"))
            t_e = None
            for hh in range(NPH):
                t_e = P.emit("scalar", lambda h, hh=hh: h.activation(
                    stat[:, 16 + 5 * hh:21 + 5 * hh], stat[:, 16 + 5 * hh:21 + 5 * hh],
                    AF.Exp, bias=stat[:, 40 + hh:41 + hh],
                    scale=scb[:, 3 * hh:3 * hh + 1]), "s")
            P.wait("vector", t_e if NPH else None)
            for hh in range(NPH):
                P.emit("vector", lambda h, hh=hh: h.tensor_reduce(
                    stat[:, 36 + hh:37 + hh], stat[:, 16 + 5 * hh:21 + 5 * hh], AXX,
                    ALU.add), "v")
            for hh in range(NPH):
                P.emit("vector", lambda h, hh=hh: h.reciprocal(
                    stat[:, 40 + hh:41 + hh], stat[:, 36 + hh:37 + hh]), "v")
            for hh in range(NPH):
                P.emit("vector", lambda h, hh=hh: h.tensor_tensor(
                    stat[:, 40 + hh:41 + hh], stat[:, 40 + hh:41 + hh],
                    scb[:, 3 * hh + 2:3 * hh + 3], ALU.mult), "v")
            for hh in range(NPH):
                P.emit("vector", lambda h, tc=tc, hh=hh: h.tensor_scalar(
                    psc[:, tc, 5 * hh:5 * hh + 5], stat[:, 16 + 5 * hh:21 + 5 * hh],
                    stat[:, 40 + hh:41 + hh], scb[:, 3 * hh + 1:3 * hh + 2],
                    ALU.mult, ALU.add), "v")
            if tc + 2 < NC:
                P.wait("sync", P.now("v"))
                P.wait("sync", P.now("s"))
        t_ln_all = P.now("s")
        ckpt("ln" + str(half))

        if "ln1f" in tapd:
            P.wait("scalar", t_ln_all)
            for cc in range(CH):
                P.emit("scalar", lambda h, cc=cc: h.dma_start(
                    out=tapd["ln1f"].ap()[half, cc], in_=ln1f[:, cc, :]), "ds")

        t_war_free = None
        for wi, (wsrc, dst) in enumerate(((wq_d, qf), (wk_d, kf), (wv_d, vf))):
            P.wait("gpsimd", t_war_free)
            t_w = P.emit("gpsimd", lambda h, wsrc=wsrc: h.dma_start(
                out=war[:, :, :], in_=wsrc.ap().transpose([1, 0, 2])), "dg")
            P.wait("tensor", t_w)
            P.wait("tensor", t_ln_all)
            ckpt(f"pwdma{half}_{wi}")
            for tt in range(int(os.environ.get("TT_LIMIT", NT))):
                for mc in range(int(os.environ.get("MC_LIMIT", CH))):
                    bk = pb[int(os.environ.get('PB_BASE', '0')) + mc % 2] if os.environ.get('PB_ONE') != '1' else pb[0]
                    bk.begin_write()
                    for kc in range(CH):
                        mco = 0 if os.environ.get("LHS0") == "1" else mc
                        tk = P.emit("tensor", lambda h, mco=mco, kc=kc, tt=tt: h.matmul(
                            bk.h[:, 0:512], war[:, kc, mco * 128:(mco + 1) * 128],
                            ln1f[:, kc, tt * 512:tt * 512 + 512],
                            start=(kc == 0), stop=(kc == CH - 1)),
                            "t" if kc == CH - 1 else None)
                    bk.end_write(tk)
                    if os.environ.get("EVAC_V") == "1":
                        bk.read_from("vector")
                        tev = P.emit("vector", lambda h, mc=mc, tt=tt, dst=dst:
                                     h.tensor_copy(
                            dst[:, mc, PAD + tt * 512:PAD + tt * 512 + 512],
                            bk.h[:, 0:512]), "v")
                    else:
                        bk.read_from("scalar")
                        tev = P.emit("scalar", lambda h, mc=mc, tt=tt, dst=dst: h.activation(
                            dst[:, mc, PAD + tt * 512:PAD + tt * 512 + 512],
                            bk.h[:, 0:512], AF.Copy), "s")
                    bk.add_reader(tev)
                    ckpt(f"pwmm{half}_{wi}_{mc}")
            t_war_free = P.now("t")
            ckpt(f"pw{half}_{wi}")
        t_proj = P.now("s")
        ckpt("proj" + str(half))

        if half == 0 and os.environ.get("NO_PADMS") != "1":
            P.wait("vector", t_sinit)
            for buf in (qf, kf, vf):
                P.emit("vector", lambda h, buf=buf: h.memset(
                    flat(buf[:, :, 0:PAD]), 0.0), "v")
        else:
            P.wait("vector", t_proj)
            for buf, car in ((qf, cq), (kf, ck), (vf, cv)):
                P.emit("vector", lambda h, buf=buf, car=car: h.tensor_copy(
                    flat(buf[:, :, 0:PAD]), flat(car[:, :, :])), "v")
        t_pad = P.now("v")
        P.barrier()
        ax.close()

        bx = ExitStack()
        da = bx.enter_context(nc.sbuf_tensor(f"da_{half}", [128, 117, 128], BF16,
                                             side="left"))
        fsb = bx.enter_context(nc.sbuf_tensor(f"fs_{half}", [128, TH], BF16,
                                              side="left"))
        P.wait("vector", t_proj)
        if half + 1 < NHALF:
            for buf, car in ((qf, cq), (kf, ck), (vf, cv)):
                P.emit("vector", lambda h, buf=buf, car=car: h.tensor_copy(
                    flat(car[:, :, :]), flat(buf[:, :, TH:TH + PAD])), "v")
        t_stash = P.now("v")

        t_mmB = {}
        t_daB = {}

        def load_shorts(cc):
            sb = 12 * (cc % 2)
            P.wait("gpsimd", t_mmB.get(cc - 2))
            t_daB[cc] = P.emit("gpsimd", lambda h, cc=cc, sb=sb: h.dma_start(
                out=da[:, sb:sb + 12, :],
                in_=diag_d.ap()[cc, 0:12, :, :].transpose([1, 0, 2])), "dg")

        load_shorts(0)
        load_shorts(1)
        for cc in range(CH):
            sb = 12 * (cc % 2)
            P.wait("tensor", t_daB[cc])
            P.wait("tensor", t_stash)
            P.wait("tensor", t_pad)
            for ti, (buf, f0) in enumerate(((qf, sb + 0), (kf, sb + 4), (vf, sb + 8))):
                b0, b1 = pb[2 * (ti % 2)], pb[2 * (ti % 2) + 1]
                b0.begin_write()
                b1.begin_write()
                for j in range(4):
                    off = PAD - 3 + j
                    for tt in range(NT):
                        bk = b0 if tt % 2 == 0 else b1
                        tk = P.emit("tensor", lambda h, cc=cc, j=j, f0=f0, tt=tt,
                                    off=off, buf=buf, bk=bk: h.matmul(
                            bk.h[:, 0:512], da[:, f0 + j, :],
                            buf[:, cc, off + tt * 512:off + tt * 512 + 512],
                            start=(j == 0), stop=(j == 3)),
                            "t" if j == 3 else None)
                        (b0 if tt % 2 == 0 else b1).end_write(tk)
                for tt in range(NT):
                    bk = b0 if tt % 2 == 0 else b1
                    bk.read_from("scalar")
                    tev = P.emit("scalar", lambda h, cc=cc, tt=tt, buf=buf, bk=bk:
                                 h.activation(
                        buf[:, cc, PAD + tt * 512:PAD + tt * 512 + 512],
                        bk.h[:, 0:512], AF.Silu), "s")
                    bk.add_reader(tev)
            t_mmB[cc] = P.now("t")
            if cc + 2 < CH:
                load_shorts(cc + 2)
        t_conv = P.now("s")
        ckpt("conv" + str(half))

        P.wait("vector", t_conv)
        if half > 0:
            P.emit("vector", lambda h: h.tensor_copy(
                flat(vf[:, :, 0:PAD]), flat(cvp[:, :, :])), "v")
        if half + 1 < NHALF:
            P.emit("vector", lambda h: h.tensor_copy(
                flat(cvp[:, :, :]), flat(vf[:, :, TH:TH + PAD])), "v")
        t_vpad = P.now("v")

        for nm, buf in (("qf", qf), ("kf", kf), ("vf", vf)):
            if nm in tapd:
                P.wait("scalar", t_conv)
                P.wait("scalar", t_vpad)
                for cc in range(CH):
                    P.emit("scalar", lambda h, nm=nm, cc=cc, buf=buf: h.dma_start(
                        out=tapd[nm].ap()[half, cc], in_=buf[:, cc, PAD:PAD + TH]), "ds")

        for tname in ("k", "q", "v"):
            src = {"k": kf, "q": qf, "v": vf}[tname]
            t_need = t_vpad if tname == "v" else t_conv
            for tc in range(NC):
                pbT.begin_write()
                P.wait("tensor", t_need)
                for cc in range(CH):
                    tk = P.emit("tensor", lambda h, src=src, cc=cc, tc=tc: h.transpose(
                        pbT.h[:, cc * 128:(cc + 1) * 128],
                        src[:, cc, PAD + tc * 128:PAD + tc * 128 + 128],
                        eyeb[:, :]), "t" if cc == CH - 1 else None)
                pbT.end_write(tk)
                if tname in ("k", "q"):
                    dc = 24 if tname == "k" else 28
                    pbT.read_from("scalar")
                    for hh in range(NH):
                        tr = P.emit("scalar", lambda h, hh=hh, tc=tc, dc=dc:
                                    h.activation(
                            scrap[:, 256 * hh:256 * hh + 256],
                            pbT.h[:, hh * 256:hh * 256 + 256], AF.Square,
                            accum_out=psc[:, tc, dc + hh:dc + hh + 1]), "s")
                        pbT.add_reader(tr)
                    P.sgap()
                    t_s = P.emit("scalar", lambda h, tc=tc, dc=dc: h.activation(
                        psc[:, tc, dc:dc + 4], psc[:, tc, dc:dc + 4], AF.Sqrt,
                        bias=ecst[:, 1:2]), "s")
                    P.wait("vector", t_s)
                    P.emit("vector", lambda h, tc=tc, dc=dc: h.reciprocal(
                        psc[:, tc, dc:dc + 4], psc[:, tc, dc:dc + 4]), "v")
                    if tname == "k":
                        pbT.read_from("vector")
                        P.vgap()
                        for hh in range(NH):
                            tr = P.emit("vector", lambda h, hh=hh, tc=tc: h.tensor_scalar(
                                ktok[:, tc, hh * 256:hh * 256 + 256],
                                pbT.h[:, hh * 256:hh * 256 + 256],
                                psc[:, tc, 24 + hh:25 + hh], None, ALU.mult), "v")
                            pbT.add_reader(tr)
                else:
                    pbT.read_from("scalar")
                    tr = P.emit("scalar", lambda h, tc=tc: h.activation(
                        vtok[:, tc, :], pbT.h[:, :], AF.Copy), "s")
                    pbT.add_reader(tr)
        P.wait("vector", P.now("s"))
        for tc in range(NC):
            P.emit("vector", lambda h, tc=tc: h.tensor_tensor(
                psc[:, tc, 32:36], psc[:, tc, 24:28], psc[:, tc, 20:24], ALU.mult), "v")
            for hh in range(NH):
                P.emit("vector", lambda h, tc=tc, hh=hh: h.tensor_tensor(
                    psc[:, tc, 36 + hh:37 + hh], psc[:, tc, 28 + hh:29 + hh],
                    psc[:, tc, 5 * hh + 3:5 * hh + 4], ALU.mult), "v")
        t_scl = P.now("v")
        P.wait("tensor", t_scl)
        for tc in range(NC):
            for g in range(2):
                pbS.begin_write()
                for cc in range(4 * g, 4 * g + 4):
                    tk = P.emit("tensor", lambda h, cc=cc, tc=tc: h.transpose(
                        pbS.h[:, (cc % 4) * 128:(cc % 4) * 128 + 128],
                        ktok[:, tc, cc * 128:(cc + 1) * 128], eyeb[:, :]),
                        "t" if cc % 4 == 3 else None)
                pbS.end_write(tk)
                pbS.read_from("scalar")
                tr = P.emit("scalar", lambda h, g=g, tc=tc: h.activation(
                    flat(khatf[:, 4 * g:4 * g + 4, tc * 128:tc * 128 + 128]),
                    pbS.h[:, 0:512], AF.Copy), "s")
                pbS.add_reader(tr)
        t_phC = P.now("s")
        ckpt("phC" + str(half))

        for nm, buf in (("ktok", ktok), ("vtok", vtok)):
            if nm in tapd:
                P.wait("scalar", t_phC)
                for tc in range(NC):
                    P.emit("scalar", lambda h, nm=nm, tc=tc, buf=buf: h.dma_start(
                        out=tapd[nm].ap()[half, tc], in_=buf[:, tc, :]), "ds")
        if "khatf" in tapd:
            P.wait("scalar", t_phC)
            for cc in range(CH):
                P.emit("scalar", lambda h, cc=cc: h.dma_start(
                    out=tapd["khatf"].ap()[half, cc], in_=khatf[:, cc, :]), "ds")

        P.wait("vector", t_phC)
        for tc in range(NC):
            for hh in range(NH):
                P.emit("vector", lambda h, tc=tc, hh=hh: h.tensor_scalar(
                    ACC[:, tc, hh * 256:hh * 256 + 256],
                    vtok[:, tc, hh * 256:hh * 256 + 256],
                    psc[:, tc, 5 * hh + 4:5 * hh + 5], None, ALU.mult), "v")
        t_acc0 = P.now("v")

        t_smm = {}
        t_lmm = {}
        t_dasm = {}
        t_dal = {}

        def load_sm(cc):
            sb = 18 * (cc % 2)
            P.wait("gpsimd", t_smm.get(cc - 2))
            t_dasm[cc] = P.emit("gpsimd", lambda h, cc=cc, sb=sb: h.dma_start(
                out=da[:, sb:sb + 18, :],
                in_=diag_d.ap()[cc, 12:30, :, :].transpose([1, 0, 2])), "dg")

        def load_l(cc):
            P.wait("gpsimd", t_lmm.get(cc - 1))
            t_dal[cc] = P.emit("gpsimd", lambda h, cc=cc: h.dma_start(
                out=da[:, 36:99, :],
                in_=diag_d.ap()[cc, 30:93, :, :].transpose([1, 0, 2])), "dg")

        P.wait("gpsimd", P.now("t"))
        load_sm(0)
        load_l(0)
        load_sm(1)
        for cc in range(CH):
            smb = 18 * (cc % 2)
            hh = cc // 2
            fir_specs = ((smb + 0, 3, 0), (smb + 3, 15, 1), (36, 63, 2))
            for si, (lo, ntap, pcol) in enumerate(fir_specs):
                b0, b1 = pb[0], pb[1]
                b0.begin_write()
                b1.begin_write()
                P.wait("tensor", t_dal[cc] if si == 2 else t_dasm[cc])
                P.wait("tensor", t_vpad)
                for j in range(ntap):
                    off = PAD - (ntap - 1) + j
                    for tt in range(NT):
                        bk = b0 if tt % 2 == 0 else b1
                        tk = P.emit("tensor", lambda h, cc=cc, lo=lo, j=j, off=off,
                                    tt=tt, bk=bk, ntap=ntap: h.matmul(
                            bk.h[:, 0:512], da[:, lo + j, :],
                            vf[:, cc, off + tt * 512:off + tt * 512 + 512],
                            start=(j == 0), stop=(j == ntap - 1)),
                            "t" if j == ntap - 1 else None)
                        (b0 if tt % 2 == 0 else b1).end_write(tk)
                if si == 1:
                    t_smm[cc] = P.now("t")
                    if cc + 2 < CH:
                        load_sm(cc + 2)
                if si == 2:
                    t_lmm[cc] = P.now("t")
                    if cc + 1 < CH:
                        load_l(cc + 1)
                for tt in range(NT):
                    bk = b0 if tt % 2 == 0 else b1
                    bk.read_from("scalar")
                    tev = P.emit("scalar", lambda h, tt=tt, bk=bk: h.activation(
                        fsb[:, tt * 512:tt * 512 + 512], bk.h[:, 0:512], AF.Copy), "s")
                    bk.add_reader(tev)
                t_fs = P.now("s")
                P.wait("tensor", t_fs)
                P.wait("tensor", t_acc0)
                for tc in range(NC):
                    pbS.begin_write()
                    tk = P.emit("tensor", lambda h, tc=tc: h.transpose(
                        pbS.h[:, (tc % 4) * 128:(tc % 4) * 128 + 128],
                        fsb[:, tc * 128:tc * 128 + 128], eyeb[:, :]), "t")
                    pbS.end_write(tk)
                    pbS.read_from("vector")
                    tr = P.emit("vector", lambda h, tc=tc, cc=cc, hh=hh, pcol=pcol:
                                h.scalar_tensor_tensor(
                        ACC[:, tc, cc * 128:(cc + 1) * 128],
                        pbS.h[:, (tc % 4) * 128:(tc % 4) * 128 + 128],
                        psc[:, tc, 5 * hh + pcol:5 * hh + pcol + 1],
                        ACC[:, tc, cc * 128:(cc + 1) * 128], ALU.mult, ALU.add), "v")
                    pbS.add_reader(tr)
        t_fir = P.now("v")
        P.barrier()
        bx.close()
        ckpt("fir" + str(half))

        if "accmix" in tapd:
            P.wait("scalar", t_fir)
            for tc in range(NC):
                P.emit("scalar", lambda h, tc=tc: h.dma_start(
                    out=tapd["accmix"].ap()[half, tc], in_=ACC[:, tc, :]), "ds")
        if "psc" in tapd:
            P.wait("scalar", t_fir)
            P.wait("scalar", t_scl)
            for tc in range(NC):
                P.emit("scalar", lambda h, tc=tc: h.dma_start(
                    out=tapd["psc"].ap()[half, tc], in_=psc[:, tc, 0:40]), "ds")

        ex = ExitStack()
        sbe = lambda n, sh, dt: ex.enter_context(
            nc.sbuf_tensor(f"{n}_{half}", sh, dt, side="left"))
        dA = sbe("dA", [128, NH, 128], BF16)
        dC = sbe("dC", [128, NH, 128], BF16)
        dB = sbe("dB", [128, NH, 128], BF16)
        dQa = sbe("dQa", [128, NH, 128], BF16)
        dQb = sbe("dQb", [128, NH, 128], BF16)
        du = sbe("du", [128, NH, 256], BF16)
        dua = sbe("dua", [128, NH, 256], BF16)
        datn = sbe("datn", [128, 2, NH, 128], BF16)
        dwf = sbe("dwf", [128, 2 * NH, 128], BF16)
        dvh = sbe("dvh", [128, NH, 256], BF16)
        dkb = sbe("dkb", [128, NH, 256], BF16)

        NLEV = 6
        CS = {"t_wf": None, "t_uwmm": None, "t_scan_prev": t_scan_prev}

        def pre_scales(tn):
            P.wait("vector", t_fir if tn == 0 else P.now("v"))
            if CS["t_uwmm"] is not None:
                P.wait("vector", CS["t_uwmm"])
            for hh in range(NH):
                P.emit("vector", lambda h, tn=tn, hh=hh: h.tensor_scalar(
                    dvh[:, hh, :], vtok[:, tn, hh * 256:hh * 256 + 256],
                    psc[:, tn, 20 + hh:21 + hh], None, ALU.mult), "v")
                P.emit("vector", lambda h, tn=tn, hh=hh: h.tensor_scalar(
                    dkb[:, hh, :], ktok[:, tn, hh * 256:hh * 256 + 256],
                    psc[:, tn, 20 + hh:21 + hh], None, ALU.mult), "v")
            CS["t_sc"] = P.now("v")

        def pre_Gmm(tn):
            tr0 = PAD + tn * 128
            pb7.begin_write()
            P.wait("tensor", CS["t_sc"])
            for hh in range(NH):
                for kc in range(2):
                    tk = P.emit("tensor", lambda h, hh=hh, kc=kc, tn=tn, tr0=tr0:
                                h.matmul(
                        pb7.h[:, hh * 128:hh * 128 + 128],
                        kf[:, 2 * hh + kc, tr0:tr0 + 128],
                        khatf[:, 2 * hh + kc, tn * 128:tn * 128 + 128],
                        start=(kc == 0), stop=(kc == 1)),
                        "t" if hh == NH - 1 and kc == 1 else None)
            pb7.end_write(tk)

        def pre_Aevac(tn):
            pb7.read_from("vector")
            for hh in range(NH):
                tr = P.emit("vector", lambda h, hh=hh, tn=tn: h.scalar_tensor_tensor(
                    dA[:, hh, :], pb7.h[:, hh * 128:hh * 128 + 128],
                    psc[:, tn, 32 + hh:33 + hh], negm[:, :], ALU.mult, ALU.mult), "v")
            pb7.add_reader(tr)
            CS["t_A"] = P.now("v")

        def pre_attnT(tn):
            tr0 = PAD + tn * 128
            pp = tn % 2
            pb[2].begin_write()
            P.wait("tensor", CS["t_A"])
            for hh in range(NH):
                for kc in range(2):
                    tk = P.emit("tensor", lambda h, hh=hh, kc=kc, tn=tn, tr0=tr0:
                                h.matmul(
                        pb[2].h[:, hh * 128:hh * 128 + 128],
                        khatf[:, 2 * hh + kc, tn * 128:tn * 128 + 128],
                        qf[:, 2 * hh + kc, tr0:tr0 + 128],
                        start=(kc == 0), stop=(kc == 1)),
                        "t" if hh == NH - 1 and kc == 1 else None)
            pb[2].end_write(tk)
            pb[2].read_from("vector")
            t_atn = P.emit("vector", lambda h, pp=pp: h.tensor_tensor(
                datn[:, pp, :, :],
                pb[2].h[:, 0:512].rearrange("p (a b) -> p a b", a=NH),
                cmT[:, :].unsqueeze(1).broadcast_to([128, NH, 128]), ALU.mult), "v")
            pb[2].add_reader(t_atn)

        grpbank = (pb7, pb[3])
        trbank = (pbT, pb[2])
        trview = (pbT.h[:, 0:256], pb[2].h[:, :].bitcast(BF16)[:, 0:256])

        def pre_C0(tn):
            CS["t_c"] = [None, None]
            CS["t_b"] = [None, None]
            CS["t_q"] = [None, None]
            CS["Qcur"], CS["Qnxt"] = dQa, dQb
            for g in range(2):
                bk = trbank[g]
                bk.begin_write()
                P.wait("tensor", CS["t_A"])
                for j in range(2):
                    hh = 2 * g + j
                    tk = P.emit("tensor", lambda h, hh=hh, j=j, g=g: h.transpose(
                        trview[g][:, j * 128:j * 128 + 128], dA[:, hh, :],
                        eyeb[:, :]), "t" if j == 1 else None)
                bk.end_write(tk)
            for g in range(2):
                bk = trbank[g]
                bk.read_from("scalar")
                CS["t_c"][g] = P.emit("scalar", lambda h, g=g: h.activation(
                    flat(dC[:, 2 * g:2 * g + 2, :]), trview[g][:, 0:256],
                    AF.Copy), "s")
                bk.add_reader(CS["t_c"][g])
            for g in range(2):
                P.wait("vector", CS["t_c"][g])
                CS["t_q"][g] = P.emit("vector", lambda h, g=g, Q=CS["Qcur"]:
                                      h.tensor_tensor(
                    Q[:, 2 * g:2 * g + 2, :], dC[:, 2 * g:2 * g + 2, :],
                    eyeb[:, :].unsqueeze(1).broadcast_to([128, 2, 128]),
                    ALU.add), "v")

        def ch_CMM(lev):
            for g in range(2):
                bk = grpbank[g]
                bk.begin_write()
                P.wait("tensor", CS["t_b"][g] if lev > 0 else CS["t_A"])
                P.wait("tensor", CS["t_c"][g])
                for j in range(2):
                    hh = 2 * g + j
                    src = dB if lev > 0 else dA
                    tk = P.emit("tensor", lambda h, hh=hh, j=j, g=g, src=src:
                                h.matmul(
                        grpbank[g].h[:, j * 128:j * 128 + 128], src[:, hh, :],
                        dC[:, hh, :], start=True, stop=True),
                        "t" if j == 1 else None)
                bk.end_write(tk)
            for g in range(2):
                bk = grpbank[g]
                bk.read_from("scalar")
                CS["t_c"][g] = P.emit("scalar", lambda h, g=g: h.activation(
                    flat(dC[:, 2 * g:2 * g + 2, :]),
                    grpbank[g].h[:, 0:256], AF.Copy), "s")
                bk.add_reader(CS["t_c"][g])

        def ch_TR(lev):
            for g in range(2):
                bk = trbank[g]
                bk.begin_write()
                P.wait("tensor", CS["t_c"][g])
                for j in range(2):
                    hh = 2 * g + j
                    tk = P.emit("tensor", lambda h, hh=hh, j=j, g=g: h.transpose(
                        trview[g][:, j * 128:j * 128 + 128], dC[:, hh, :],
                        eyeb[:, :]), "t" if j == 1 else None)
                bk.end_write(tk)
            for g in range(2):
                bk = trbank[g]
                bk.read_from("scalar")
                CS["t_b"][g] = P.emit("scalar", lambda h, g=g: h.activation(
                    flat(dB[:, 2 * g:2 * g + 2, :]), trview[g][:, 0:256],
                    AF.Copy), "s")
                bk.add_reader(CS["t_b"][g])

        def ch_QMM(lev):
            Qcur = CS["Qcur"]
            for g in range(2):
                bk = grpbank[g]
                bk.begin_write()
                P.wait("tensor", CS["t_b"][g])
                P.wait("tensor", CS["t_q"][g])
                for j in range(2):
                    hh = 2 * g + j
                    P.emit("tensor", lambda h, hh=hh, j=j, g=g, Q=Qcur: h.matmul(
                        grpbank[g].h[:, j * 128:j * 128 + 128], dB[:, hh, :],
                        Q[:, hh, :], start=True, stop=False), None)
                    tk = P.emit("tensor", lambda h, hh=hh, j=j, g=g, Q=Qcur:
                                h.matmul(
                        grpbank[g].h[:, j * 128:j * 128 + 128], eyeb[:, :],
                        Q[:, hh, :], start=False, stop=True),
                        "t" if j == 1 else None)
                bk.end_write(tk)
            for g in range(2):
                bk = grpbank[g]
                bk.read_from("scalar")
                CS["t_q"][g] = P.emit("scalar", lambda h, g=g, Q=CS["Qnxt"]:
                                      h.activation(
                    flat(Q[:, 2 * g:2 * g + 2, :]),
                    grpbank[g].h[:, 0:256], AF.Copy), "s")
                bk.add_reader(CS["t_q"][g])
            CS["Qcur"], CS["Qnxt"] = CS["Qnxt"], CS["Qcur"]

        def pre_uw(tn):
            dTT = CS["Qcur"]
            P.wait("tensor", CS["t_q"][0])
            P.wait("tensor", CS["t_q"][1])
            pb7.begin_write()
            pb[3].begin_write()
            for i in range(8):
                hh, dv = divmod(i, 2)
                bk = pb7 if i < 4 else pb[3]
                tk = P.emit("tensor", lambda h, hh=hh, dv=dv, bk=bk, i=i, dTT=dTT:
                            h.matmul(
                    bk.h[:, (i % 4) * 128:(i % 4) * 128 + 128],
                    dTT[:, hh, :], dvh[:, hh, dv * 128:dv * 128 + 128],
                    start=True, stop=True), "t" if i in (3, 7) else None)
                if i == 3:
                    pb7.end_write(tk)
            pb[3].end_write(tk)
            pb7.read_from("scalar")
            tu1 = P.emit("scalar", lambda h: h.activation(
                flat(du[:, 0:2, :]), pb7.h[:, 0:512], AF.Copy), "s")
            pb7.add_reader(tu1)
            pb[3].read_from("scalar")
            tu2 = P.emit("scalar", lambda h: h.activation(
                flat(du[:, 2:4, :]), pb[3].h[:, 0:512], AF.Copy), "s")
            pb[3].add_reader(tu2)

            pb[2].begin_write()
            pb7.begin_write()
            for i in range(8):
                hh, kc = divmod(i, 2)
                bk = pb[2] if i < 4 else pb7
                tk = P.emit("tensor", lambda h, hh=hh, kc=kc, bk=bk, i=i, dTT=dTT:
                            h.matmul(
                    bk.h[:, (i % 4) * 128:(i % 4) * 128 + 128],
                    dkb[:, hh, kc * 128:kc * 128 + 128], dTT[:, hh, :],
                    start=True, stop=True), "t" if i in (3, 7) else None)
                if i == 3:
                    pb[2].end_write(tk)
            pb7.end_write(tk)
            CS["t_uwmm"] = tk
            pb[2].read_from("scalar")
            tw1 = P.emit("scalar", lambda h: h.activation(
                flat(dwf[:, 0:4, :]), pb[2].h[:, 0:512], AF.Copy), "s")
            pb[2].add_reader(tw1)
            pb7.read_from("scalar")
            tw2 = P.emit("scalar", lambda h: h.activation(
                flat(dwf[:, 4:8, :]), pb7.h[:, 0:512], AF.Copy), "s")
            pb7.add_reader(tw2)
            CS["t_wf"] = P.now("s")
            CS["dTT"] = dTT

        def scan_wS(tc):
            pb[0].begin_write()
            pb[1].begin_write()
            P.wait("tensor", CS["t_wf"])
            P.wait("tensor", CS["t_scan_prev"])
            for hh in range(NH):
                bk = pb[0] if hh < 2 else pb[1]
                for kc in range(2):
                    tk = P.emit("tensor", lambda h, hh=hh, kc=kc, bk=bk: h.matmul(
                        bk.h[:, (hh % 2) * 256:(hh % 2) * 256 + 256],
                        dwf[:, 2 * hh + kc, :], Sbf[:, 2 * hh + kc, :],
                        start=(kc == 0), stop=(kc == 1)),
                        "t" if kc == 1 and hh in (1, 3) else None)
                if hh == 1:
                    pb[0].end_write(tk)
            pb[1].end_write(tk)

        def scan_uadj(tc):
            pb[0].read_from("vector")
            ta1 = P.emit("vector", lambda h: h.tensor_tensor(
                flat(dua[:, 0:2, :]), flat(du[:, 0:2, :]), pb[0].h[:, 0:512],
                ALU.subtract), "v")
            pb[0].add_reader(ta1)
            pb[1].read_from("vector")
            ta2 = P.emit("vector", lambda h: h.tensor_tensor(
                flat(dua[:, 2:4, :]), flat(du[:, 2:4, :]), pb[1].h[:, 0:512],
                ALU.subtract), "v")
            pb[1].add_reader(ta2)
            CS["t_ua"] = P.now("v")

        def scan_oMM(tc):
            tr0 = PAD + tc * 128
            pp = tc % 2
            pb[0].begin_write()
            pb[1].begin_write()
            P.wait("tensor", CS["t_ua"])
            for hh in range(NH):
                bk = pb[0] if hh < 2 else pb[1]
                for kc in range(2):
                    P.emit("tensor", lambda h, hh=hh, kc=kc, bk=bk, tc=tc, tr0=tr0:
                           h.matmul(
                        bk.h[:, (hh % 2) * 256:(hh % 2) * 256 + 256],
                        qf[:, 2 * hh + kc, tr0:tr0 + 128], Sbf[:, 2 * hh + kc, :],
                        start=(kc == 0), stop=False), None)
                tk = P.emit("tensor", lambda h, hh=hh, bk=bk, pp=pp: h.matmul(
                    bk.h[:, (hh % 2) * 256:(hh % 2) * 256 + 256],
                    datn[:, pp, hh, :], dua[:, hh, :], start=False, stop=True),
                    "t" if hh in (1, 3) else None)
                if hh == 1:
                    pb[0].end_write(tk)
            pb[1].end_write(tk)

        def scan_oevac(tc):
            for hh in range(NH):
                bk = pb[0] if hh < 2 else pb[1]
                bk.read_from("vector")
                tr = P.emit("vector", lambda h, hh=hh, bk=bk, tc=tc:
                            h.scalar_tensor_tensor(
                    ACC[:, tc, hh * 256:hh * 256 + 256],
                    bk.h[:, (hh % 2) * 256:(hh % 2) * 256 + 256],
                    psc[:, tc, 36 + hh:37 + hh],
                    ACC[:, tc, hh * 256:hh * 256 + 256], ALU.mult, ALU.add), "v")
                bk.add_reader(tr)

        def scan_dS(tc, rnd):
            h0, h1 = 2 * rnd, 2 * rnd + 1
            pb[4].begin_write()
            pb[5].begin_write()
            P.wait("tensor", CS["t_ua"])
            for hh, bk in ((h0, pb[4]), (h1, pb[5])):
                for kc in range(2):
                    tk = P.emit("tensor", lambda h, hh=hh, kc=kc, bk=bk, tc=tc:
                                h.matmul(
                        bk.h[:, kc * 256:kc * 256 + 256],
                        ktok[:, tc, hh * 256 + kc * 128:hh * 256 + kc * 128 + 128],
                        dua[:, hh, :], start=True, stop=True),
                        "t" if kc == 1 else None)
                if hh == h0:
                    pb[4].end_write(tk)
            pb[5].end_write(tk)
            for hh, bk in ((h0, pb[4]), (h1, pb[5])):
                bk.read_from("vector")
                ts1 = P.emit("vector", lambda h, hh=hh, bk=bk: h.tensor_tensor(
                    flat(S32[:, 2 * hh:2 * hh + 2, :]),
                    flat(S32[:, 2 * hh:2 * hh + 2, :]),
                    bk.h[:, 0:512], ALU.add), "v")
                bk.add_reader(ts1)
                P.wait("scalar", ts1)
                CS["t_scan_prev"] = P.emit("scalar", lambda h, hh=hh: h.activation(
                    flat(Sbf[:, 2 * hh:2 * hh + 2, :]),
                    flat(S32[:, 2 * hh:2 * hh + 2, :]), AF.Copy), "s")

        pre_scales(0)
        pre_Gmm(0)
        pre_Aevac(0)
        pre_attnT(0)
        pre_C0(0)
        for lev in range(NLEV):
            ch_CMM(lev)
            ch_TR(lev)
            ch_QMM(lev)
        pre_uw(0)

        if half == 0:
            for nm, buf in (("tt0", CS["dTT"]), ("u0", du),
                            ("wf0", dwf), ("atn0", datn[:, 0])):
                if nm in tapd:
                    P.wait("scalar", CS["t_wf"])
                    for a in range(NH):
                        P.emit("scalar", lambda h, nm=nm, a=a, buf=buf: h.dma_start(
                            out=tapd[nm].ap()[a], in_=buf[:, a, :]), "ds")

        for tc in range(NC):
            tn = tc + 1
            have = tn < NC
            if have:
                pre_scales(tn)
            scan_wS(tc)
            if have:
                pre_Gmm(tn)
            scan_uadj(tc)
            if have:
                pre_Aevac(tn)
                pre_attnT(tn)
                pre_C0(tn)
                ch_CMM(0)
            scan_oMM(tc)
            if have:
                ch_TR(0)
            scan_oevac(tc)
            scan_dS(tc, 0)
            if have:
                ch_QMM(0)
            scan_dS(tc, 1)
            if have:
                for lev in range(1, NLEV):
                    ch_CMM(lev)
                    ch_TR(lev)
                    ch_QMM(lev)
                pre_uw(tn)

            ckpt("delta" + str(half) + "_" + str(tc))
        t_scan_prev = CS["t_scan_prev"]
        P.barrier()
        ex.close()
        qx.close()
        ckpt("delta" + str(half))

        if "accfin" in tapd:
            P.wait("scalar", P.now("v"))
            for tc in range(NC):
                P.emit("scalar", lambda h, tc=tc: h.dma_start(
                    out=tapd["accfin"].ap()[half, tc], in_=ACC[:, tc, :]), "ds")

        fx = ExitStack()
        sbf_ = lambda n, sh, dt: fx.enter_context(
            nc.sbuf_tensor(f"{n}_{half}", sh, dt, side="left"))
        ot = sbf_("ot", [128, D], BF16)
        of = sbf_("of", [128, CH, TH], BF16)
        wof = sbf_("wof", [128, CH, TH], F32)
        xr = sbf_("xr", [128, D], F32)
        xnt = sbf_("xnt", [128, D], F32)

        t_delta = P.now("v")
        t_ot_free = None
        for tc in range(NC):
            P.wait("vector", t_delta)
            for hh in range(NH):
                P.emit("vector", lambda h, tc=tc, hh=hh: h.tensor_tensor(
                    scrap[:, 256 * hh:256 * hh + 256],
                    ACC[:, tc, hh * 256:hh * 256 + 256],
                    ACC[:, tc, hh * 256:hh * 256 + 256], ALU.mult), "v")
            for hh in range(NH):
                P.emit("vector", lambda h, tc=tc, hh=hh: h.tensor_reduce(
                    psc[:, tc, 40 + hh:41 + hh], scrap[:, 256 * hh:256 * hh + 256],
                    AXX, ALU.add), "v")
            P.wait("scalar", P.now("v"))
            t_r = P.emit("scalar", lambda h, tc=tc: h.activation(
                psc[:, tc, 40:44], psc[:, tc, 40:44], AF.Sqrt,
                scale=1.0 / 256, bias=ecst[:, 0:1]), "s")
            P.wait("vector", t_r)
            P.emit("vector", lambda h, tc=tc: h.reciprocal(
                psc[:, tc, 40:44], psc[:, tc, 40:44]), "v")
            P.vgap()
            P.wait("vector", t_ot_free)
            for hh in range(NH):
                P.emit("vector", lambda h, tc=tc, hh=hh: h.tensor_scalar(
                    ot[:, hh * 256:hh * 256 + 256], ACC[:, tc, hh * 256:hh * 256 + 256],
                    psc[:, tc, 40 + hh:41 + hh], None, ALU.mult), "v")
            t_otk = P.now("v")
            P.wait("tensor", t_otk)
            pbT.begin_write()
            for cc in range(CH):
                tk = P.emit("tensor", lambda h, cc=cc: h.transpose(
                    pbT.h[:, cc * 128:(cc + 1) * 128], ot[:, cc * 128:(cc + 1) * 128],
                    eyeb[:, :]), "t" if cc == CH - 1 else None)
            pbT.end_write(tk)
            t_ot_free = tk
            pbT.read_from("scalar")
            tev = P.emit("scalar", lambda h, tc=tc: h.activation(
                flat(of[:, :, tc * 128:tc * 128 + 128]), pbT.h[:, :], AF.Copy), "s")
            pbT.add_reader(tev)
        t_of = P.now("s")
        ckpt("of" + str(half))

        if "of" in tapd:
            P.wait("scalar", t_of)
            for cc in range(CH):
                P.emit("scalar", lambda h, cc=cc: h.dma_start(
                    out=tapd["of"].ap()[half, cc], in_=of[:, cc, :]), "ds")

        P.wait("tensor", t_of)
        P.wait("tensor", t_wofold if half == 0 else t_of)
        for tt in range(NT):
            for mc in range(CH):
                bk = pb[mc % 2]
                bk.begin_write()
                for kc in range(CH):
                    tk = P.emit("tensor", lambda h, mc=mc, kc=kc, tt=tt: h.matmul(
                        bk.h[:, 0:512], wo[:, kc, mc * 128:(mc + 1) * 128],
                        of[:, kc, tt * 512:tt * 512 + 512],
                        start=(kc == 0), stop=(kc == CH - 1)),
                        "t" if kc == CH - 1 else None)
                bk.end_write(tk)
                bk.read_from("scalar")
                tev = P.emit("scalar", lambda h, mc=mc, tt=tt: h.activation(
                    wof[:, mc, tt * 512:tt * 512 + 512], bk.h[:, 0:512], AF.Copy), "s")
                bk.add_reader(tev)
        t_wof = P.now("s")
        ckpt("wof" + str(half))

        for tc in range(NC):
            t0 = half * TH + tc * 128
            t_x = P.emit("sync", lambda h, t0=t0: h.dma_start(
                out=xr[:, :], in_=x_d.ap()[t0:t0 + 128, :]), "dl")
            pb[4].begin_write()
            pb[5].begin_write()
            P.wait("tensor", t_wof)
            for cc in range(CH):
                bk = pb[4] if cc < 4 else pb[5]
                tk = P.emit("tensor", lambda h, cc=cc, tc=tc, bk=bk: h.transpose(
                    bk.h[:, (cc % 4) * 128:(cc % 4) * 128 + 128],
                    wof[:, cc, tc * 128:tc * 128 + 128], eyef[:, :]),
                    "t" if cc in (3, 7) else None)
                if cc == 3:
                    pb[4].end_write(tk)
            pb[5].end_write(tk)
            P.wait("vector", t_x)
            for g, bk in ((0, pb[4]), (1, pb[5])):
                bk.read_from("vector")
                tr = P.emit("vector", lambda h, g=g, bk=bk: h.tensor_tensor(
                    xnt[:, g * 512:g * 512 + 512], xr[:, g * 512:g * 512 + 512],
                    bk.h[:, 0:512], ALU.add), "v")
                bk.add_reader(tr)
            t_xn = P.now("v")
            P.wait("scalar", t_xn)
            t_st = P.emit("scalar", lambda h, t0=t0: h.dma_start(
                out=xnew_d.ap()[t0:t0 + 128, :], in_=xnt[:, :]), "ds")
            if "xnew" in tapd:
                P.emit("scalar", lambda h, t0=t0: h.dma_start(
                    out=tapd["xnew"].ap()[t0:t0 + 128, :], in_=xnt[:, :]), "ds")
            P.wait("sync", t_st)
            P.wait("vector", t_st)
        P.barrier()
        fx.close()
        hs.close()
        ckpt("resid" + str(half))

    P.barrier()
    att.close()
    ckpt("attn")

    fz = ExitStack()
    sbf2 = lambda n, sh, dt: fz.enter_context(
        nc.sbuf_tensor(n, sh, dt, side="left"))
    wgs = sbf2("wgs", [128, CH, HID], BF16)
    wus = sbf2("wus", [128, CH, HID], BF16)
    wds = sbf2("wds", [128, HC, D], BF16)
    hbuf = sbf2("hbuf", [128, HC, FT], BF16)
    ln2f = sbf2("ln2f", [128, CH, FT], BF16)
    ga = sbf2("ga", [128, FT], BF16)
    lnt2 = sbf2("lnt2", [128, D], BF16)
    xr2 = sbf2("xr2", [128, 2, D], F32)
    dwn = sbf2("dwn", [128, CH, FT], F32)
    ont = sbf2("ont", [128, D], F32)

    t_wgc = {}
    t_wuc = {}
    t_wdc = {}
    for kc in range(CH):
        t_wgc[kc] = P.emit("gpsimd", lambda h, kc=kc: h.dma_start(
            out=wgs[:, kc, :], in_=wg_d.ap()[kc]), "dg")
        t_wuc[kc] = P.emit("gpsimd", lambda h, kc=kc: h.dma_start(
            out=wus[:, kc, :], in_=wu_d.ap()[kc]), "dg")
    for kc in range(HC):
        t_wdc[kc] = P.emit("gpsimd", lambda h, kc=kc: h.dma_start(
            out=wds[:, kc, :], in_=wd_d.ap()[kc]), "dg")

    t_lnt2_free = None
    t_xr2_free = None
    t_tile_prev = None
    for ft in range(NFT):
        f0 = ft * FT
        P.wait("sync", t_xr2_free)
        P.wait("scalar", t_tile_prev)
        P.wait("vector", t_tile_prev)
        for tj in range(2):
            t0 = f0 + tj * 128
            t_x = P.emit("sync", lambda h, t0=t0, tj=tj: h.dma_start(
                out=xr2[:, tj, :], in_=xnew_d.ap()[t0:t0 + 128, :]), "dl")
            t_ln = emit_ln_chunk(t_x, xr2[:, tj, :], lnt2[:, :], t_lnt2_free)
            P.wait("tensor", t_ln)
            pbT.begin_write()
            for cc in range(CH):
                tk = P.emit("tensor", lambda h, cc=cc: h.transpose(
                    pbT.h[:, cc * 128:(cc + 1) * 128],
                    lnt2[:, cc * 128:(cc + 1) * 128], eyeb[:, :]),
                    "t" if cc == CH - 1 else None)
            pbT.end_write(tk)
            t_lnt2_free = tk
            pbT.read_from("scalar")
            tev = P.emit("scalar", lambda h, tj=tj: h.activation(
                flat(ln2f[:, :, tj * 128:tj * 128 + 128]), pbT.h[:, :], AF.Copy), "s")
            pbT.add_reader(tev)
        t_ln2 = P.now("s")
        if "ln2f" in tapd:
            P.wait("scalar", t_ln2)
            for cc in range(CH):
                P.emit("scalar", lambda h, cc=cc, ft=ft: h.dma_start(
                    out=tapd["ln2f"].ap()[ft, cc], in_=ln2f[:, cc, :]), "ds")

        P.wait("tensor", t_ln2)
        for mc in range(HC):
            bg_ = pb[(mc % 2) * 2]
            bu_ = pb[(mc % 2) * 2 + 1]
            bg_.begin_write()
            bu_.begin_write()
            for kc in range(CH):
                P.wait("tensor", t_wgc[kc])
                P.wait("tensor", t_wuc[kc])
                tkg = P.emit("tensor", lambda h, mc=mc, kc=kc, bg_=bg_: h.matmul(
                    bg_.h[:, 0:FT], wgs[:, kc, mc * 128:(mc + 1) * 128],
                    ln2f[:, kc, :], start=(kc == 0), stop=(kc == CH - 1)),
                    "t" if kc == CH - 1 else None)
                tk = P.emit("tensor", lambda h, mc=mc, kc=kc, bu_=bu_: h.matmul(
                    bu_.h[:, 0:FT], wus[:, kc, mc * 128:(mc + 1) * 128],
                    ln2f[:, kc, :], start=(kc == 0), stop=(kc == CH - 1)),
                    "t" if kc == CH - 1 else None)
            bg_.end_write(tkg)
            bu_.end_write(tk)
            bg_.read_from("scalar")
            t_g = P.emit("scalar", lambda h, bg_=bg_: h.activation(
                ga[:, :], bg_.h[:, 0:FT], AF.Silu, scale=QS), "s")
            bg_.add_reader(t_g)
            bu_.read_from("vector")
            P.wait("vector", t_g)
            t_h = P.emit("vector", lambda h, mc=mc, bu_=bu_: h.scalar_tensor_tensor(
                hbuf[:, mc, :], bu_.h[:, 0:FT], QS, ga[:, :], ALU.mult, ALU.mult), "v")
            bu_.add_reader(t_h)
            P.wait("scalar", t_h)
        t_hb = P.now("v")
        ckpt("gup" + str(ft))
        if "hbuf" in tapd:
            P.wait("scalar", t_hb)
            for mc in range(HC):
                P.emit("scalar", lambda h, mc=mc, ft=ft: h.dma_start(
                    out=tapd["hbuf"].ap()[ft, mc], in_=hbuf[:, mc, :]), "ds")

        P.wait("tensor", t_hb)
        for kc in range(HC):
            P.wait("tensor", t_wdc[kc])
        for mc in range(CH):
            bk = pb[2 + mc % 4]
            bk.begin_write()
            for kc in range(HC):
                tk = P.emit("tensor", lambda h, kc=kc, mc=mc, bk=bk: h.matmul(
                    bk.h[:, 0:FT], wds[:, kc, mc * 128:(mc + 1) * 128],
                    hbuf[:, kc, :], start=(kc == 0), stop=(kc == HC - 1)),
                    "t" if kc == HC - 1 else None)
            bk.end_write(tk)
            bk.read_from("scalar")
            tev = P.emit("scalar", lambda h, mc=mc, bk=bk: h.activation(
                dwn[:, mc, :], bk.h[:, 0:FT], AF.Copy), "s")
            bk.add_reader(tev)
        t_dwn = P.now("s")
        ckpt("dwn" + str(ft))

        for tj in range(2):
            t0 = f0 + tj * 128
            pb[0].begin_write()
            pb[1].begin_write()
            P.wait("tensor", t_dwn)
            for cc in range(CH):
                bk = pb[0] if cc < 4 else pb[1]
                tk = P.emit("tensor", lambda h, cc=cc, tj=tj, bk=bk: h.transpose(
                    bk.h[:, (cc % 4) * 128:(cc % 4) * 128 + 128],
                    dwn[:, cc, tj * 128:tj * 128 + 128], eyef[:, :]),
                    "t" if cc in (3, 7) else None)
                if cc == 3:
                    pb[0].end_write(tk)
            pb[1].end_write(tk)
            for g, bk in ((0, pb[0]), (1, pb[1])):
                bk.read_from("vector")
                tr = P.emit("vector", lambda h, g=g, tj=tj, bk=bk:
                            h.scalar_tensor_tensor(
                    ont[:, g * 512:g * 512 + 512], bk.h[:, 0:512], QS,
                    xr2[:, tj, g * 512:g * 512 + 512], ALU.mult, ALU.add), "v")
                bk.add_reader(tr)
            t_o = P.now("v")
            P.wait("scalar", t_o)
            t_ost = P.emit("scalar", lambda h, t0=t0: h.dma_start(
                out=out_d.ap()[t0:t0 + 128, :], in_=ont[:, :]), "ds")
            P.wait("vector", t_ost)
        t_xr2_free = t_o
        t_tile_prev = P.now("t")

    P.wait_queue("sync", "ds")
    P.wait_queue("sync", "dl")
    P.wait_queue("sync", "dg")

    assemble(nc, P)
    return nc, tapd


def assemble(nc, P):
    import bisect as _b
    keep = P.finalize_maps()
    sems = P.sems

    def runner(eng_name):
        def run(h):
            for rec in P.recs[eng_name]:
                kind = rec[0]
                if kind == "dma":
                    _, fn, key = rec
                    fn(h).then_inc(sems[key], 16)
                elif kind == "op":
                    _, fn, inc, pval = rec
                    ins = fn(h)
                    if inc is not None and pval in keep[inc][1]:
                        ins.then_inc(sems[inc], 1)
                else:
                    _, sem, val = rec
                    if sem in keep:
                        kept, m = keep[sem]
                        i = _b.bisect_left(kept, val)
                        if i >= len(kept):
                            i = len(kept) - 1
                        h.wait_ge(sems[sem], m[kept[i]])
                    else:
                        h.wait_ge(sems[sem], val)
        return run

    with nc.Block() as block:
        block.sync(runner("sync"))
        block.gpsimd(runner("gpsimd"))
        block.vector(runner("vector"))
        block.scalar(runner("scalar"))
        block.tensor(runner("tensor"))


def host_prep(inputs, L=2048):
    f32 = np.float32
    Wq = inputs["L0_attn_Wq"]
    Wk = inputs["L0_attn_Wk"]
    Wv = inputs["L0_attn_Wv"]
    Wo = inputs["L0_attn_Wo"]
    Wb = inputs["L0_attn_Wb"]
    gw = inputs["L0_attn_gate_w"].reshape(20, D)
    gb = inputs["L0_attn_gate_b"].reshape(20)
    conv_q = inputs["L0_attn_conv_q"]
    conv_k = inputs["L0_attn_conv_k"]
    conv_v = inputs["L0_attn_conv_v"]
    fir_s = inputs["L0_attn_fir_s"].reshape(D, 3)
    fir_m = inputs["L0_attn_fir_m"].reshape(D, 15)
    fir_l = inputs["L0_attn_fir_l"].reshape(D, 63)

    def wt(w):
        return np.ascontiguousarray(np.asarray(w, f32).T).reshape(CH, 128, D)

    gbw = np.concatenate([np.asarray(gw, f32), np.asarray(Wb, f32)], axis=0)
    wgbT = np.ascontiguousarray(gbw.T).reshape(CH, 128, 24)

    diag = np.zeros((CH, NTAP, 128, 128), dtype=f32)
    idx = np.arange(128)
    filts = [np.asarray(conv_q, f32), np.asarray(conv_k, f32),
             np.asarray(conv_v, f32), np.asarray(fir_s, f32),
             np.asarray(fir_m, f32), np.asarray(fir_l, f32)]
    for cc in range(CH):
        t = 0
        for fm in filts:
            K = fm.shape[1]
            for j in range(K):
                diag[cc, t, idx, idx] = fm[cc * 128:(cc + 1) * 128, j]
                t += 1
    assert t == NTAP

    m = {
        "wqT": wt(Wq), "wkT": wt(Wk), "wvT": wt(Wv), "woT": wt(Wo),
        "wgbT": wgbT, "diag_all": diag,
        "gb_exp": np.ascontiguousarray(np.broadcast_to(gb, (128, 20))).astype(f32),
        "log_temp": np.asarray(inputs["L0_attn_log_temp"], f32).reshape(4, 1),
        "eps_param": np.asarray(inputs["L0_attn_eps_param"], f32).reshape(4, 1),
        "nw": np.ascontiguousarray(
            np.asarray(inputs["L0_attn_norm_w"], f32).reshape(2, 128).T),
        "wgT": np.ascontiguousarray(
            np.asarray(inputs["L0_ffn_w_gate"]).T).reshape(CH, 128, HID),
        "wuT": np.ascontiguousarray(
            np.asarray(inputs["L0_ffn_w_up"]).T).reshape(CH, 128, HID),
        "wdT": np.ascontiguousarray(
            np.asarray(inputs["L0_ffn_w_down"]).T).reshape(HC, 128, D),
    }
    return m


def kernel(**inputs):
    x = np.asarray(inputs["L0_attn_hidden_states"], np.float32)
    B, L, _ = x.shape
    shared = host_prep(inputs, L)
    nc, _ = build(L)
    in_maps = []
    for b in range(B):
        im = dict(shared)
        im["x"] = np.ascontiguousarray(x[b])
        in_maps.append(im)
    res = run_bass_kernel_spmd(nc, in_maps, core_ids=list(range(B)))
    out = np.stack([np.asarray(res.results[b]["out"]) for b in range(B)], axis=0)
    return out.astype(np.float32)


# version 05
# speedup vs baseline: 1.1968x; 1.0435x over previous
from contextlib import ExitStack

import os
import numpy as np
import ml_dtypes

import concourse.bass as bass
import concourse.mybir as mybir
from concourse.bass_utils import run_bass_kernel_spmd

F32 = mybir.dt.float32
BF16 = mybir.dt.bfloat16
I32 = mybir.dt.int32
AF = mybir.ActivationFunctionType
ALU = mybir.AluOpType
AXX = mybir.AxisListType.X

D = 1024
NH = 4
CH = D // 128
HID = 3072
HC = HID // 128
PAD = 64
NTAP = 93
QS = 1.0 / 127.0

ENGS = ("sync", "gpsimd", "vector", "scalar", "tensor")


DMA_QUEUES = {"dl": "sync", "dg": "gpsimd", "ds": "scalar"}
DMA_POOL = 8


def _freeze(fn):
    import types
    if fn.__closure__:
        fn = types.FunctionType(
            fn.__code__, fn.__globals__, fn.__name__, fn.__defaults__,
            tuple(types.CellType(c.cell_contents) for c in fn.__closure__))
    return fn


class Prog:

    def __init__(self, nc, es):
        self.nc = nc
        self.recs = {e: [] for e in ENGS}
        self.sems = {}
        self.cnt = {}
        for name in ("v", "s", "t", "g"):
            self.sems[name] = es.enter_context(nc.semaphore("sem_" + name))
            self.cnt[name] = 0
        for q in DMA_QUEUES:
            for i in range(DMA_POOL):
                key = f"{q}{i}"
                self.sems[key] = es.enter_context(nc.semaphore("sem_" + key))
                self.cnt[key] = 0
        self.qidx = {q: 0 for q in DMA_QUEUES}
        self.lastw = {}
        self.waited = {name: set() for name in ("v", "s", "t", "g")}

    def emit(self, eng, fn, inc=None):
        fn = _freeze(fn)
        if inc in DMA_QUEUES:
            q = inc
            key = f"{q}{self.qidx[q] % DMA_POOL}"
            self.qidx[q] += 1
            prev = self.cnt[key]
            if prev > 0:
                self.wait(DMA_QUEUES[q], (key, prev))
            self.cnt[key] = prev + 16
            tok = (key, self.cnt[key])
            self.recs[eng].append(("dma", fn, key))
            return tok
        tok = None
        pval = 0
        if inc is not None:
            self.cnt[inc] += 1
            pval = self.cnt[inc]
            tok = (inc, pval)
        self.recs[eng].append(("op", fn, inc, pval))
        return tok

    def wait(self, eng, tok):
        if tok is None:
            return
        sem, val = tok
        if val <= 0:
            return
        key = (eng, sem)
        if self.lastw.get(key, -1) >= val:
            return
        self.lastw[key] = val
        if sem in self.waited:
            self.waited[sem].add(val)
        self.recs[eng].append(("wait", sem, val))

    def now(self, sem):
        return (sem, self.cnt[sem])

    def wait_queue(self, eng, q):
        for i in range(DMA_POOL):
            key = f"{q}{i}"
            if self.cnt[key] > 0:
                self.wait(eng, (key, self.cnt[key]))

    def barrier(self, bop=None):
        if os.environ.get("BARRIER_OFF") == "1":
            return
        for sem in ("s", "t"):
            self.wait("vector", (sem, self.cnt[sem]))
        for q in DMA_QUEUES:
            self.wait_queue("vector", q)
        btok = self.emit("vector", self._bop, "v")
        for eng in ("sync", "gpsimd", "scalar", "tensor"):
            self.wait(eng, btok)

    def finalize_maps(self):
        keep = {}
        for sem in ("v", "s", "t", "g"):
            kept = set()
            total = self.cnt[sem]
            for w in self.waited[sem]:
                if 0 < w <= total:
                    kept.add(w)
            if total > 0:
                kept.add(total)
            kept = sorted(kept)
            m = {v: i + 1 for i, v in enumerate(kept)}
            keep[sem] = (kept, m)
        return keep


class Bank:
    def __init__(self, prog, handle):
        self.p = prog
        self.h = handle
        self.readers = []
        self.wtok = None

    def begin_write(self):
        for tok in self.readers:
            self.p.wait("tensor", tok)
        self.readers = []

    def end_write(self, tok):
        self.wtok = tok

    def read_from(self, eng):
        self.p.wait(eng, self.wtok)

    def add_reader(self, tok):
        self.readers.append(tok)


def flat(ap):
    try:
        return ap.rearrange("p a b -> p (a b)")
    except ValueError:
        return ap


class _Stop(Exception):
    pass


_LAST = {}


def build(L=2048, taps=(), until=None):
    TH = min(1024, L // 2)
    NHALF = L // TH
    NC = TH // 128
    NT = TH // 512
    FT = 256
    NFT = L // FT

    nc = bass.Bass("TRN2", target_bir_lowering=False,
                   detect_race_conditions=False)
    es = ExitStack()
    P = Prog(nc, es)
    _LAST["nc"] = nc
    _LAST["P"] = P
    _LAST["tapd"] = tapd = {}
    tapset = set(taps)

    dr = lambda n, sh, dt, k: nc.dram_tensor(n, sh, dt, kind=k)
    x_d = dr("x", [L, D], F32, "ExternalInput")
    wq_d = dr("wqT", [CH, 128, D], F32, "ExternalInput")
    wk_d = dr("wkT", [CH, 128, D], F32, "ExternalInput")
    wv_d = dr("wvT", [CH, 128, D], F32, "ExternalInput")
    wo_d = dr("woT", [CH, 128, D], F32, "ExternalInput")
    wgb_d = dr("wgbT", [CH, 128, 24], F32, "ExternalInput")
    diag_d = dr("diag_all", [CH, NTAP, 128, 128], F32, "ExternalInput")
    gbx_d = dr("gb_exp", [128, 20], F32, "ExternalInput")
    ltemp_d = dr("log_temp", [4, 1], F32, "ExternalInput")
    epsp_d = dr("eps_param", [4, 1], F32, "ExternalInput")
    nw_d = dr("nw", [128, 2], F32, "ExternalInput")
    wg_d = dr("wgT", [CH, 128, HID], I32, "ExternalInput")
    wu_d = dr("wuT", [CH, 128, HID], I32, "ExternalInput")
    wd_d = dr("wdT", [HC, 128, D], I32, "ExternalInput")
    out_d = dr("out", [L, D], F32, "ExternalOutput")
    xnew_d = dr("xnew", [L, D], F32, "Internal")
    bsc_d = dr("bscr", [4, 3], F32, "Internal")

    def mktap(name, shape, dt=F32):
        if name in tapset:
            tapd[name] = dr("tap_" + name, shape, dt, "ExternalOutput")

    mktap("ln1f", [NHALF, CH, 128, TH], BF16)
    mktap("qf", [NHALF, CH, 128, TH], BF16)
    mktap("kf", [NHALF, CH, 128, TH], BF16)
    mktap("vf", [NHALF, CH, 128, TH], BF16)
    mktap("ktok", [NHALF, NC, 128, D], BF16)
    mktap("vtok", [NHALF, NC, 128, D], BF16)
    mktap("khatf", [NHALF, CH, 128, TH], BF16)
    mktap("psc", [NHALF, NC, 128, 40], F32)
    mktap("accmix", [NHALF, NC, 128, D], BF16)
    mktap("accfin", [NHALF, NC, 128, D], BF16)
    mktap("of", [NHALF, CH, 128, TH], BF16)
    mktap("xnew", [L, D], F32)
    mktap("ln2f", [NFT, CH, 128, FT], BF16)
    mktap("hbuf", [NFT, HC, 128, FT], BF16)

    eyeb_d = nc.inline_tensor(np.eye(128, dtype=ml_dtypes.bfloat16), "eyeb_c")
    eyef_d = nc.inline_tensor(np.eye(128, dtype=np.float32), "eyef_c")
    negm_d = nc.inline_tensor(-np.tril(np.ones((128, 128), dtype=np.float32), -1),
                              "negm_c")
    ecst_np = np.zeros((128, 2), dtype=np.float32)
    ecst_np[:, 0] = 1e-5
    ecst_np[:, 1] = 1e-6
    ecst_d = nc.inline_tensor(ecst_np, "ecst_c")
    cmT_d = nc.inline_tensor(np.triu(np.ones((128, 128), dtype=np.float32), 0),
                             "cmT_c")

    _side = "left" if os.environ.get("ALL_LEFT") == "1" else "right"
    sbr = lambda n, sh, dt: es.enter_context(nc.sbuf_tensor(n, sh, dt, side=_side))
    eyeb = sbr("eyeb", [128, 128], BF16)
    eyef = sbr("eyef", [128, 128], F32)
    negm = sbr("negm", [128, 128], F32)
    cmT = sbr("cmT", [128, 128], F32)
    gbx = sbr("gbx", [128, 20], F32)
    nwt = sbr("nwt", [128, 2], F32)
    scb = sbr("scb", [128, 12], F32)
    tiny = sbr("tiny", [128, 8], F32)
    stat = sbr("stat", [128, 96], F32)
    scrap = sbr("scrap", [128, 1024], F32)
    ecst = sbr("ecst", [128, 2], F32)
    P._bop = lambda h: h.tensor_copy(scrap[0:1, 0:1], scrap[0:1, 1:2])
    P.vgap = lambda: P.wait("vector", P.now("v"))
    P.sgap = lambda: P.wait("scalar", P.now("s"))
    wgb = sbr("wgb", [128, CH, 24], BF16)
    att = ExitStack()
    sbar = lambda n, sh, dt: att.enter_context(nc.sbuf_tensor(n, sh, dt, side=_side))
    wo = sbar("wo", [128, CH, D], BF16)
    cq = sbar("cq", [128, CH, PAD], BF16)
    ck = sbar("ck", [128, CH, PAD], BF16)
    cv = sbar("cv", [128, CH, PAD], BF16)
    cvp = sbar("cvp", [128, CH, PAD], BF16)
    S32 = sbar("S32", [128, 2 * NH, 256], F32)
    Sbf = sbar("Sbf", [128, 2 * NH, 256], BF16)

    pb = [Bank(P, es.enter_context(nc.psum_tensor(f"pb{i}", [128, 512], F32)))
          for i in range(6)]
    pbT = Bank(P, es.enter_context(nc.psum_tensor("pbT", [128, 1024], BF16)))
    pbS = pbT
    pb7 = Bank(P, es.enter_context(nc.psum_tensor("pb7", [128, 512], F32)))

    lds = []
    _consts = ((eyeb, eyeb_d), (eyef, eyef_d), (negm, negm_d),
               (cmT, cmT_d), (gbx, gbx_d), (ecst, ecst_d))
    if os.environ.get("NO_CONSTS") == "1":
        _consts = ((eyeb, eyeb_d),)
    for dst, src in _consts:
        lds.append(P.emit("sync", lambda h, d=dst, s=src: h.dma_start(
            out=d[:, :], in_=s.ap()[:, :]), "dl"))
    if os.environ.get("NO_CONSTS") != "1":
        lds.append(P.emit("sync", lambda h: h.dma_start(
            out=nwt[:, :], in_=nw_d.ap()[:, :]), "dl"))
    if os.environ.get("NO_LTEMP") != "1":
        lds.append(P.emit("sync", lambda h: h.dma_start(
            out=tiny[0:4, 0:1], in_=ltemp_d.ap()[:, :]), "dl"))
        lds.append(P.emit("sync", lambda h: h.dma_start(
            out=tiny[0:4, 1:2], in_=epsp_d.ap()[:, :]), "dl"))
    if os.environ.get("NO_WOWGB") != "1":
        t_wo = P.emit("gpsimd", lambda h: h.dma_start(
            out=wo[:, :, :], in_=wo_d.ap().transpose([1, 0, 2])), "dg")
        t_wgb = P.emit("gpsimd", lambda h: h.dma_start(
            out=wgb[:, :, :], in_=wgb_d.ap().transpose([1, 0, 2])), "dg")
    else:
        t_wo = t_wgb = None

    for tok in lds:
        P.wait("scalar", tok)
        P.wait("vector", tok)
    P.emit("vector", lambda h: h.memset(scrap[:, :], 0.0), "v")
    if os.environ.get("NO_TINY") != "1":
        P.emit("scalar", lambda h: h.activation(tiny[0:4, 2:3], tiny[0:4, 0:1], AF.Exp), "s")
        P.wait("vector", P.now("s"))
        P.emit("vector", lambda h: h.tensor_scalar(
            tiny[0:4, 5:6], tiny[0:4, 1:2], 0.0, 0.2, ALU.max, ALU.min), "v")
        P.emit("vector", lambda h: h.reciprocal(tiny[0:4, 4:5], tiny[0:4, 2:3]), "v")
        P.emit("vector", lambda h: h.tensor_scalar(
            tiny[0:4, 6:7], tiny[0:4, 5:6], -5.0, 1.0, ALU.mult, ALU.add), "v")
    P.wait("gpsimd", P.now("v"))
    if os.environ.get("NO_SCB") != "1":
        P.emit("gpsimd", lambda h: h.dma_start(out=bsc_d.ap()[:, :], in_=tiny[0:4, 4:7]), "dg")
        P.wait_queue("gpsimd", "dg")
        t_scb = P.emit("gpsimd", lambda h: h.dma_start(
            out=scb[:, 0:12],
            in_=bsc_d.ap().rearrange("h c -> (h c)").unsqueeze(0).broadcast_to([128, 12])),
            "dg")
        P.wait("vector", t_scb)
        P.wait("scalar", t_scb)
    P.wait("vector", t_wo)
    if os.environ.get("NO_FOLD") != "1":
        for kc in range(CH):
            P.emit("vector", lambda h, kc=kc: h.tensor_scalar(
                wo[:, kc, :], wo[:, kc, :], nwt[:, kc % 2:kc % 2 + 1], None, ALU.mult), "v")
    t_wofold = P.now("v")
    if os.environ.get("NO_MS") != "1":
        P.emit("vector", lambda h: h.memset(flat(S32[:, :, :]), 0.0), "v")
        P.emit("vector", lambda h: h.memset(flat(Sbf[:, :, :]), 0.0), "v")
    t_sinit = P.now("v")
    P.wait("tensor", t_wgb)

    def emit_ln_chunk(t_x, xsl, lnt_dst, t_lnt_free):
        P.wait("vector", t_x)
        P.wait("scalar", t_x)
        P.emit("vector", lambda h: h.tensor_reduce(
            stat[:, 0:1], xsl, AXX, ALU.add), "v")
        P.wait("scalar", t_lnt_free)
        t_sq = P.emit("scalar", lambda h: h.activation(
            lnt_dst, xsl, AF.Square, accum_out=stat[:, 1:2]), "s")
        P.wait("vector", t_sq)
        P.vgap()
        P.emit("vector", lambda h: h.tensor_scalar(
            stat[:, 0:1], stat[:, 0:1], 1.0 / D, None, ALU.mult), "v")
        P.emit("vector", lambda h: h.tensor_scalar(
            stat[:, 1:2], stat[:, 1:2], 1.0 / D, None, ALU.mult), "v")
        P.emit("vector", lambda h: h.tensor_tensor(
            stat[:, 2:3], stat[:, 0:1], stat[:, 0:1], ALU.mult), "v")
        P.vgap()
        P.emit("vector", lambda h: h.tensor_tensor(
            stat[:, 2:3], stat[:, 1:2], stat[:, 2:3], ALU.subtract), "v")
        P.wait("scalar", P.now("v"))
        t_rs = P.emit("scalar", lambda h: h.activation(
            stat[:, 2:3], stat[:, 2:3], AF.Sqrt, bias=ecst[:, 0:1]), "s")
        P.wait("vector", t_rs)
        P.emit("vector", lambda h: h.reciprocal(stat[:, 3:4], stat[:, 2:3]), "v")
        P.vgap()
        P.wait("vector", t_lnt_free)
        return P.emit("vector", lambda h: h.tensor_scalar(
            lnt_dst, xsl, stat[:, 0:1], stat[:, 3:4], ALU.subtract, ALU.mult), "v")

    def ckpt(name):
        if until == name:
            raise _Stop

    ckpt("prologue")
    t_scan_prev = t_sinit
    for half in range(NHALF):
        hs = ExitStack()
        sbl = lambda n, sh, dt: hs.enter_context(
            nc.sbuf_tensor(f"{n}_{half}", sh, dt, side="left"))
        ACC = sbl("ACC", [128, NC, D], BF16)
        vtok = sbl("vtok", [128, NC, D], BF16)
        ktok = sbl("ktok", [128, NC, D], BF16)
        khatf = sbl("khatf", [128, CH, TH], BF16)
        psc = sbl("psc", [128, NC, 48], F32)
        qx = ExitStack()
        sbq = lambda n, sh, dt: qx.enter_context(
            nc.sbuf_tensor(f"{n}_{half}", sh, dt, side="left"))
        qf = sbq("qf", [128, CH, PAD + TH], BF16)
        kf = sbq("kf", [128, CH, PAD + TH], BF16)
        vf = sbq("vf", [128, CH, PAD + TH], BF16)

        ax = ExitStack()
        sba = lambda n, sh, dt: ax.enter_context(
            nc.sbuf_tensor(f"{n}_{half}", sh, dt, side="left"))
        xst = sba("xst", [128, 2, D], F32)
        lnt = sba("lnt", [128, 2, D], BF16)
        ln1f = sba("ln1f", [128, CH, TH], BF16)
        war = sba("war", [128, CH, D], BF16)
        lgt = sba("lgt", [128, NC, 20], F32)

        for tc in range(NC):
            t0 = half * TH + tc * 128
            t_x = P.emit("sync", lambda h, tc=tc, t0=t0: h.dma_start(
                out=xst[:, tc % 2, :], in_=x_d.ap()[t0:t0 + 128, :]), "dl")
            P.wait("vector", t_x)
            P.emit("vector", lambda h, tc=tc: h.tensor_reduce(
                stat[:, tc:tc + 1], xst[:, tc % 2, :], AXX, ALU.add), "v")
            P.wait("scalar", t_x)
            P.emit("scalar", lambda h, tc=tc: h.activation(
                scrap[:, 0:D], xst[:, tc % 2, :], AF.Square,
                accum_out=stat[:, 8 + tc:9 + tc]), "s")
            if tc + 2 < NC:
                P.wait("sync", P.now("v"))
                P.wait("sync", P.now("s"))
        t_p1v, t_p1s = P.now("v"), P.now("s")
        P.wait("vector", t_p1s)
        P.emit("vector", lambda h: h.tensor_scalar(
            stat[:, 0:NC], stat[:, 0:NC], 1.0 / D, None, ALU.mult), "v")
        P.emit("vector", lambda h: h.tensor_scalar(
            stat[:, 8:8 + NC], stat[:, 8:8 + NC], 1.0 / D, None, ALU.mult), "v")
        P.emit("vector", lambda h: h.tensor_tensor(
            stat[:, 16:16 + NC], stat[:, 0:NC], stat[:, 0:NC], ALU.mult), "v")
        P.vgap()
        P.emit("vector", lambda h: h.tensor_tensor(
            stat[:, 16:16 + NC], stat[:, 8:8 + NC], stat[:, 16:16 + NC],
            ALU.subtract), "v")
        P.wait("scalar", P.now("v"))
        t_rs = P.emit("scalar", lambda h: h.activation(
            stat[:, 16:16 + NC], stat[:, 16:16 + NC], AF.Sqrt,
            bias=ecst[:, 0:1]), "s")
        P.wait("vector", t_rs)
        P.emit("vector", lambda h: h.reciprocal(
            stat[:, 24:24 + NC], stat[:, 16:16 + NC]), "v")
        P.vgap()
        t_chain = P.now("v")

        t_ap2 = [t_p1v, t_p1v]
        t_trs = [None, None]
        for tc in range(NC):
            t0 = half * TH + tc * 128
            P.wait("sync", t_ap2[tc % 2])
            P.wait("sync", t_p1s if tc < 2 else None)
            t_x = P.emit("sync", lambda h, tc=tc, t0=t0: h.dma_start(
                out=xst[:, tc % 2, :], in_=x_d.ap()[t0:t0 + 128, :]), "dl")
            P.wait("vector", t_x)
            P.wait("vector", t_chain if tc == 0 else None)
            P.wait("vector", t_trs[tc % 2])
            t_ap2[tc % 2] = P.emit("vector", lambda h, tc=tc: h.tensor_scalar(
                lnt[:, tc % 2, :], xst[:, tc % 2, :], stat[:, tc:tc + 1],
                stat[:, 24 + tc:25 + tc], ALU.subtract, ALU.mult), "v")
            P.wait("tensor", t_ap2[tc % 2])
            pbT.begin_write()
            for cc in range(CH):
                tk = P.emit("tensor", lambda h, cc=cc, tc=tc: h.transpose(
                    pbT.h[:, cc * 128:(cc + 1) * 128],
                    lnt[:, tc % 2, cc * 128:(cc + 1) * 128], eyeb[:, :]),
                    "t" if cc == CH - 1 else None)
            pbT.end_write(tk)
            t_trs[tc % 2] = tk
            pbT.read_from("scalar")
            tev = P.emit("scalar", lambda h, tc=tc: h.activation(
                flat(ln1f[:, :, tc * 128:(tc + 1) * 128]), pbT.h[:, :], AF.Copy), "s")
            pbT.add_reader(tev)
            if os.environ.get("NO_GATE") == "1":
                continue
            bg = pb[4] if tc % 2 == 0 else pb[5]
            bg.begin_write()
            P.wait("tensor", tev)
            for kc in range(CH):
                tk = P.emit("tensor", lambda h, tc=tc, kc=kc: h.matmul(
                    bg.h[:, 0:24], ln1f[:, kc, tc * 128:(tc + 1) * 128],
                    wgb[:, kc, :], start=(kc == 0), stop=(kc == CH - 1)),
                    "t" if kc == CH - 1 else None)
            bg.end_write(tk)
            bg.read_from("vector")
            t_gl = P.emit("vector", lambda h, tc=tc: h.tensor_tensor(
                lgt[:, tc, :], bg.h[:, 0:20], gbx[:, :], ALU.add), "v")
            bg.read_from("scalar")
            t_bt = P.emit("scalar", lambda h, tc=tc: h.activation(
                psc[:, tc, 20:24], bg.h[:, 20:24], AF.Sigmoid), "s")
            bg.add_reader(t_gl)
            bg.add_reader(t_bt)
        t_ln_all = P.now("s")

        if os.environ.get("NO_PROBS") != "1":
            P.vgap()
            for tc in range(NC):
                for hh in range(NH):
                    P.emit("vector", lambda h, tc=tc, hh=hh: h.reduce_max(
                        stat[:, 16 + hh * NC + tc:17 + hh * NC + tc],
                        lgt[:, tc, 5 * hh:5 * hh + 5], AXX), "v")
            P.vgap()
            for hh in range(NH):
                P.emit("vector", lambda h, hh=hh: h.tensor_scalar(
                    stat[:, 16 + hh * NC:16 + hh * NC + NC],
                    stat[:, 16 + hh * NC:16 + hh * NC + NC],
                    scb[:, 3 * hh:3 * hh + 1], -1.0, ALU.mult, ALU.mult), "v")
            P.wait("scalar", P.now("v"))
            for tc in range(NC):
                for hh in range(NH):
                    P.emit("scalar", lambda h, tc=tc, hh=hh: h.activation(
                        lgt[:, tc, 5 * hh:5 * hh + 5], lgt[:, tc, 5 * hh:5 * hh + 5],
                        AF.Exp, bias=stat[:, 16 + hh * NC + tc:17 + hh * NC + tc],
                        scale=scb[:, 3 * hh:3 * hh + 1]), "s")
            P.wait("vector", P.now("s"))
            for tc in range(NC):
                for hh in range(NH):
                    P.emit("vector", lambda h, tc=tc, hh=hh: h.tensor_reduce(
                        stat[:, 16 + hh * NC + tc:17 + hh * NC + tc],
                        lgt[:, tc, 5 * hh:5 * hh + 5], AXX, ALU.add), "v")
            P.vgap()
            P.emit("vector", lambda h: h.reciprocal(
                stat[:, 16:16 + NH * NC], stat[:, 16:16 + NH * NC]), "v")
            P.vgap()
            for hh in range(NH):
                P.emit("vector", lambda h, hh=hh: h.tensor_scalar(
                    stat[:, 16 + hh * NC:16 + hh * NC + NC],
                    stat[:, 16 + hh * NC:16 + hh * NC + NC],
                    scb[:, 3 * hh + 2:3 * hh + 3], None, ALU.mult), "v")
            P.vgap()
            for tc in range(NC):
                for hh in range(NH):
                    P.emit("vector", lambda h, tc=tc, hh=hh: h.tensor_scalar(
                        psc[:, tc, 5 * hh:5 * hh + 5], lgt[:, tc, 5 * hh:5 * hh + 5],
                        stat[:, 16 + hh * NC + tc:17 + hh * NC + tc],
                        scb[:, 3 * hh + 1:3 * hh + 2], ALU.mult, ALU.add), "v")
        ckpt("ln" + str(half))

        if "ln1f" in tapd:
            P.wait("scalar", t_ln_all)
            for cc in range(CH):
                P.emit("scalar", lambda h, cc=cc: h.dma_start(
                    out=tapd["ln1f"].ap()[half, cc], in_=ln1f[:, cc, :]), "ds")

        t_war_free = None
        for wi, (wsrc, dst) in enumerate(((wq_d, qf), (wk_d, kf), (wv_d, vf))):
            P.wait("gpsimd", t_war_free)
            t_w = P.emit("gpsimd", lambda h, wsrc=wsrc: h.dma_start(
                out=war[:, :, :], in_=wsrc.ap().transpose([1, 0, 2])), "dg")
            P.wait("tensor", t_w)
            P.wait("tensor", t_ln_all)
            ckpt(f"pwdma{half}_{wi}")
            for tt in range(int(os.environ.get("TT_LIMIT", NT))):
                for mc in range(int(os.environ.get("MC_LIMIT", CH))):
                    bk = pb[int(os.environ.get('PB_BASE', '0')) + mc % 2] if os.environ.get('PB_ONE') != '1' else pb[0]
                    bk.begin_write()
                    for kc in range(CH):
                        mco = 0 if os.environ.get("LHS0") == "1" else mc
                        tk = P.emit("tensor", lambda h, mco=mco, kc=kc, tt=tt: h.matmul(
                            bk.h[:, 0:512], war[:, kc, mco * 128:(mco + 1) * 128],
                            ln1f[:, kc, tt * 512:tt * 512 + 512],
                            start=(kc == 0), stop=(kc == CH - 1)),
                            "t" if kc == CH - 1 else None)
                    bk.end_write(tk)
                    if os.environ.get("EVAC_V") == "1":
                        bk.read_from("vector")
                        tev = P.emit("vector", lambda h, mc=mc, tt=tt, dst=dst:
                                     h.tensor_copy(
                            dst[:, mc, PAD + tt * 512:PAD + tt * 512 + 512],
                            bk.h[:, 0:512]), "v")
                    else:
                        bk.read_from("scalar")
                        tev = P.emit("scalar", lambda h, mc=mc, tt=tt, dst=dst: h.activation(
                            dst[:, mc, PAD + tt * 512:PAD + tt * 512 + 512],
                            bk.h[:, 0:512], AF.Copy), "s")
                    bk.add_reader(tev)
                    ckpt(f"pwmm{half}_{wi}_{mc}")
            t_war_free = P.now("t")
            ckpt(f"pw{half}_{wi}")
        t_proj = P.now("s")
        ckpt("proj" + str(half))

        if half == 0 and os.environ.get("NO_PADMS") != "1":
            P.wait("vector", t_sinit)
            for buf in (qf, kf, vf):
                P.emit("vector", lambda h, buf=buf: h.memset(
                    flat(buf[:, :, 0:PAD]), 0.0), "v")
        else:
            P.wait("vector", t_proj)
            for buf, car in ((qf, cq), (kf, ck), (vf, cv)):
                P.emit("vector", lambda h, buf=buf, car=car: h.tensor_copy(
                    flat(buf[:, :, 0:PAD]), flat(car[:, :, :])), "v")
        t_pad = P.now("v")
        P.barrier()
        ax.close()

        bx = ExitStack()
        da = bx.enter_context(nc.sbuf_tensor(f"da_{half}", [128, 117, 128], BF16,
                                             side="left"))
        fsb = bx.enter_context(nc.sbuf_tensor(f"fs_{half}", [128, TH], BF16,
                                              side="left"))
        P.wait("vector", t_proj)
        if half + 1 < NHALF:
            for buf, car in ((qf, cq), (kf, ck), (vf, cv)):
                P.emit("vector", lambda h, buf=buf, car=car: h.tensor_copy(
                    flat(car[:, :, :]), flat(buf[:, :, TH:TH + PAD])), "v")
        t_stash = P.now("v")

        t_mmB = {}
        t_daB = {}

        def load_shorts(cc):
            sb = 12 * (cc % 2)
            P.wait("gpsimd", t_mmB.get(cc - 2))
            t_daB[cc] = P.emit("gpsimd", lambda h, cc=cc, sb=sb: h.dma_start(
                out=da[:, sb:sb + 12, :],
                in_=diag_d.ap()[cc, 0:12, :, :].transpose([1, 0, 2])), "dg")

        load_shorts(0)
        load_shorts(1)
        for cc in range(CH):
            sb = 12 * (cc % 2)
            P.wait("tensor", t_daB[cc])
            P.wait("tensor", t_stash)
            P.wait("tensor", t_pad)
            for ti, (buf, f0) in enumerate(((qf, sb + 0), (kf, sb + 4), (vf, sb + 8))):
                b0, b1 = pb[2 * (ti % 2)], pb[2 * (ti % 2) + 1]
                b0.begin_write()
                b1.begin_write()
                for j in range(4):
                    off = PAD - 3 + j
                    for tt in range(NT):
                        bk = b0 if tt % 2 == 0 else b1
                        tk = P.emit("tensor", lambda h, cc=cc, j=j, f0=f0, tt=tt,
                                    off=off, buf=buf, bk=bk: h.matmul(
                            bk.h[:, 0:512], da[:, f0 + j, :],
                            buf[:, cc, off + tt * 512:off + tt * 512 + 512],
                            start=(j == 0), stop=(j == 3)),
                            "t" if j == 3 else None)
                        (b0 if tt % 2 == 0 else b1).end_write(tk)
                for tt in range(NT):
                    bk = b0 if tt % 2 == 0 else b1
                    bk.read_from("scalar")
                    tev = P.emit("scalar", lambda h, cc=cc, tt=tt, buf=buf, bk=bk:
                                 h.activation(
                        buf[:, cc, PAD + tt * 512:PAD + tt * 512 + 512],
                        bk.h[:, 0:512], AF.Silu), "s")
                    bk.add_reader(tev)
            t_mmB[cc] = P.now("t")
            if cc + 2 < CH:
                load_shorts(cc + 2)
        t_conv = P.now("s")
        ckpt("conv" + str(half))

        P.wait("vector", t_conv)
        if half > 0:
            P.emit("vector", lambda h: h.tensor_copy(
                flat(vf[:, :, 0:PAD]), flat(cvp[:, :, :])), "v")
        if half + 1 < NHALF:
            P.emit("vector", lambda h: h.tensor_copy(
                flat(cvp[:, :, :]), flat(vf[:, :, TH:TH + PAD])), "v")
        t_vpad = P.now("v")

        for nm, buf in (("qf", qf), ("kf", kf), ("vf", vf)):
            if nm in tapd:
                P.wait("scalar", t_conv)
                P.wait("scalar", t_vpad)
                for cc in range(CH):
                    P.emit("scalar", lambda h, nm=nm, cc=cc, buf=buf: h.dma_start(
                        out=tapd[nm].ap()[half, cc], in_=buf[:, cc, PAD:PAD + TH]), "ds")

        for tname in ("k", "q", "v"):
            src = {"k": kf, "q": qf, "v": vf}[tname]
            t_need = t_vpad if tname == "v" else t_conv
            for tc in range(NC):
                pbT.begin_write()
                P.wait("tensor", t_need)
                for cc in range(CH):
                    tk = P.emit("tensor", lambda h, src=src, cc=cc, tc=tc: h.transpose(
                        pbT.h[:, cc * 128:(cc + 1) * 128],
                        src[:, cc, PAD + tc * 128:PAD + tc * 128 + 128],
                        eyeb[:, :]), "t" if cc == CH - 1 else None)
                pbT.end_write(tk)
                if tname in ("k", "q"):
                    dc = 24 if tname == "k" else 28
                    pbT.read_from("scalar")
                    for hh in range(NH):
                        tr = P.emit("scalar", lambda h, hh=hh, tc=tc, dc=dc:
                                    h.activation(
                            scrap[:, 256 * hh:256 * hh + 256],
                            pbT.h[:, hh * 256:hh * 256 + 256], AF.Square,
                            accum_out=psc[:, tc, dc + hh:dc + hh + 1]), "s")
                        pbT.add_reader(tr)
                    P.sgap()
                    t_s = P.emit("scalar", lambda h, tc=tc, dc=dc: h.activation(
                        psc[:, tc, dc:dc + 4], psc[:, tc, dc:dc + 4], AF.Sqrt,
                        bias=ecst[:, 1:2]), "s")
                    P.wait("vector", t_s)
                    P.emit("vector", lambda h, tc=tc, dc=dc: h.reciprocal(
                        psc[:, tc, dc:dc + 4], psc[:, tc, dc:dc + 4]), "v")
                    if tname == "k":
                        pbT.read_from("vector")
                        P.vgap()
                        for hh in range(NH):
                            tr = P.emit("vector", lambda h, hh=hh, tc=tc: h.tensor_scalar(
                                ktok[:, tc, hh * 256:hh * 256 + 256],
                                pbT.h[:, hh * 256:hh * 256 + 256],
                                psc[:, tc, 24 + hh:25 + hh], None, ALU.mult), "v")
                            pbT.add_reader(tr)
                else:
                    pbT.read_from("scalar")
                    tr = P.emit("scalar", lambda h, tc=tc: h.activation(
                        vtok[:, tc, :], pbT.h[:, :], AF.Copy), "s")
                    pbT.add_reader(tr)
        P.wait("vector", P.now("s"))
        for tc in range(NC):
            P.emit("vector", lambda h, tc=tc: h.tensor_tensor(
                psc[:, tc, 32:36], psc[:, tc, 24:28], psc[:, tc, 20:24], ALU.mult), "v")
            for hh in range(NH):
                P.emit("vector", lambda h, tc=tc, hh=hh: h.tensor_tensor(
                    psc[:, tc, 36 + hh:37 + hh], psc[:, tc, 28 + hh:29 + hh],
                    psc[:, tc, 5 * hh + 3:5 * hh + 4], ALU.mult), "v")
        t_scl = P.now("v")
        P.wait("tensor", t_scl)
        for tc in range(NC):
            for g in range(2):
                pbS.begin_write()
                for cc in range(4 * g, 4 * g + 4):
                    tk = P.emit("tensor", lambda h, cc=cc, tc=tc: h.transpose(
                        pbS.h[:, (cc % 4) * 128:(cc % 4) * 128 + 128],
                        ktok[:, tc, cc * 128:(cc + 1) * 128], eyeb[:, :]),
                        "t" if cc % 4 == 3 else None)
                pbS.end_write(tk)
                pbS.read_from("scalar")
                tr = P.emit("scalar", lambda h, g=g, tc=tc: h.activation(
                    flat(khatf[:, 4 * g:4 * g + 4, tc * 128:tc * 128 + 128]),
                    pbS.h[:, 0:512], AF.Copy), "s")
                pbS.add_reader(tr)
        t_phC = P.now("s")
        ckpt("phC" + str(half))

        for nm, buf in (("ktok", ktok), ("vtok", vtok)):
            if nm in tapd:
                P.wait("scalar", t_phC)
                for tc in range(NC):
                    P.emit("scalar", lambda h, nm=nm, tc=tc, buf=buf: h.dma_start(
                        out=tapd[nm].ap()[half, tc], in_=buf[:, tc, :]), "ds")
        if "khatf" in tapd:
            P.wait("scalar", t_phC)
            for cc in range(CH):
                P.emit("scalar", lambda h, cc=cc: h.dma_start(
                    out=tapd["khatf"].ap()[half, cc], in_=khatf[:, cc, :]), "ds")

        P.wait("vector", t_phC)
        for tc in range(NC):
            for hh in range(NH):
                P.emit("vector", lambda h, tc=tc, hh=hh: h.tensor_scalar(
                    ACC[:, tc, hh * 256:hh * 256 + 256],
                    vtok[:, tc, hh * 256:hh * 256 + 256],
                    psc[:, tc, 5 * hh + 4:5 * hh + 5], None, ALU.mult), "v")
        t_acc0 = P.now("v")

        t_smm = {}
        t_lmm = {}
        t_dasm = {}
        t_dal = {}

        def load_sm(cc):
            sb = 18 * (cc % 2)
            P.wait("gpsimd", t_smm.get(cc - 2))
            t_dasm[cc] = P.emit("gpsimd", lambda h, cc=cc, sb=sb: h.dma_start(
                out=da[:, sb:sb + 18, :],
                in_=diag_d.ap()[cc, 12:30, :, :].transpose([1, 0, 2])), "dg")

        def load_l(cc):
            P.wait("gpsimd", t_lmm.get(cc - 1))
            t_dal[cc] = P.emit("gpsimd", lambda h, cc=cc: h.dma_start(
                out=da[:, 36:99, :],
                in_=diag_d.ap()[cc, 30:93, :, :].transpose([1, 0, 2])), "dg")

        P.wait("gpsimd", P.now("t"))
        load_sm(0)
        load_l(0)
        load_sm(1)
        for cc in range(CH):
            smb = 18 * (cc % 2)
            hh = cc // 2
            fir_specs = ((smb + 0, 3, 0), (smb + 3, 15, 1), (36, 63, 2))
            for si, (lo, ntap, pcol) in enumerate(fir_specs):
                b0, b1 = pb[0], pb[1]
                b0.begin_write()
                b1.begin_write()
                P.wait("tensor", t_dal[cc] if si == 2 else t_dasm[cc])
                P.wait("tensor", t_vpad)
                for j in range(ntap):
                    off = PAD - (ntap - 1) + j
                    for tt in range(NT):
                        bk = b0 if tt % 2 == 0 else b1
                        tk = P.emit("tensor", lambda h, cc=cc, lo=lo, j=j, off=off,
                                    tt=tt, bk=bk, ntap=ntap: h.matmul(
                            bk.h[:, 0:512], da[:, lo + j, :],
                            vf[:, cc, off + tt * 512:off + tt * 512 + 512],
                            start=(j == 0), stop=(j == ntap - 1)),
                            "t" if j == ntap - 1 else None)
                        (b0 if tt % 2 == 0 else b1).end_write(tk)
                if si == 1:
                    t_smm[cc] = P.now("t")
                    if cc + 2 < CH:
                        load_sm(cc + 2)
                if si == 2:
                    t_lmm[cc] = P.now("t")
                    if cc + 1 < CH:
                        load_l(cc + 1)
                for tt in range(NT):
                    bk = b0 if tt % 2 == 0 else b1
                    bk.read_from("scalar")
                    tev = P.emit("scalar", lambda h, tt=tt, bk=bk: h.activation(
                        fsb[:, tt * 512:tt * 512 + 512], bk.h[:, 0:512], AF.Copy), "s")
                    bk.add_reader(tev)
                t_fs = P.now("s")
                P.wait("tensor", t_fs)
                P.wait("tensor", t_acc0)
                for tc in range(NC):
                    pbS.begin_write()
                    tk = P.emit("tensor", lambda h, tc=tc: h.transpose(
                        pbS.h[:, (tc % 4) * 128:(tc % 4) * 128 + 128],
                        fsb[:, tc * 128:tc * 128 + 128], eyeb[:, :]), "t")
                    pbS.end_write(tk)
                    pbS.read_from("vector")
                    tr = P.emit("vector", lambda h, tc=tc, cc=cc, hh=hh, pcol=pcol:
                                h.scalar_tensor_tensor(
                        ACC[:, tc, cc * 128:(cc + 1) * 128],
                        pbS.h[:, (tc % 4) * 128:(tc % 4) * 128 + 128],
                        psc[:, tc, 5 * hh + pcol:5 * hh + pcol + 1],
                        ACC[:, tc, cc * 128:(cc + 1) * 128], ALU.mult, ALU.add), "v")
                    pbS.add_reader(tr)
        t_fir = P.now("v")
        P.barrier()
        bx.close()
        ckpt("fir" + str(half))

        if "accmix" in tapd:
            P.wait("scalar", t_fir)
            for tc in range(NC):
                P.emit("scalar", lambda h, tc=tc: h.dma_start(
                    out=tapd["accmix"].ap()[half, tc], in_=ACC[:, tc, :]), "ds")
        if "psc" in tapd:
            P.wait("scalar", t_fir)
            P.wait("scalar", t_scl)
            for tc in range(NC):
                P.emit("scalar", lambda h, tc=tc: h.dma_start(
                    out=tapd["psc"].ap()[half, tc], in_=psc[:, tc, 0:40]), "ds")

        ex = ExitStack()
        sbe = lambda n, sh, dt: ex.enter_context(
            nc.sbuf_tensor(f"{n}_{half}", sh, dt, side="left"))
        dA = sbe("dA", [128, NH, 128], BF16)
        dC = sbe("dC", [128, NH, 128], BF16)
        dB = sbe("dB", [128, NH, 128], BF16)
        dQa = sbe("dQa", [128, NH, 128], BF16)
        dQb = sbe("dQb", [128, NH, 128], BF16)
        du = sbe("du", [128, NH, 256], BF16)
        dua = sbe("dua", [128, NH, 256], BF16)
        datn = sbe("datn", [128, 2, NH, 128], BF16)
        dwf = sbe("dwf", [128, 2 * NH, 128], BF16)
        dvh = sbe("dvh", [128, NH, 256], BF16)
        dkb = sbe("dkb", [128, NH, 256], BF16)

        NLEV = 6
        CS = {"t_wf": None, "t_uwmm": None, "t_scan_prev": t_scan_prev}

        def pre_scales(tn):
            P.wait("vector", t_fir if tn == 0 else P.now("v"))
            if CS["t_uwmm"] is not None:
                P.wait("vector", CS["t_uwmm"])
            for hh in range(NH):
                P.emit("vector", lambda h, tn=tn, hh=hh: h.tensor_scalar(
                    dvh[:, hh, :], vtok[:, tn, hh * 256:hh * 256 + 256],
                    psc[:, tn, 20 + hh:21 + hh], None, ALU.mult), "v")
                P.emit("vector", lambda h, tn=tn, hh=hh: h.tensor_scalar(
                    dkb[:, hh, :], ktok[:, tn, hh * 256:hh * 256 + 256],
                    psc[:, tn, 20 + hh:21 + hh], None, ALU.mult), "v")
            CS["t_sc"] = P.now("v")

        def pre_Gmm(tn):
            tr0 = PAD + tn * 128
            pb7.begin_write()
            P.wait("tensor", CS["t_sc"])
            for hh in range(NH):
                for kc in range(2):
                    tk = P.emit("tensor", lambda h, hh=hh, kc=kc, tn=tn, tr0=tr0:
                                h.matmul(
                        pb7.h[:, hh * 128:hh * 128 + 128],
                        kf[:, 2 * hh + kc, tr0:tr0 + 128],
                        khatf[:, 2 * hh + kc, tn * 128:tn * 128 + 128],
                        start=(kc == 0), stop=(kc == 1)),
                        "t" if hh == NH - 1 and kc == 1 else None)
            pb7.end_write(tk)

        def pre_Aevac(tn):
            pb7.read_from("vector")
            for hh in range(NH):
                tr = P.emit("vector", lambda h, hh=hh, tn=tn: h.scalar_tensor_tensor(
                    dA[:, hh, :], pb7.h[:, hh * 128:hh * 128 + 128],
                    psc[:, tn, 32 + hh:33 + hh], negm[:, :], ALU.mult, ALU.mult), "v")
            pb7.add_reader(tr)
            CS["t_A"] = P.now("v")

        def pre_attnT(tn):
            tr0 = PAD + tn * 128
            pp = tn % 2
            pb[2].begin_write()
            P.wait("tensor", CS["t_A"])
            for hh in range(NH):
                for kc in range(2):
                    tk = P.emit("tensor", lambda h, hh=hh, kc=kc, tn=tn, tr0=tr0:
                                h.matmul(
                        pb[2].h[:, hh * 128:hh * 128 + 128],
                        khatf[:, 2 * hh + kc, tn * 128:tn * 128 + 128],
                        qf[:, 2 * hh + kc, tr0:tr0 + 128],
                        start=(kc == 0), stop=(kc == 1)),
                        "t" if hh == NH - 1 and kc == 1 else None)
            pb[2].end_write(tk)
            pb[2].read_from("vector")
            t_atn = P.emit("vector", lambda h, pp=pp: h.tensor_tensor(
                datn[:, pp, :, :],
                pb[2].h[:, 0:512].rearrange("p (a b) -> p a b", a=NH),
                cmT[:, :].unsqueeze(1).broadcast_to([128, NH, 128]), ALU.mult), "v")
            pb[2].add_reader(t_atn)

        grpbank = (pb7, pb[3])
        trbank = (pbT, pb[2])
        trview = (pbT.h[:, 0:256], pb[2].h[:, :].bitcast(BF16)[:, 0:256])

        def pre_C0(tn):
            CS["t_c"] = [None, None]
            CS["t_b"] = [None, None]
            CS["t_q"] = [None, None]
            CS["Qcur"], CS["Qnxt"] = dQa, dQb
            for g in range(2):
                bk = trbank[g]
                bk.begin_write()
                P.wait("tensor", CS["t_A"])
                for j in range(2):
                    hh = 2 * g + j
                    tk = P.emit("tensor", lambda h, hh=hh, j=j, g=g: h.transpose(
                        trview[g][:, j * 128:j * 128 + 128], dA[:, hh, :],
                        eyeb[:, :]), "t" if j == 1 else None)
                bk.end_write(tk)
            for g in range(2):
                bk = trbank[g]
                bk.read_from("scalar")
                CS["t_c"][g] = P.emit("scalar", lambda h, g=g: h.activation(
                    flat(dC[:, 2 * g:2 * g + 2, :]), trview[g][:, 0:256],
                    AF.Copy), "s")
                bk.add_reader(CS["t_c"][g])
            for g in range(2):
                P.wait("vector", CS["t_c"][g])
                CS["t_q"][g] = P.emit("vector", lambda h, g=g, Q=CS["Qcur"]:
                                      h.tensor_tensor(
                    Q[:, 2 * g:2 * g + 2, :], dC[:, 2 * g:2 * g + 2, :],
                    eyeb[:, :].unsqueeze(1).broadcast_to([128, 2, 128]),
                    ALU.add), "v")

        def ch_CMM(lev):
            for g in range(2):
                bk = grpbank[g]
                bk.begin_write()
                P.wait("tensor", CS["t_b"][g] if lev > 0 else CS["t_A"])
                P.wait("tensor", CS["t_c"][g])
                for j in range(2):
                    hh = 2 * g + j
                    src = dB if lev > 0 else dA
                    tk = P.emit("tensor", lambda h, hh=hh, j=j, g=g, src=src:
                                h.matmul(
                        grpbank[g].h[:, j * 128:j * 128 + 128], src[:, hh, :],
                        dC[:, hh, :], start=True, stop=True),
                        "t" if j == 1 else None)
                bk.end_write(tk)
            for g in range(2):
                bk = grpbank[g]
                bk.read_from("scalar")
                CS["t_c"][g] = P.emit("scalar", lambda h, g=g: h.activation(
                    flat(dC[:, 2 * g:2 * g + 2, :]),
                    grpbank[g].h[:, 0:256], AF.Copy), "s")
                bk.add_reader(CS["t_c"][g])

        def ch_TR(lev):
            for g in range(2):
                bk = trbank[g]
                bk.begin_write()
                P.wait("tensor", CS["t_c"][g])
                for j in range(2):
                    hh = 2 * g + j
                    tk = P.emit("tensor", lambda h, hh=hh, j=j, g=g: h.transpose(
                        trview[g][:, j * 128:j * 128 + 128], dC[:, hh, :],
                        eyeb[:, :]), "t" if j == 1 else None)
                bk.end_write(tk)
            for g in range(2):
                bk = trbank[g]
                bk.read_from("scalar")
                CS["t_b"][g] = P.emit("scalar", lambda h, g=g: h.activation(
                    flat(dB[:, 2 * g:2 * g + 2, :]), trview[g][:, 0:256],
                    AF.Copy), "s")
                bk.add_reader(CS["t_b"][g])

        def ch_QMM(lev):
            Qcur = CS["Qcur"]
            for g in range(2):
                bk = grpbank[g]
                bk.begin_write()
                P.wait("tensor", CS["t_b"][g])
                P.wait("tensor", CS["t_q"][g])
                for j in range(2):
                    hh = 2 * g + j
                    P.emit("tensor", lambda h, hh=hh, j=j, g=g, Q=Qcur: h.matmul(
                        grpbank[g].h[:, j * 128:j * 128 + 128], dB[:, hh, :],
                        Q[:, hh, :], start=True, stop=False), None)
                    tk = P.emit("tensor", lambda h, hh=hh, j=j, g=g, Q=Qcur:
                                h.matmul(
                        grpbank[g].h[:, j * 128:j * 128 + 128], eyeb[:, :],
                        Q[:, hh, :], start=False, stop=True),
                        "t" if j == 1 else None)
                bk.end_write(tk)
            for g in range(2):
                bk = grpbank[g]
                bk.read_from("scalar")
                CS["t_q"][g] = P.emit("scalar", lambda h, g=g, Q=CS["Qnxt"]:
                                      h.activation(
                    flat(Q[:, 2 * g:2 * g + 2, :]),
                    grpbank[g].h[:, 0:256], AF.Copy), "s")
                bk.add_reader(CS["t_q"][g])
            CS["Qcur"], CS["Qnxt"] = CS["Qnxt"], CS["Qcur"]

        def pre_uw(tn):
            dTT = CS["Qcur"]
            P.wait("tensor", CS["t_q"][0])
            P.wait("tensor", CS["t_q"][1])
            pb7.begin_write()
            pb[3].begin_write()
            for i in range(8):
                hh, dv = divmod(i, 2)
                bk = pb7 if i < 4 else pb[3]
                tk = P.emit("tensor", lambda h, hh=hh, dv=dv, bk=bk, i=i, dTT=dTT:
                            h.matmul(
                    bk.h[:, (i % 4) * 128:(i % 4) * 128 + 128],
                    dTT[:, hh, :], dvh[:, hh, dv * 128:dv * 128 + 128],
                    start=True, stop=True), "t" if i in (3, 7) else None)
                if i == 3:
                    pb7.end_write(tk)
            pb[3].end_write(tk)
            pb7.read_from("scalar")
            tu1 = P.emit("scalar", lambda h: h.activation(
                flat(du[:, 0:2, :]), pb7.h[:, 0:512], AF.Copy), "s")
            pb7.add_reader(tu1)
            pb[3].read_from("scalar")
            tu2 = P.emit("scalar", lambda h: h.activation(
                flat(du[:, 2:4, :]), pb[3].h[:, 0:512], AF.Copy), "s")
            pb[3].add_reader(tu2)

            pb[2].begin_write()
            pb7.begin_write()
            for i in range(8):
                hh, kc = divmod(i, 2)
                bk = pb[2] if i < 4 else pb7
                tk = P.emit("tensor", lambda h, hh=hh, kc=kc, bk=bk, i=i, dTT=dTT:
                            h.matmul(
                    bk.h[:, (i % 4) * 128:(i % 4) * 128 + 128],
                    dkb[:, hh, kc * 128:kc * 128 + 128], dTT[:, hh, :],
                    start=True, stop=True), "t" if i in (3, 7) else None)
                if i == 3:
                    pb[2].end_write(tk)
            pb7.end_write(tk)
            CS["t_uwmm"] = tk
            pb[2].read_from("scalar")
            tw1 = P.emit("scalar", lambda h: h.activation(
                flat(dwf[:, 0:4, :]), pb[2].h[:, 0:512], AF.Copy), "s")
            pb[2].add_reader(tw1)
            pb7.read_from("scalar")
            tw2 = P.emit("scalar", lambda h: h.activation(
                flat(dwf[:, 4:8, :]), pb7.h[:, 0:512], AF.Copy), "s")
            pb7.add_reader(tw2)
            CS["t_wf"] = P.now("s")
            CS["dTT"] = dTT

        def scan_wS(tc):
            pb[0].begin_write()
            pb[1].begin_write()
            P.wait("tensor", CS["t_wf"])
            P.wait("tensor", CS["t_scan_prev"])
            for hh in range(NH):
                bk = pb[0] if hh < 2 else pb[1]
                for kc in range(2):
                    tk = P.emit("tensor", lambda h, hh=hh, kc=kc, bk=bk: h.matmul(
                        bk.h[:, (hh % 2) * 256:(hh % 2) * 256 + 256],
                        dwf[:, 2 * hh + kc, :], Sbf[:, 2 * hh + kc, :],
                        start=(kc == 0), stop=(kc == 1)),
                        "t" if kc == 1 and hh in (1, 3) else None)
                if hh == 1:
                    pb[0].end_write(tk)
            pb[1].end_write(tk)

        def scan_uadj(tc):
            pb[0].read_from("vector")
            ta1 = P.emit("vector", lambda h: h.tensor_tensor(
                flat(dua[:, 0:2, :]), flat(du[:, 0:2, :]), pb[0].h[:, 0:512],
                ALU.subtract), "v")
            pb[0].add_reader(ta1)
            pb[1].read_from("vector")
            ta2 = P.emit("vector", lambda h: h.tensor_tensor(
                flat(dua[:, 2:4, :]), flat(du[:, 2:4, :]), pb[1].h[:, 0:512],
                ALU.subtract), "v")
            pb[1].add_reader(ta2)
            CS["t_ua"] = P.now("v")

        def scan_oMM(tc):
            tr0 = PAD + tc * 128
            pp = tc % 2
            pb[0].begin_write()
            pb[1].begin_write()
            P.wait("tensor", CS["t_ua"])
            for hh in range(NH):
                bk = pb[0] if hh < 2 else pb[1]
                for kc in range(2):
                    P.emit("tensor", lambda h, hh=hh, kc=kc, bk=bk, tc=tc, tr0=tr0:
                           h.matmul(
                        bk.h[:, (hh % 2) * 256:(hh % 2) * 256 + 256],
                        qf[:, 2 * hh + kc, tr0:tr0 + 128], Sbf[:, 2 * hh + kc, :],
                        start=(kc == 0), stop=False), None)
                tk = P.emit("tensor", lambda h, hh=hh, bk=bk, pp=pp: h.matmul(
                    bk.h[:, (hh % 2) * 256:(hh % 2) * 256 + 256],
                    datn[:, pp, hh, :], dua[:, hh, :], start=False, stop=True),
                    "t" if hh in (1, 3) else None)
                if hh == 1:
                    pb[0].end_write(tk)
            pb[1].end_write(tk)

        def scan_oevac(tc):
            for hh in range(NH):
                bk = pb[0] if hh < 2 else pb[1]
                bk.read_from("vector")
                tr = P.emit("vector", lambda h, hh=hh, bk=bk, tc=tc:
                            h.scalar_tensor_tensor(
                    ACC[:, tc, hh * 256:hh * 256 + 256],
                    bk.h[:, (hh % 2) * 256:(hh % 2) * 256 + 256],
                    psc[:, tc, 36 + hh:37 + hh],
                    ACC[:, tc, hh * 256:hh * 256 + 256], ALU.mult, ALU.add), "v")
                bk.add_reader(tr)

        def scan_dS(tc, rnd):
            h0, h1 = 2 * rnd, 2 * rnd + 1
            pb[4].begin_write()
            pb[5].begin_write()
            P.wait("tensor", CS["t_ua"])
            for hh, bk in ((h0, pb[4]), (h1, pb[5])):
                for kc in range(2):
                    tk = P.emit("tensor", lambda h, hh=hh, kc=kc, bk=bk, tc=tc:
                                h.matmul(
                        bk.h[:, kc * 256:kc * 256 + 256],
                        ktok[:, tc, hh * 256 + kc * 128:hh * 256 + kc * 128 + 128],
                        dua[:, hh, :], start=True, stop=True),
                        "t" if kc == 1 else None)
                if hh == h0:
                    pb[4].end_write(tk)
            pb[5].end_write(tk)
            for hh, bk in ((h0, pb[4]), (h1, pb[5])):
                bk.read_from("vector")
                ts1 = P.emit("vector", lambda h, hh=hh, bk=bk: h.tensor_tensor(
                    flat(S32[:, 2 * hh:2 * hh + 2, :]),
                    flat(S32[:, 2 * hh:2 * hh + 2, :]),
                    bk.h[:, 0:512], ALU.add), "v")
                bk.add_reader(ts1)
                P.wait("scalar", ts1)
                CS["t_scan_prev"] = P.emit("scalar", lambda h, hh=hh: h.activation(
                    flat(Sbf[:, 2 * hh:2 * hh + 2, :]),
                    flat(S32[:, 2 * hh:2 * hh + 2, :]), AF.Copy), "s")

        pre_scales(0)
        pre_Gmm(0)
        pre_Aevac(0)
        pre_attnT(0)
        pre_C0(0)
        for lev in range(NLEV):
            ch_CMM(lev)
            ch_TR(lev)
            ch_QMM(lev)
        pre_uw(0)

        if half == 0:
            for nm, buf in (("tt0", CS["dTT"]), ("u0", du),
                            ("wf0", dwf), ("atn0", datn[:, 0])):
                if nm in tapd:
                    P.wait("scalar", CS["t_wf"])
                    for a in range(NH):
                        P.emit("scalar", lambda h, nm=nm, a=a, buf=buf: h.dma_start(
                            out=tapd[nm].ap()[a], in_=buf[:, a, :]), "ds")

        for tc in range(NC):
            tn = tc + 1
            have = tn < NC
            if have:
                pre_scales(tn)
            scan_wS(tc)
            if have:
                pre_Gmm(tn)
            scan_uadj(tc)
            if have:
                pre_Aevac(tn)
                pre_attnT(tn)
                pre_C0(tn)
                ch_CMM(0)
            scan_oMM(tc)
            if have:
                ch_TR(0)
            scan_oevac(tc)
            scan_dS(tc, 0)
            if have:
                ch_QMM(0)
            scan_dS(tc, 1)
            if have:
                for lev in range(1, NLEV):
                    ch_CMM(lev)
                    ch_TR(lev)
                    ch_QMM(lev)
                pre_uw(tn)

            ckpt("delta" + str(half) + "_" + str(tc))
        t_scan_prev = CS["t_scan_prev"]
        P.barrier()
        ex.close()
        qx.close()
        ckpt("delta" + str(half))

        if "accfin" in tapd:
            P.wait("scalar", P.now("v"))
            for tc in range(NC):
                P.emit("scalar", lambda h, tc=tc: h.dma_start(
                    out=tapd["accfin"].ap()[half, tc], in_=ACC[:, tc, :]), "ds")

        fx = ExitStack()
        sbf_ = lambda n, sh, dt: fx.enter_context(
            nc.sbuf_tensor(f"{n}_{half}", sh, dt, side="left"))
        ot = sbf_("ot", [128, D], BF16)
        of = sbf_("of", [128, CH, TH], BF16)
        wof = sbf_("wof", [128, CH, TH], F32)
        xr = sbf_("xr", [128, D], F32)
        xnt = sbf_("xnt", [128, D], F32)
        sqs = sbf_("sqs", [128, D], BF16)

        t_delta = P.now("v")
        t_ot_free = None
        for tc in range(NC):
            P.wait("vector", t_delta)
            for hh in range(NH):
                P.emit("vector", lambda h, tc=tc, hh=hh: h.tensor_tensor(
                    scrap[:, 256 * hh:256 * hh + 256],
                    ACC[:, tc, hh * 256:hh * 256 + 256],
                    ACC[:, tc, hh * 256:hh * 256 + 256], ALU.mult), "v")
            for hh in range(NH):
                P.emit("vector", lambda h, tc=tc, hh=hh: h.tensor_reduce(
                    psc[:, tc, 40 + hh:41 + hh], scrap[:, 256 * hh:256 * hh + 256],
                    AXX, ALU.add), "v")
            P.wait("scalar", P.now("v"))
            t_r = P.emit("scalar", lambda h, tc=tc: h.activation(
                psc[:, tc, 40:44], psc[:, tc, 40:44], AF.Sqrt,
                scale=1.0 / 256, bias=ecst[:, 0:1]), "s")
            P.wait("vector", t_r)
            P.emit("vector", lambda h, tc=tc: h.reciprocal(
                psc[:, tc, 40:44], psc[:, tc, 40:44]), "v")
            P.vgap()
            P.wait("vector", t_ot_free)
            for hh in range(NH):
                P.emit("vector", lambda h, tc=tc, hh=hh: h.tensor_scalar(
                    ot[:, hh * 256:hh * 256 + 256], ACC[:, tc, hh * 256:hh * 256 + 256],
                    psc[:, tc, 40 + hh:41 + hh], None, ALU.mult), "v")
            t_otk = P.now("v")
            P.wait("tensor", t_otk)
            pbT.begin_write()
            for cc in range(CH):
                tk = P.emit("tensor", lambda h, cc=cc: h.transpose(
                    pbT.h[:, cc * 128:(cc + 1) * 128], ot[:, cc * 128:(cc + 1) * 128],
                    eyeb[:, :]), "t" if cc == CH - 1 else None)
            pbT.end_write(tk)
            t_ot_free = tk
            pbT.read_from("scalar")
            tev = P.emit("scalar", lambda h, tc=tc: h.activation(
                flat(of[:, :, tc * 128:tc * 128 + 128]), pbT.h[:, :], AF.Copy), "s")
            pbT.add_reader(tev)
        t_of = P.now("s")
        ckpt("of" + str(half))

        if "of" in tapd:
            P.wait("scalar", t_of)
            for cc in range(CH):
                P.emit("scalar", lambda h, cc=cc: h.dma_start(
                    out=tapd["of"].ap()[half, cc], in_=of[:, cc, :]), "ds")

        P.wait("tensor", t_of)
        P.wait("tensor", t_wofold if half == 0 else t_of)
        for tt in range(NT):
            for mc in range(CH):
                bk = pb[mc % 2]
                bk.begin_write()
                for kc in range(CH):
                    tk = P.emit("tensor", lambda h, mc=mc, kc=kc, tt=tt: h.matmul(
                        bk.h[:, 0:512], wo[:, kc, mc * 128:(mc + 1) * 128],
                        of[:, kc, tt * 512:tt * 512 + 512],
                        start=(kc == 0), stop=(kc == CH - 1)),
                        "t" if kc == CH - 1 else None)
                bk.end_write(tk)
                bk.read_from("scalar")
                tev = P.emit("scalar", lambda h, mc=mc, tt=tt: h.activation(
                    wof[:, mc, tt * 512:tt * 512 + 512], bk.h[:, 0:512], AF.Copy), "s")
                bk.add_reader(tev)
        t_wof = P.now("s")
        ckpt("wof" + str(half))

        for tc in range(NC):
            t0 = half * TH + tc * 128
            t_x = P.emit("sync", lambda h, t0=t0: h.dma_start(
                out=xr[:, :], in_=x_d.ap()[t0:t0 + 128, :]), "dl")
            pb[4].begin_write()
            pb[5].begin_write()
            P.wait("tensor", t_wof)
            for cc in range(CH):
                bk = pb[4] if cc < 4 else pb[5]
                tk = P.emit("tensor", lambda h, cc=cc, tc=tc, bk=bk: h.transpose(
                    bk.h[:, (cc % 4) * 128:(cc % 4) * 128 + 128],
                    wof[:, cc, tc * 128:tc * 128 + 128], eyef[:, :]),
                    "t" if cc in (3, 7) else None)
                if cc == 3:
                    pb[4].end_write(tk)
            pb[5].end_write(tk)
            P.wait("vector", t_x)
            for g, bk in ((0, pb[4]), (1, pb[5])):
                bk.read_from("vector")
                tr = P.emit("vector", lambda h, g=g, bk=bk: h.tensor_tensor(
                    xnt[:, g * 512:g * 512 + 512], xr[:, g * 512:g * 512 + 512],
                    bk.h[:, 0:512], ALU.add), "v")
                bk.add_reader(tr)
            t_xn = P.now("v")
            P.wait("scalar", t_xn)
            ci = half * NC + tc
            P.emit("scalar", lambda h, ci=ci: h.activation(
                sqs[:, :], xnt[:, :], AF.Copy,
                accum_out=stat[:, 48 + ci:49 + ci]), "s")
            P.emit("scalar", lambda h, ci=ci: h.activation(
                sqs[:, :], xnt[:, :], AF.Square,
                accum_out=stat[:, 64 + ci:65 + ci]), "s")
            t_st = P.emit("scalar", lambda h, t0=t0: h.dma_start(
                out=xnew_d.ap()[t0:t0 + 128, :], in_=xnt[:, :]), "ds")
            if "xnew" in tapd:
                P.emit("scalar", lambda h, t0=t0: h.dma_start(
                    out=tapd["xnew"].ap()[t0:t0 + 128, :], in_=xnt[:, :]), "ds")
            P.wait("sync", t_st)
            P.wait("vector", t_st)
        P.barrier()
        fx.close()
        hs.close()
        ckpt("resid" + str(half))

    P.barrier()
    att.close()
    ckpt("attn")

    fz = ExitStack()
    sbf2 = lambda n, sh, dt: fz.enter_context(
        nc.sbuf_tensor(n, sh, dt, side="left"))
    wgs = sbf2("wgs", [128, CH, HID], BF16)
    wus = sbf2("wus", [128, CH, HID], BF16)
    wds = sbf2("wds", [128, HC, D], BF16)
    hbuf = sbf2("hbuf", [128, HC, FT], BF16)
    ln2f = sbf2("ln2f", [128, CH, FT], BF16)
    ga = sbf2("ga", [128, FT], BF16)
    lnt2 = sbf2("lnt2", [128, 2, D], BF16)
    xr2 = sbf2("xr2", [128, 2, D], F32)
    dwn = sbf2("dwn", [128, CH, FT], F32)
    ont = sbf2("ont", [128, D], F32)

    t_wgc = {}
    t_wuc = {}
    t_wdc = {}
    for kc in range(CH):
        t_wgc[kc] = P.emit("gpsimd", lambda h, kc=kc: h.dma_start(
            out=wgs[:, kc, :], in_=wg_d.ap()[kc]), "dg")
        t_wuc[kc] = P.emit("gpsimd", lambda h, kc=kc: h.dma_start(
            out=wus[:, kc, :], in_=wu_d.ap()[kc]), "dg")
    for kc in range(HC):
        t_wdc[kc] = P.emit("gpsimd", lambda h, kc=kc: h.dma_start(
            out=wds[:, kc, :], in_=wd_d.ap()[kc]), "dg")

    NCH2 = L // 128
    P.wait("vector", P.now("s"))
    P.emit("vector", lambda h: h.tensor_scalar(
        stat[:, 48:48 + NCH2], stat[:, 48:48 + NCH2], 1.0 / D, None,
        ALU.mult), "v")
    P.emit("vector", lambda h: h.tensor_scalar(
        stat[:, 64:64 + NCH2], stat[:, 64:64 + NCH2], 1.0 / D, None,
        ALU.mult), "v")
    P.emit("vector", lambda h: h.tensor_tensor(
        stat[:, 80:80 + NCH2], stat[:, 48:48 + NCH2], stat[:, 48:48 + NCH2],
        ALU.mult), "v")
    P.vgap()
    P.emit("vector", lambda h: h.tensor_tensor(
        stat[:, 80:80 + NCH2], stat[:, 64:64 + NCH2], stat[:, 80:80 + NCH2],
        ALU.subtract), "v")
    P.wait("scalar", P.now("v"))
    t_rs2 = P.emit("scalar", lambda h: h.activation(
        stat[:, 80:80 + NCH2], stat[:, 80:80 + NCH2], AF.Sqrt,
        bias=ecst[:, 0:1]), "s")
    P.wait("vector", t_rs2)
    P.emit("vector", lambda h: h.reciprocal(
        stat[:, 64:64 + NCH2], stat[:, 80:80 + NCH2]), "v")
    P.vgap()
    t_chain2 = P.now("v")

    t_lnt2_free = [None, None]
    t_xr2_free = None
    t_tile_prev = None
    for ft in range(NFT):
        f0 = ft * FT
        P.wait("sync", t_xr2_free)
        P.wait("scalar", t_tile_prev)
        P.wait("vector", t_tile_prev)
        for tj in range(2):
            t0 = f0 + tj * 128
            ci = 2 * ft + tj
            t_x = P.emit("sync", lambda h, t0=t0, tj=tj: h.dma_start(
                out=xr2[:, tj, :], in_=xnew_d.ap()[t0:t0 + 128, :]), "dl")
            P.wait("vector", t_x)
            P.wait("vector", t_chain2 if ft == 0 and tj == 0 else None)
            P.wait("vector", t_lnt2_free[ci % 2])
            t_ln = P.emit("vector", lambda h, tj=tj, ci=ci: h.tensor_scalar(
                lnt2[:, ci % 2, :], xr2[:, tj, :], stat[:, 48 + ci:49 + ci],
                stat[:, 64 + ci:65 + ci], ALU.subtract, ALU.mult), "v")
            P.wait("tensor", t_ln)
            pbT.begin_write()
            for cc in range(CH):
                tk = P.emit("tensor", lambda h, cc=cc, ci=ci: h.transpose(
                    pbT.h[:, cc * 128:(cc + 1) * 128],
                    lnt2[:, ci % 2, cc * 128:(cc + 1) * 128], eyeb[:, :]),
                    "t" if cc == CH - 1 else None)
            pbT.end_write(tk)
            t_lnt2_free[ci % 2] = tk
            pbT.read_from("scalar")
            tev = P.emit("scalar", lambda h, tj=tj: h.activation(
                flat(ln2f[:, :, tj * 128:tj * 128 + 128]), pbT.h[:, :], AF.Copy), "s")
            pbT.add_reader(tev)
        t_ln2 = P.now("s")
        if "ln2f" in tapd:
            P.wait("scalar", t_ln2)
            for cc in range(CH):
                P.emit("scalar", lambda h, cc=cc, ft=ft: h.dma_start(
                    out=tapd["ln2f"].ap()[ft, cc], in_=ln2f[:, cc, :]), "ds")

        P.wait("tensor", t_ln2)
        for mc in range(HC):
            bg_ = pb[(mc % 2) * 2]
            bu_ = pb[(mc % 2) * 2 + 1]
            bg_.begin_write()
            bu_.begin_write()
            for kc in range(CH):
                P.wait("tensor", t_wgc[kc])
                P.wait("tensor", t_wuc[kc])
                tkg = P.emit("tensor", lambda h, mc=mc, kc=kc, bg_=bg_: h.matmul(
                    bg_.h[:, 0:FT], wgs[:, kc, mc * 128:(mc + 1) * 128],
                    ln2f[:, kc, :], start=(kc == 0), stop=(kc == CH - 1)),
                    "t" if kc == CH - 1 else None)
                tk = P.emit("tensor", lambda h, mc=mc, kc=kc, bu_=bu_: h.matmul(
                    bu_.h[:, 0:FT], wus[:, kc, mc * 128:(mc + 1) * 128],
                    ln2f[:, kc, :], start=(kc == 0), stop=(kc == CH - 1)),
                    "t" if kc == CH - 1 else None)
            bg_.end_write(tkg)
            bu_.end_write(tk)
            bg_.read_from("scalar")
            t_g = P.emit("scalar", lambda h, bg_=bg_: h.activation(
                ga[:, :], bg_.h[:, 0:FT], AF.Silu, scale=QS), "s")
            bg_.add_reader(t_g)
            bu_.read_from("vector")
            P.wait("vector", t_g)
            t_h = P.emit("vector", lambda h, mc=mc, bu_=bu_: h.scalar_tensor_tensor(
                hbuf[:, mc, :], bu_.h[:, 0:FT], QS, ga[:, :], ALU.mult, ALU.mult), "v")
            bu_.add_reader(t_h)
            P.wait("scalar", t_h)
        t_hb = P.now("v")
        ckpt("gup" + str(ft))
        if "hbuf" in tapd:
            P.wait("scalar", t_hb)
            for mc in range(HC):
                P.emit("scalar", lambda h, mc=mc, ft=ft: h.dma_start(
                    out=tapd["hbuf"].ap()[ft, mc], in_=hbuf[:, mc, :]), "ds")

        P.wait("tensor", t_hb)
        for kc in range(HC):
            P.wait("tensor", t_wdc[kc])
        for mc in range(CH):
            bk = pb[2 + mc % 4]
            bk.begin_write()
            for kc in range(HC):
                tk = P.emit("tensor", lambda h, kc=kc, mc=mc, bk=bk: h.matmul(
                    bk.h[:, 0:FT], wds[:, kc, mc * 128:(mc + 1) * 128],
                    hbuf[:, kc, :], start=(kc == 0), stop=(kc == HC - 1)),
                    "t" if kc == HC - 1 else None)
            bk.end_write(tk)
            bk.read_from("scalar")
            tev = P.emit("scalar", lambda h, mc=mc, bk=bk: h.activation(
                dwn[:, mc, :], bk.h[:, 0:FT], AF.Copy), "s")
            bk.add_reader(tev)
        t_dwn = P.now("s")
        ckpt("dwn" + str(ft))

        for tj in range(2):
            t0 = f0 + tj * 128
            pb[0].begin_write()
            pb[1].begin_write()
            P.wait("tensor", t_dwn)
            for cc in range(CH):
                bk = pb[0] if cc < 4 else pb[1]
                tk = P.emit("tensor", lambda h, cc=cc, tj=tj, bk=bk: h.transpose(
                    bk.h[:, (cc % 4) * 128:(cc % 4) * 128 + 128],
                    dwn[:, cc, tj * 128:tj * 128 + 128], eyef[:, :]),
                    "t" if cc in (3, 7) else None)
                if cc == 3:
                    pb[0].end_write(tk)
            pb[1].end_write(tk)
            for g, bk in ((0, pb[0]), (1, pb[1])):
                bk.read_from("vector")
                tr = P.emit("vector", lambda h, g=g, tj=tj, bk=bk:
                            h.scalar_tensor_tensor(
                    ont[:, g * 512:g * 512 + 512], bk.h[:, 0:512], QS,
                    xr2[:, tj, g * 512:g * 512 + 512], ALU.mult, ALU.add), "v")
                bk.add_reader(tr)
            t_o = P.now("v")
            P.wait("scalar", t_o)
            t_ost = P.emit("scalar", lambda h, t0=t0: h.dma_start(
                out=out_d.ap()[t0:t0 + 128, :], in_=ont[:, :]), "ds")
            P.wait("vector", t_ost)
        t_xr2_free = t_o
        t_tile_prev = P.now("t")

    P.wait_queue("sync", "ds")
    P.wait_queue("sync", "dl")
    P.wait_queue("sync", "dg")

    assemble(nc, P)
    return nc, tapd


def assemble(nc, P):
    import bisect as _b
    keep = P.finalize_maps()
    sems = P.sems

    def runner(eng_name):
        def run(h):
            for rec in P.recs[eng_name]:
                kind = rec[0]
                if kind == "dma":
                    _, fn, key = rec
                    fn(h).then_inc(sems[key], 16)
                elif kind == "op":
                    _, fn, inc, pval = rec
                    ins = fn(h)
                    if inc is not None and pval in keep[inc][1]:
                        ins.then_inc(sems[inc], 1)
                else:
                    _, sem, val = rec
                    if sem in keep:
                        kept, m = keep[sem]
                        i = _b.bisect_left(kept, val)
                        if i >= len(kept):
                            i = len(kept) - 1
                        h.wait_ge(sems[sem], m[kept[i]])
                    else:
                        h.wait_ge(sems[sem], val)
        return run

    with nc.Block() as block:
        block.sync(runner("sync"))
        block.gpsimd(runner("gpsimd"))
        block.vector(runner("vector"))
        block.scalar(runner("scalar"))
        block.tensor(runner("tensor"))


def host_prep(inputs, L=2048):
    f32 = np.float32
    Wq = inputs["L0_attn_Wq"]
    Wk = inputs["L0_attn_Wk"]
    Wv = inputs["L0_attn_Wv"]
    Wo = inputs["L0_attn_Wo"]
    Wb = inputs["L0_attn_Wb"]
    gw = inputs["L0_attn_gate_w"].reshape(20, D)
    gb = inputs["L0_attn_gate_b"].reshape(20)
    conv_q = inputs["L0_attn_conv_q"]
    conv_k = inputs["L0_attn_conv_k"]
    conv_v = inputs["L0_attn_conv_v"]
    fir_s = inputs["L0_attn_fir_s"].reshape(D, 3)
    fir_m = inputs["L0_attn_fir_m"].reshape(D, 15)
    fir_l = inputs["L0_attn_fir_l"].reshape(D, 63)

    def wt(w):
        return np.ascontiguousarray(np.asarray(w, f32).T).reshape(CH, 128, D)

    gbw = np.concatenate([np.asarray(gw, f32), np.asarray(Wb, f32)], axis=0)
    wgbT = np.ascontiguousarray(gbw.T).reshape(CH, 128, 24)

    diag = np.zeros((CH, NTAP, 128, 128), dtype=f32)
    idx = np.arange(128)
    filts = [np.asarray(conv_q, f32), np.asarray(conv_k, f32),
             np.asarray(conv_v, f32), np.asarray(fir_s, f32),
             np.asarray(fir_m, f32), np.asarray(fir_l, f32)]
    for cc in range(CH):
        t = 0
        for fm in filts:
            K = fm.shape[1]
            for j in range(K):
                diag[cc, t, idx, idx] = fm[cc * 128:(cc + 1) * 128, j]
                t += 1
    assert t == NTAP

    m = {
        "wqT": wt(Wq), "wkT": wt(Wk), "wvT": wt(Wv), "woT": wt(Wo),
        "wgbT": wgbT, "diag_all": diag,
        "gb_exp": np.ascontiguousarray(np.broadcast_to(gb, (128, 20))).astype(f32),
        "log_temp": np.asarray(inputs["L0_attn_log_temp"], f32).reshape(4, 1),
        "eps_param": np.asarray(inputs["L0_attn_eps_param"], f32).reshape(4, 1),
        "nw": np.ascontiguousarray(
            np.asarray(inputs["L0_attn_norm_w"], f32).reshape(2, 128).T),
        "wgT": np.ascontiguousarray(
            np.asarray(inputs["L0_ffn_w_gate"]).T).reshape(CH, 128, HID),
        "wuT": np.ascontiguousarray(
            np.asarray(inputs["L0_ffn_w_up"]).T).reshape(CH, 128, HID),
        "wdT": np.ascontiguousarray(
            np.asarray(inputs["L0_ffn_w_down"]).T).reshape(HC, 128, D),
    }
    return m


def kernel(**inputs):
    x = np.asarray(inputs["L0_attn_hidden_states"], np.float32)
    B, L, _ = x.shape
    shared = host_prep(inputs, L)
    nc, _ = build(L)
    in_maps = []
    for b in range(B):
        im = dict(shared)
        im["x"] = np.ascontiguousarray(x[b])
        in_maps.append(im)
    res = run_bass_kernel_spmd(nc, in_maps, core_ids=list(range(B)))
    out = np.stack([np.asarray(res.results[b]["out"]) for b in range(B)], axis=0)
    return out.astype(np.float32)


# version 06
# speedup vs baseline: 1.2339x; 1.0310x over previous
from contextlib import ExitStack

import os
import numpy as np
import ml_dtypes

import concourse.bass as bass
import concourse.mybir as mybir
from concourse.bass_utils import run_bass_kernel_spmd

F32 = mybir.dt.float32
BF16 = mybir.dt.bfloat16
I32 = mybir.dt.int32
AF = mybir.ActivationFunctionType
ALU = mybir.AluOpType
AXX = mybir.AxisListType.X

D = 1024
NH = 4
CH = D // 128
HID = 3072
HC = HID // 128
PAD = 64
NTAP = 93
QS = 1.0 / 127.0

ENGS = ("sync", "gpsimd", "vector", "scalar", "tensor")


DMA_QUEUES = {"dl": "sync", "dg": "gpsimd", "ds": "scalar"}
DMA_POOL = 8


def _freeze(fn):
    import types
    if fn.__closure__:
        fn = types.FunctionType(
            fn.__code__, fn.__globals__, fn.__name__, fn.__defaults__,
            tuple(types.CellType(c.cell_contents) for c in fn.__closure__))
    return fn


class Prog:

    def __init__(self, nc, es):
        self.nc = nc
        self.recs = {e: [] for e in ENGS}
        self.sems = {}
        self.cnt = {}
        for name in ("v", "s", "t", "g"):
            self.sems[name] = es.enter_context(nc.semaphore("sem_" + name))
            self.cnt[name] = 0
        for q in DMA_QUEUES:
            for i in range(DMA_POOL):
                key = f"{q}{i}"
                self.sems[key] = es.enter_context(nc.semaphore("sem_" + key))
                self.cnt[key] = 0
        self.qidx = {q: 0 for q in DMA_QUEUES}
        self.lastw = {}
        self.waited = {name: set() for name in ("v", "s", "t", "g")}

    def emit(self, eng, fn, inc=None):
        fn = _freeze(fn)
        if inc in DMA_QUEUES:
            q = inc
            key = f"{q}{self.qidx[q] % DMA_POOL}"
            self.qidx[q] += 1
            prev = self.cnt[key]
            if prev > 0:
                self.wait(DMA_QUEUES[q], (key, prev))
            self.cnt[key] = prev + 16
            tok = (key, self.cnt[key])
            self.recs[eng].append(("dma", fn, key))
            return tok
        tok = None
        pval = 0
        if inc is not None:
            self.cnt[inc] += 1
            pval = self.cnt[inc]
            tok = (inc, pval)
        self.recs[eng].append(("op", fn, inc, pval))
        return tok

    def wait(self, eng, tok):
        if tok is None:
            return
        sem, val = tok
        if val <= 0:
            return
        key = (eng, sem)
        if self.lastw.get(key, -1) >= val:
            return
        self.lastw[key] = val
        if sem in self.waited:
            self.waited[sem].add(val)
        self.recs[eng].append(("wait", sem, val))

    def now(self, sem):
        return (sem, self.cnt[sem])

    def wait_queue(self, eng, q):
        for i in range(DMA_POOL):
            key = f"{q}{i}"
            if self.cnt[key] > 0:
                self.wait(eng, (key, self.cnt[key]))

    def barrier(self, bop=None):
        if os.environ.get("BARRIER_OFF") == "1":
            return
        for sem in ("s", "t"):
            self.wait("vector", (sem, self.cnt[sem]))
        for q in DMA_QUEUES:
            self.wait_queue("vector", q)
        btok = self.emit("vector", self._bop, "v")
        for eng in ("sync", "gpsimd", "scalar", "tensor"):
            self.wait(eng, btok)

    def finalize_maps(self):
        keep = {}
        for sem in ("v", "s", "t", "g"):
            kept = set()
            total = self.cnt[sem]
            for w in self.waited[sem]:
                if 0 < w <= total:
                    kept.add(w)
            if total > 0:
                kept.add(total)
            kept = sorted(kept)
            m = {v: i + 1 for i, v in enumerate(kept)}
            keep[sem] = (kept, m)
        return keep


class Bank:
    def __init__(self, prog, handle):
        self.p = prog
        self.h = handle
        self.readers = []
        self.wtok = None

    def begin_write(self):
        for tok in self.readers:
            self.p.wait("tensor", tok)
        self.readers = []

    def end_write(self, tok):
        self.wtok = tok

    def read_from(self, eng):
        self.p.wait(eng, self.wtok)

    def add_reader(self, tok):
        self.readers.append(tok)


def flat(ap):
    try:
        return ap.rearrange("p a b -> p (a b)")
    except ValueError:
        return ap


class _Stop(Exception):
    pass


_LAST = {}


def build(L=2048, taps=(), until=None):
    TH = min(1024, L // 2)
    NHALF = L // TH
    NC = TH // 128
    NT = TH // 512
    FT = 256
    NFT = L // FT

    nc = bass.Bass("TRN2", target_bir_lowering=False,
                   detect_race_conditions=False)
    es = ExitStack()
    P = Prog(nc, es)
    _LAST["nc"] = nc
    _LAST["P"] = P
    _LAST["tapd"] = tapd = {}
    tapset = set(taps)

    dr = lambda n, sh, dt, k: nc.dram_tensor(n, sh, dt, kind=k)
    x_d = dr("x", [L, D], F32, "ExternalInput")
    wq_d = dr("wqT", [CH, 128, D], F32, "ExternalInput")
    wk_d = dr("wkT", [CH, 128, D], F32, "ExternalInput")
    wv_d = dr("wvT", [CH, 128, D], F32, "ExternalInput")
    wo_d = dr("woT", [CH, 128, D], F32, "ExternalInput")
    wgb_d = dr("wgbT", [CH, 128, 24], F32, "ExternalInput")
    diag_d = dr("diag_all", [CH, NTAP, 128, 128], F32, "ExternalInput")
    gbx_d = dr("gb_exp", [128, 20], F32, "ExternalInput")
    ltemp_d = dr("log_temp", [4, 1], F32, "ExternalInput")
    epsp_d = dr("eps_param", [4, 1], F32, "ExternalInput")
    nw_d = dr("nw", [128, 2], F32, "ExternalInput")
    wg_d = dr("wgT", [CH, 128, HID], I32, "ExternalInput")
    wu_d = dr("wuT", [CH, 128, HID], I32, "ExternalInput")
    wd_d = dr("wdT", [HC, 128, D], I32, "ExternalInput")
    out_d = dr("out", [L, D], F32, "ExternalOutput")
    xnew_d = dr("xnew", [L, D], F32, "Internal")
    bsc_d = dr("bscr", [4, 3], F32, "Internal")

    def mktap(name, shape, dt=F32):
        if name in tapset:
            tapd[name] = dr("tap_" + name, shape, dt, "ExternalOutput")

    mktap("ln1f", [NHALF, CH, 128, TH], BF16)
    mktap("qf", [NHALF, CH, 128, TH], BF16)
    mktap("kf", [NHALF, CH, 128, TH], BF16)
    mktap("vf", [NHALF, CH, 128, TH], BF16)
    mktap("ktok", [NHALF, NC, 128, D], BF16)
    mktap("vtok", [NHALF, NC, 128, D], BF16)
    mktap("khatf", [NHALF, CH, 128, TH], BF16)
    mktap("psc", [NHALF, NC, 128, 40], F32)
    mktap("accmix", [NHALF, NC, 128, D], BF16)
    mktap("accfin", [NHALF, NC, 128, D], BF16)
    mktap("of", [NHALF, CH, 128, TH], BF16)
    mktap("xnew", [L, D], F32)
    mktap("ln2f", [NFT, CH, 128, FT], BF16)
    mktap("hbuf", [NFT, HC, 128, FT], BF16)

    eyeb_d = nc.inline_tensor(np.eye(128, dtype=ml_dtypes.bfloat16), "eyeb_c")
    eyef_d = nc.inline_tensor(np.eye(128, dtype=np.float32), "eyef_c")
    negm_d = nc.inline_tensor(-np.tril(np.ones((128, 128), dtype=np.float32), -1),
                              "negm_c")
    ecst_np = np.zeros((128, 2), dtype=np.float32)
    ecst_np[:, 0] = 1e-5
    ecst_np[:, 1] = 1e-6
    ecst_d = nc.inline_tensor(ecst_np, "ecst_c")
    cmT_d = nc.inline_tensor(np.triu(np.ones((128, 128), dtype=np.float32), 0),
                             "cmT_c")

    _side = "left" if os.environ.get("ALL_LEFT") == "1" else "right"
    sbr = lambda n, sh, dt: es.enter_context(nc.sbuf_tensor(n, sh, dt, side=_side))
    eyeb = sbr("eyeb", [128, 128], BF16)
    eyef = sbr("eyef", [128, 128], F32)
    negm = sbr("negm", [128, 128], F32)
    cmT = sbr("cmT", [128, 128], F32)
    gbx = sbr("gbx", [128, 20], F32)
    nwt = sbr("nwt", [128, 2], F32)
    scb = sbr("scb", [128, 12], F32)
    tiny = sbr("tiny", [128, 8], F32)
    stat = sbr("stat", [128, 128], F32)
    scrap = sbr("scrap", [128, 1024], F32)
    ecst = sbr("ecst", [128, 2], F32)
    P._bop = lambda h: h.tensor_copy(scrap[0:1, 0:1], scrap[0:1, 1:2])
    P.vgap = lambda: P.wait("vector", P.now("v"))
    P.sgap = lambda: P.wait("scalar", P.now("s"))
    wgb = sbr("wgb", [128, CH, 24], BF16)
    att = ExitStack()
    sbar = lambda n, sh, dt: att.enter_context(nc.sbuf_tensor(n, sh, dt, side=_side))
    wo = sbar("wo", [128, CH, D], BF16)
    cq = sbar("cq", [128, CH, PAD], BF16)
    ck = sbar("ck", [128, CH, PAD], BF16)
    cv = sbar("cv", [128, CH, PAD], BF16)
    cvp = sbar("cvp", [128, CH, PAD], BF16)
    S32 = sbar("S32", [128, 2 * NH, 256], F32)
    Sbf = sbar("Sbf", [128, 2 * NH, 256], BF16)

    pb = [Bank(P, es.enter_context(nc.psum_tensor(f"pb{i}", [128, 512], F32)))
          for i in range(6)]
    pbT = Bank(P, es.enter_context(nc.psum_tensor("pbT", [128, 1024], BF16)))
    pbS = pbT
    pb7 = Bank(P, es.enter_context(nc.psum_tensor("pb7", [128, 512], F32)))

    lds = []
    _consts = ((eyeb, eyeb_d), (eyef, eyef_d), (negm, negm_d),
               (cmT, cmT_d), (gbx, gbx_d), (ecst, ecst_d))
    if os.environ.get("NO_CONSTS") == "1":
        _consts = ((eyeb, eyeb_d),)
    for dst, src in _consts:
        lds.append(P.emit("sync", lambda h, d=dst, s=src: h.dma_start(
            out=d[:, :], in_=s.ap()[:, :]), "dl"))
    if os.environ.get("NO_CONSTS") != "1":
        lds.append(P.emit("sync", lambda h: h.dma_start(
            out=nwt[:, :], in_=nw_d.ap()[:, :]), "dl"))
    if os.environ.get("NO_LTEMP") != "1":
        lds.append(P.emit("sync", lambda h: h.dma_start(
            out=tiny[0:4, 0:1], in_=ltemp_d.ap()[:, :]), "dl"))
        lds.append(P.emit("sync", lambda h: h.dma_start(
            out=tiny[0:4, 1:2], in_=epsp_d.ap()[:, :]), "dl"))
    if os.environ.get("NO_WOWGB") != "1":
        t_wo = P.emit("gpsimd", lambda h: h.dma_start(
            out=wo[:, :, :], in_=wo_d.ap().transpose([1, 0, 2])), "dg")
        t_wgb = P.emit("gpsimd", lambda h: h.dma_start(
            out=wgb[:, :, :], in_=wgb_d.ap().transpose([1, 0, 2])), "dg")
    else:
        t_wo = t_wgb = None

    for tok in lds:
        P.wait("scalar", tok)
        P.wait("vector", tok)
    P.emit("vector", lambda h: h.memset(scrap[:, :], 0.0), "v")
    if os.environ.get("NO_TINY") != "1":
        P.emit("scalar", lambda h: h.activation(tiny[0:4, 2:3], tiny[0:4, 0:1], AF.Exp), "s")
        P.wait("vector", P.now("s"))
        P.emit("vector", lambda h: h.tensor_scalar(
            tiny[0:4, 5:6], tiny[0:4, 1:2], 0.0, 0.2, ALU.max, ALU.min), "v")
        P.emit("vector", lambda h: h.reciprocal(tiny[0:4, 4:5], tiny[0:4, 2:3]), "v")
        P.emit("vector", lambda h: h.tensor_scalar(
            tiny[0:4, 6:7], tiny[0:4, 5:6], -5.0, 1.0, ALU.mult, ALU.add), "v")
    P.wait("gpsimd", P.now("v"))
    if os.environ.get("NO_SCB") != "1":
        P.emit("gpsimd", lambda h: h.dma_start(out=bsc_d.ap()[:, :], in_=tiny[0:4, 4:7]), "dg")
        P.wait_queue("gpsimd", "dg")
        t_scb = P.emit("gpsimd", lambda h: h.dma_start(
            out=scb[:, 0:12],
            in_=bsc_d.ap().rearrange("h c -> (h c)").unsqueeze(0).broadcast_to([128, 12])),
            "dg")
        P.wait("vector", t_scb)
        P.wait("scalar", t_scb)
    P.wait("vector", t_wo)
    if os.environ.get("NO_FOLD") != "1":
        for kc in range(CH):
            P.emit("vector", lambda h, kc=kc: h.tensor_scalar(
                wo[:, kc, :], wo[:, kc, :], nwt[:, kc % 2:kc % 2 + 1], None, ALU.mult), "v")
    t_wofold = P.now("v")
    if os.environ.get("NO_MS") != "1":
        P.emit("vector", lambda h: h.memset(flat(S32[:, :, :]), 0.0), "v")
        P.emit("vector", lambda h: h.memset(flat(Sbf[:, :, :]), 0.0), "v")
    t_sinit = P.now("v")
    P.wait("tensor", t_wgb)

    def emit_ln_chunk(t_x, xsl, lnt_dst, t_lnt_free):
        P.wait("vector", t_x)
        P.wait("scalar", t_x)
        P.emit("vector", lambda h: h.tensor_reduce(
            stat[:, 0:1], xsl, AXX, ALU.add), "v")
        P.wait("scalar", t_lnt_free)
        t_sq = P.emit("scalar", lambda h: h.activation(
            lnt_dst, xsl, AF.Square, accum_out=stat[:, 1:2]), "s")
        P.wait("vector", t_sq)
        P.vgap()
        P.emit("vector", lambda h: h.tensor_scalar(
            stat[:, 0:1], stat[:, 0:1], 1.0 / D, None, ALU.mult), "v")
        P.emit("vector", lambda h: h.tensor_scalar(
            stat[:, 1:2], stat[:, 1:2], 1.0 / D, None, ALU.mult), "v")
        P.emit("vector", lambda h: h.tensor_tensor(
            stat[:, 2:3], stat[:, 0:1], stat[:, 0:1], ALU.mult), "v")
        P.vgap()
        P.emit("vector", lambda h: h.tensor_tensor(
            stat[:, 2:3], stat[:, 1:2], stat[:, 2:3], ALU.subtract), "v")
        P.wait("scalar", P.now("v"))
        t_rs = P.emit("scalar", lambda h: h.activation(
            stat[:, 2:3], stat[:, 2:3], AF.Sqrt, bias=ecst[:, 0:1]), "s")
        P.wait("vector", t_rs)
        P.emit("vector", lambda h: h.reciprocal(stat[:, 3:4], stat[:, 2:3]), "v")
        P.vgap()
        P.wait("vector", t_lnt_free)
        return P.emit("vector", lambda h: h.tensor_scalar(
            lnt_dst, xsl, stat[:, 0:1], stat[:, 3:4], ALU.subtract, ALU.mult), "v")

    def ckpt(name):
        if until == name:
            raise _Stop

    NCH2 = L // 128
    px = ExitStack()
    xstp = px.enter_context(nc.sbuf_tensor("xstp", [128, 2, D], F32, side=_side))
    t_rdp = [(None, None), (None, None)]
    for ci in range(NCH2):
        P.wait("sync", t_rdp[ci % 2][0])
        P.wait("sync", t_rdp[ci % 2][1])
        t_x = P.emit("sync", lambda h, ci=ci: h.dma_start(
            out=xstp[:, ci % 2, :], in_=x_d.ap()[ci * 128:ci * 128 + 128, :]),
            "dl")
        P.wait("vector", t_x)
        tv = P.emit("vector", lambda h, ci=ci: h.tensor_reduce(
            stat[:, ci:ci + 1], xstp[:, ci % 2, :], AXX, ALU.add), "v")
        P.wait("scalar", t_x)
        ts = P.emit("scalar", lambda h, ci=ci: h.activation(
            scrap[:, 0:D], xstp[:, ci % 2, :], AF.Square,
            accum_out=stat[:, 112 + ci:113 + ci]), "s")
        t_rdp[ci % 2] = (tv, ts)
    P.wait("vector", P.now("s"))
    P.emit("vector", lambda h: h.tensor_scalar(
        stat[:, 0:NCH2], stat[:, 0:NCH2], 1.0 / D, None, ALU.mult), "v")
    P.emit("vector", lambda h: h.tensor_scalar(
        stat[:, 112:112 + NCH2], stat[:, 112:112 + NCH2], 1.0 / D, None,
        ALU.mult), "v")
    P.emit("vector", lambda h: h.tensor_tensor(
        stat[:, 16:16 + NCH2], stat[:, 0:NCH2], stat[:, 0:NCH2], ALU.mult), "v")
    P.vgap()
    P.emit("vector", lambda h: h.tensor_tensor(
        stat[:, 16:16 + NCH2], stat[:, 112:112 + NCH2], stat[:, 16:16 + NCH2],
        ALU.subtract), "v")
    P.wait("scalar", P.now("v"))
    t_rs1 = P.emit("scalar", lambda h: h.activation(
        stat[:, 16:16 + NCH2], stat[:, 16:16 + NCH2], AF.Sqrt,
        bias=ecst[:, 0:1]), "s")
    P.wait("vector", t_rs1)
    P.emit("vector", lambda h: h.reciprocal(
        stat[:, 16:16 + NCH2], stat[:, 16:16 + NCH2]), "v")
    P.vgap()
    t_ln1c = P.now("v")
    px.close()

    ckpt("prologue")
    t_scan_prev = t_sinit
    for half in range(NHALF):
        hs = ExitStack()
        sbl = lambda n, sh, dt: hs.enter_context(
            nc.sbuf_tensor(f"{n}_{half}", sh, dt, side="left"))
        ACC = sbl("ACC", [128, NC, D], BF16)
        vtok = sbl("vtok", [128, NC, D], BF16)
        ktok = sbl("ktok", [128, NC, D], BF16)
        khatf = sbl("khatf", [128, CH, TH], BF16)
        psc = sbl("psc", [128, NC, 48], F32)
        qx = ExitStack()
        sbq = lambda n, sh, dt: qx.enter_context(
            nc.sbuf_tensor(f"{n}_{half}", sh, dt, side="left"))
        qf = sbq("qf", [128, CH, PAD + TH], BF16)
        kf = sbq("kf", [128, CH, PAD + TH], BF16)
        vf = sbq("vf", [128, CH, PAD + TH], BF16)

        ax = ExitStack()
        sba = lambda n, sh, dt: ax.enter_context(
            nc.sbuf_tensor(f"{n}_{half}", sh, dt, side="left"))
        xst = sba("xst", [128, 2, D], F32)
        lnt = sba("lnt", [128, 2, D], BF16)
        ln1f = sba("ln1f", [128, CH, TH], BF16)
        war = sba("war", [128, CH, D], BF16)
        lgt = sba("lgt", [128, NC, 20], F32)

        t_ap2 = [None, None]
        t_trs = [None, None]
        for tc in range(NC):
            t0 = half * TH + tc * 128
            ci = half * NC + tc
            P.wait("sync", t_ap2[tc % 2])
            t_x = P.emit("sync", lambda h, tc=tc, t0=t0: h.dma_start(
                out=xst[:, tc % 2, :], in_=x_d.ap()[t0:t0 + 128, :]), "dl")
            P.wait("vector", t_x)
            P.wait("vector", t_ln1c if tc == 0 else None)
            P.wait("vector", t_trs[tc % 2])
            t_ap2[tc % 2] = P.emit("vector", lambda h, tc=tc, ci=ci: h.tensor_scalar(
                lnt[:, tc % 2, :], xst[:, tc % 2, :], stat[:, ci:ci + 1],
                stat[:, 16 + ci:17 + ci], ALU.subtract, ALU.mult), "v")
            P.wait("tensor", t_ap2[tc % 2])
            pbT.begin_write()
            for cc in range(CH):
                tk = P.emit("tensor", lambda h, cc=cc, tc=tc: h.transpose(
                    pbT.h[:, cc * 128:(cc + 1) * 128],
                    lnt[:, tc % 2, cc * 128:(cc + 1) * 128], eyeb[:, :]),
                    "t" if cc == CH - 1 else None)
            pbT.end_write(tk)
            t_trs[tc % 2] = tk
            pbT.read_from("scalar")
            tev = P.emit("scalar", lambda h, tc=tc: h.activation(
                flat(ln1f[:, :, tc * 128:(tc + 1) * 128]), pbT.h[:, :], AF.Copy), "s")
            pbT.add_reader(tev)
            if os.environ.get("NO_GATE") == "1":
                continue
            bg = pb[4] if tc % 2 == 0 else pb[5]
            bg.begin_write()
            P.wait("tensor", tev)
            for kc in range(CH):
                tk = P.emit("tensor", lambda h, tc=tc, kc=kc: h.matmul(
                    bg.h[:, 0:24], ln1f[:, kc, tc * 128:(tc + 1) * 128],
                    wgb[:, kc, :], start=(kc == 0), stop=(kc == CH - 1)),
                    "t" if kc == CH - 1 else None)
            bg.end_write(tk)
            bg.read_from("vector")
            t_gl = P.emit("vector", lambda h, tc=tc: h.tensor_tensor(
                lgt[:, tc, :], bg.h[:, 0:20], gbx[:, :], ALU.add), "v")
            bg.read_from("scalar")
            t_bt = P.emit("scalar", lambda h, tc=tc: h.activation(
                psc[:, tc, 20:24], bg.h[:, 20:24], AF.Sigmoid), "s")
            bg.add_reader(t_gl)
            bg.add_reader(t_bt)
        t_ln_all = P.now("s")

        if os.environ.get("NO_PROBS") != "1":
            P.vgap()
            for tc in range(NC):
                for hh in range(NH):
                    P.emit("vector", lambda h, tc=tc, hh=hh: h.reduce_max(
                        stat[:, 32 + hh * NC + tc:33 + hh * NC + tc],
                        lgt[:, tc, 5 * hh:5 * hh + 5], AXX), "v")
            P.vgap()
            for hh in range(NH):
                P.emit("vector", lambda h, hh=hh: h.tensor_scalar(
                    stat[:, 32 + hh * NC:32 + hh * NC + NC],
                    stat[:, 32 + hh * NC:32 + hh * NC + NC],
                    scb[:, 3 * hh:3 * hh + 1], -1.0, ALU.mult, ALU.mult), "v")
            P.wait("scalar", P.now("v"))
            for tc in range(NC):
                for hh in range(NH):
                    P.emit("scalar", lambda h, tc=tc, hh=hh: h.activation(
                        lgt[:, tc, 5 * hh:5 * hh + 5], lgt[:, tc, 5 * hh:5 * hh + 5],
                        AF.Exp, bias=stat[:, 32 + hh * NC + tc:33 + hh * NC + tc],
                        scale=scb[:, 3 * hh:3 * hh + 1]), "s")
            P.wait("vector", P.now("s"))
            for tc in range(NC):
                for hh in range(NH):
                    P.emit("vector", lambda h, tc=tc, hh=hh: h.tensor_reduce(
                        stat[:, 32 + hh * NC + tc:33 + hh * NC + tc],
                        lgt[:, tc, 5 * hh:5 * hh + 5], AXX, ALU.add), "v")
            P.vgap()
            P.emit("vector", lambda h: h.reciprocal(
                stat[:, 32:32 + NH * NC], stat[:, 32:32 + NH * NC]), "v")
            P.vgap()
            for hh in range(NH):
                P.emit("vector", lambda h, hh=hh: h.tensor_scalar(
                    stat[:, 32 + hh * NC:32 + hh * NC + NC],
                    stat[:, 32 + hh * NC:32 + hh * NC + NC],
                    scb[:, 3 * hh + 2:3 * hh + 3], None, ALU.mult), "v")
            P.vgap()
            for tc in range(NC):
                for hh in range(NH):
                    P.emit("vector", lambda h, tc=tc, hh=hh: h.tensor_scalar(
                        psc[:, tc, 5 * hh:5 * hh + 5], lgt[:, tc, 5 * hh:5 * hh + 5],
                        stat[:, 32 + hh * NC + tc:33 + hh * NC + tc],
                        scb[:, 3 * hh + 1:3 * hh + 2], ALU.mult, ALU.add), "v")
        ckpt("ln" + str(half))

        if "ln1f" in tapd:
            P.wait("scalar", t_ln_all)
            for cc in range(CH):
                P.emit("scalar", lambda h, cc=cc: h.dma_start(
                    out=tapd["ln1f"].ap()[half, cc], in_=ln1f[:, cc, :]), "ds")

        t_war_free = None
        for wi, (wsrc, dst) in enumerate(((wq_d, qf), (wk_d, kf), (wv_d, vf))):
            P.wait("gpsimd", t_war_free)
            t_w = P.emit("gpsimd", lambda h, wsrc=wsrc: h.dma_start(
                out=war[:, :, :], in_=wsrc.ap().transpose([1, 0, 2])), "dg")
            P.wait("tensor", t_w)
            P.wait("tensor", t_ln_all)
            ckpt(f"pwdma{half}_{wi}")
            for tt in range(int(os.environ.get("TT_LIMIT", NT))):
                for mc in range(int(os.environ.get("MC_LIMIT", CH))):
                    bk = pb[int(os.environ.get('PB_BASE', '0')) + mc % 2] if os.environ.get('PB_ONE') != '1' else pb[0]
                    bk.begin_write()
                    for kc in range(CH):
                        mco = 0 if os.environ.get("LHS0") == "1" else mc
                        tk = P.emit("tensor", lambda h, mco=mco, kc=kc, tt=tt: h.matmul(
                            bk.h[:, 0:512], war[:, kc, mco * 128:(mco + 1) * 128],
                            ln1f[:, kc, tt * 512:tt * 512 + 512],
                            start=(kc == 0), stop=(kc == CH - 1)),
                            "t" if kc == CH - 1 else None)
                    bk.end_write(tk)
                    if os.environ.get("EVAC_V") == "1":
                        bk.read_from("vector")
                        tev = P.emit("vector", lambda h, mc=mc, tt=tt, dst=dst:
                                     h.tensor_copy(
                            dst[:, mc, PAD + tt * 512:PAD + tt * 512 + 512],
                            bk.h[:, 0:512]), "v")
                    else:
                        bk.read_from("scalar")
                        tev = P.emit("scalar", lambda h, mc=mc, tt=tt, dst=dst: h.activation(
                            dst[:, mc, PAD + tt * 512:PAD + tt * 512 + 512],
                            bk.h[:, 0:512], AF.Copy), "s")
                    bk.add_reader(tev)
                    ckpt(f"pwmm{half}_{wi}_{mc}")
            t_war_free = P.now("t")
            ckpt(f"pw{half}_{wi}")
        t_proj = P.now("s")
        ckpt("proj" + str(half))

        if half == 0 and os.environ.get("NO_PADMS") != "1":
            P.wait("vector", t_sinit)
            for buf in (qf, kf, vf):
                P.emit("vector", lambda h, buf=buf: h.memset(
                    flat(buf[:, :, 0:PAD]), 0.0), "v")
        else:
            P.wait("vector", t_proj)
            for buf, car in ((qf, cq), (kf, ck), (vf, cv)):
                P.emit("vector", lambda h, buf=buf, car=car: h.tensor_copy(
                    flat(buf[:, :, 0:PAD]), flat(car[:, :, :])), "v")
        t_pad = P.now("v")
        P.barrier()
        ax.close()

        bx = ExitStack()
        da = bx.enter_context(nc.sbuf_tensor(f"da_{half}", [128, 117, 128], BF16,
                                             side="left"))
        fsb = bx.enter_context(nc.sbuf_tensor(f"fs_{half}", [128, TH], BF16,
                                              side="left"))
        P.wait("vector", t_proj)
        if half + 1 < NHALF:
            for buf, car in ((qf, cq), (kf, ck), (vf, cv)):
                P.emit("vector", lambda h, buf=buf, car=car: h.tensor_copy(
                    flat(car[:, :, :]), flat(buf[:, :, TH:TH + PAD])), "v")
        t_stash = P.now("v")

        t_mmB = {}
        t_daB = {}

        def load_shorts(cc):
            sb = 12 * (cc % 2)
            P.wait("gpsimd", t_mmB.get(cc - 2))
            t_daB[cc] = P.emit("gpsimd", lambda h, cc=cc, sb=sb: h.dma_start(
                out=da[:, sb:sb + 12, :],
                in_=diag_d.ap()[cc, 0:12, :, :].transpose([1, 0, 2])), "dg")

        load_shorts(0)
        load_shorts(1)
        for cc in range(CH):
            sb = 12 * (cc % 2)
            P.wait("tensor", t_daB[cc])
            P.wait("tensor", t_stash)
            P.wait("tensor", t_pad)
            for ti, (buf, f0) in enumerate(((qf, sb + 0), (kf, sb + 4), (vf, sb + 8))):
                b0, b1 = pb[2 * (ti % 2)], pb[2 * (ti % 2) + 1]
                b0.begin_write()
                b1.begin_write()
                for j in range(4):
                    off = PAD - 3 + j
                    for tt in range(NT):
                        bk = b0 if tt % 2 == 0 else b1
                        tk = P.emit("tensor", lambda h, cc=cc, j=j, f0=f0, tt=tt,
                                    off=off, buf=buf, bk=bk: h.matmul(
                            bk.h[:, 0:512], da[:, f0 + j, :],
                            buf[:, cc, off + tt * 512:off + tt * 512 + 512],
                            start=(j == 0), stop=(j == 3)),
                            "t" if j == 3 else None)
                        (b0 if tt % 2 == 0 else b1).end_write(tk)
                for tt in range(NT):
                    bk = b0 if tt % 2 == 0 else b1
                    bk.read_from("scalar")
                    tev = P.emit("scalar", lambda h, cc=cc, tt=tt, buf=buf, bk=bk:
                                 h.activation(
                        buf[:, cc, PAD + tt * 512:PAD + tt * 512 + 512],
                        bk.h[:, 0:512], AF.Silu), "s")
                    bk.add_reader(tev)
            t_mmB[cc] = P.now("t")
            if cc + 2 < CH:
                load_shorts(cc + 2)
        t_conv = P.now("s")
        ckpt("conv" + str(half))

        P.wait("vector", t_conv)
        if half > 0:
            P.emit("vector", lambda h: h.tensor_copy(
                flat(vf[:, :, 0:PAD]), flat(cvp[:, :, :])), "v")
        if half + 1 < NHALF:
            P.emit("vector", lambda h: h.tensor_copy(
                flat(cvp[:, :, :]), flat(vf[:, :, TH:TH + PAD])), "v")
        t_vpad = P.now("v")

        for nm, buf in (("qf", qf), ("kf", kf), ("vf", vf)):
            if nm in tapd:
                P.wait("scalar", t_conv)
                P.wait("scalar", t_vpad)
                for cc in range(CH):
                    P.emit("scalar", lambda h, nm=nm, cc=cc, buf=buf: h.dma_start(
                        out=tapd[nm].ap()[half, cc], in_=buf[:, cc, PAD:PAD + TH]), "ds")

        for tname in ("k", "q", "v"):
            src = {"k": kf, "q": qf, "v": vf}[tname]
            t_need = t_vpad if tname == "v" else t_conv
            for tc in range(NC):
                pbT.begin_write()
                P.wait("tensor", t_need)
                for cc in range(CH):
                    tk = P.emit("tensor", lambda h, src=src, cc=cc, tc=tc: h.transpose(
                        pbT.h[:, cc * 128:(cc + 1) * 128],
                        src[:, cc, PAD + tc * 128:PAD + tc * 128 + 128],
                        eyeb[:, :]), "t" if cc == CH - 1 else None)
                pbT.end_write(tk)
                if tname in ("k", "q"):
                    dc = 24 if tname == "k" else 28
                    pbT.read_from("scalar")
                    for hh in range(NH):
                        tr = P.emit("scalar", lambda h, hh=hh, tc=tc, dc=dc:
                                    h.activation(
                            scrap[:, 256 * hh:256 * hh + 256],
                            pbT.h[:, hh * 256:hh * 256 + 256], AF.Square,
                            accum_out=psc[:, tc, dc + hh:dc + hh + 1]), "s")
                        pbT.add_reader(tr)
                    P.sgap()
                    t_s = P.emit("scalar", lambda h, tc=tc, dc=dc: h.activation(
                        psc[:, tc, dc:dc + 4], psc[:, tc, dc:dc + 4], AF.Sqrt,
                        bias=ecst[:, 1:2]), "s")
                    P.wait("vector", t_s)
                    P.emit("vector", lambda h, tc=tc, dc=dc: h.reciprocal(
                        psc[:, tc, dc:dc + 4], psc[:, tc, dc:dc + 4]), "v")
                    if tname == "k":
                        pbT.read_from("vector")
                        P.vgap()
                        for hh in range(NH):
                            tr = P.emit("vector", lambda h, hh=hh, tc=tc: h.tensor_scalar(
                                ktok[:, tc, hh * 256:hh * 256 + 256],
                                pbT.h[:, hh * 256:hh * 256 + 256],
                                psc[:, tc, 24 + hh:25 + hh], None, ALU.mult), "v")
                            pbT.add_reader(tr)
                else:
                    pbT.read_from("scalar")
                    tr = P.emit("scalar", lambda h, tc=tc: h.activation(
                        vtok[:, tc, :], pbT.h[:, :], AF.Copy), "s")
                    pbT.add_reader(tr)
        P.wait("vector", P.now("s"))
        for tc in range(NC):
            P.emit("vector", lambda h, tc=tc: h.tensor_tensor(
                psc[:, tc, 32:36], psc[:, tc, 24:28], psc[:, tc, 20:24], ALU.mult), "v")
            for hh in range(NH):
                P.emit("vector", lambda h, tc=tc, hh=hh: h.tensor_tensor(
                    psc[:, tc, 36 + hh:37 + hh], psc[:, tc, 28 + hh:29 + hh],
                    psc[:, tc, 5 * hh + 3:5 * hh + 4], ALU.mult), "v")
        t_scl = P.now("v")
        P.wait("tensor", t_scl)
        for tc in range(NC):
            for g in range(2):
                pbS.begin_write()
                for cc in range(4 * g, 4 * g + 4):
                    tk = P.emit("tensor", lambda h, cc=cc, tc=tc: h.transpose(
                        pbS.h[:, (cc % 4) * 128:(cc % 4) * 128 + 128],
                        ktok[:, tc, cc * 128:(cc + 1) * 128], eyeb[:, :]),
                        "t" if cc % 4 == 3 else None)
                pbS.end_write(tk)
                pbS.read_from("scalar")
                tr = P.emit("scalar", lambda h, g=g, tc=tc: h.activation(
                    flat(khatf[:, 4 * g:4 * g + 4, tc * 128:tc * 128 + 128]),
                    pbS.h[:, 0:512], AF.Copy), "s")
                pbS.add_reader(tr)
        t_phC = P.now("s")
        ckpt("phC" + str(half))

        for nm, buf in (("ktok", ktok), ("vtok", vtok)):
            if nm in tapd:
                P.wait("scalar", t_phC)
                for tc in range(NC):
                    P.emit("scalar", lambda h, nm=nm, tc=tc, buf=buf: h.dma_start(
                        out=tapd[nm].ap()[half, tc], in_=buf[:, tc, :]), "ds")
        if "khatf" in tapd:
            P.wait("scalar", t_phC)
            for cc in range(CH):
                P.emit("scalar", lambda h, cc=cc: h.dma_start(
                    out=tapd["khatf"].ap()[half, cc], in_=khatf[:, cc, :]), "ds")

        P.wait("vector", t_phC)
        for tc in range(NC):
            for hh in range(NH):
                P.emit("vector", lambda h, tc=tc, hh=hh: h.tensor_scalar(
                    ACC[:, tc, hh * 256:hh * 256 + 256],
                    vtok[:, tc, hh * 256:hh * 256 + 256],
                    psc[:, tc, 5 * hh + 4:5 * hh + 5], None, ALU.mult), "v")
        t_acc0 = P.now("v")

        t_smm = {}
        t_lmm = {}
        t_dasm = {}
        t_dal = {}

        def load_sm(cc):
            sb = 18 * (cc % 2)
            P.wait("gpsimd", t_smm.get(cc - 2))
            t_dasm[cc] = P.emit("gpsimd", lambda h, cc=cc, sb=sb: h.dma_start(
                out=da[:, sb:sb + 18, :],
                in_=diag_d.ap()[cc, 12:30, :, :].transpose([1, 0, 2])), "dg")

        def load_l(cc):
            P.wait("gpsimd", t_lmm.get(cc - 1))
            t_dal[cc] = P.emit("gpsimd", lambda h, cc=cc: h.dma_start(
                out=da[:, 36:99, :],
                in_=diag_d.ap()[cc, 30:93, :, :].transpose([1, 0, 2])), "dg")

        P.wait("gpsimd", P.now("t"))
        load_sm(0)
        load_l(0)
        load_sm(1)
        for cc in range(CH):
            smb = 18 * (cc % 2)
            hh = cc // 2
            fir_specs = ((smb + 0, 3, 0), (smb + 3, 15, 1), (36, 63, 2))
            for si, (lo, ntap, pcol) in enumerate(fir_specs):
                b0, b1 = pb[0], pb[1]
                b0.begin_write()
                b1.begin_write()
                P.wait("tensor", t_dal[cc] if si == 2 else t_dasm[cc])
                P.wait("tensor", t_vpad)
                for j in range(ntap):
                    off = PAD - (ntap - 1) + j
                    for tt in range(NT):
                        bk = b0 if tt % 2 == 0 else b1
                        tk = P.emit("tensor", lambda h, cc=cc, lo=lo, j=j, off=off,
                                    tt=tt, bk=bk, ntap=ntap: h.matmul(
                            bk.h[:, 0:512], da[:, lo + j, :],
                            vf[:, cc, off + tt * 512:off + tt * 512 + 512],
                            start=(j == 0), stop=(j == ntap - 1)),
                            "t" if j == ntap - 1 else None)
                        (b0 if tt % 2 == 0 else b1).end_write(tk)
                if si == 1:
                    t_smm[cc] = P.now("t")
                    if cc + 2 < CH:
                        load_sm(cc + 2)
                if si == 2:
                    t_lmm[cc] = P.now("t")
                    if cc + 1 < CH:
                        load_l(cc + 1)
                for tt in range(NT):
                    bk = b0 if tt % 2 == 0 else b1
                    bk.read_from("scalar")
                    tev = P.emit("scalar", lambda h, tt=tt, bk=bk: h.activation(
                        fsb[:, tt * 512:tt * 512 + 512], bk.h[:, 0:512], AF.Copy), "s")
                    bk.add_reader(tev)
                t_fs = P.now("s")
                P.wait("tensor", t_fs)
                P.wait("tensor", t_acc0)
                for tc in range(NC):
                    pbS.begin_write()
                    tk = P.emit("tensor", lambda h, tc=tc: h.transpose(
                        pbS.h[:, (tc % 4) * 128:(tc % 4) * 128 + 128],
                        fsb[:, tc * 128:tc * 128 + 128], eyeb[:, :]), "t")
                    pbS.end_write(tk)
                    pbS.read_from("vector")
                    tr = P.emit("vector", lambda h, tc=tc, cc=cc, hh=hh, pcol=pcol:
                                h.scalar_tensor_tensor(
                        ACC[:, tc, cc * 128:(cc + 1) * 128],
                        pbS.h[:, (tc % 4) * 128:(tc % 4) * 128 + 128],
                        psc[:, tc, 5 * hh + pcol:5 * hh + pcol + 1],
                        ACC[:, tc, cc * 128:(cc + 1) * 128], ALU.mult, ALU.add), "v")
                    pbS.add_reader(tr)
        t_fir = P.now("v")
        P.barrier()
        bx.close()
        ckpt("fir" + str(half))

        if "accmix" in tapd:
            P.wait("scalar", t_fir)
            for tc in range(NC):
                P.emit("scalar", lambda h, tc=tc: h.dma_start(
                    out=tapd["accmix"].ap()[half, tc], in_=ACC[:, tc, :]), "ds")
        if "psc" in tapd:
            P.wait("scalar", t_fir)
            P.wait("scalar", t_scl)
            for tc in range(NC):
                P.emit("scalar", lambda h, tc=tc: h.dma_start(
                    out=tapd["psc"].ap()[half, tc], in_=psc[:, tc, 0:40]), "ds")

        ex = ExitStack()
        sbe = lambda n, sh, dt: ex.enter_context(
            nc.sbuf_tensor(f"{n}_{half}", sh, dt, side="left"))
        dA = sbe("dA", [128, NH, 128], BF16)
        dC = sbe("dC", [128, NH, 128], BF16)
        dB = sbe("dB", [128, NH, 128], BF16)
        dQa = sbe("dQa", [128, NH, 128], BF16)
        dQb = sbe("dQb", [128, NH, 128], BF16)
        du = sbe("du", [128, NH, 256], BF16)
        dua = sbe("dua", [128, NH, 256], BF16)
        datn = sbe("datn", [128, 2, NH, 128], BF16)
        dwf = sbe("dwf", [128, 2 * NH, 128], BF16)
        dvh = sbe("dvh", [128, NH, 256], BF16)
        dkb = sbe("dkb", [128, NH, 256], BF16)

        NLEV = 6
        CS = {"t_wf": None, "t_uwmm": None, "t_scan_prev": t_scan_prev}

        def pre_scales(tn):
            P.wait("vector", t_fir if tn == 0 else P.now("v"))
            if CS["t_uwmm"] is not None:
                P.wait("vector", CS["t_uwmm"])
            for hh in range(NH):
                P.emit("vector", lambda h, tn=tn, hh=hh: h.tensor_scalar(
                    dvh[:, hh, :], vtok[:, tn, hh * 256:hh * 256 + 256],
                    psc[:, tn, 20 + hh:21 + hh], None, ALU.mult), "v")
                P.emit("vector", lambda h, tn=tn, hh=hh: h.tensor_scalar(
                    dkb[:, hh, :], ktok[:, tn, hh * 256:hh * 256 + 256],
                    psc[:, tn, 20 + hh:21 + hh], None, ALU.mult), "v")
            CS["t_sc"] = P.now("v")

        def pre_Gmm(tn):
            tr0 = PAD + tn * 128
            pb7.begin_write()
            P.wait("tensor", CS["t_sc"])
            for hh in range(NH):
                for kc in range(2):
                    tk = P.emit("tensor", lambda h, hh=hh, kc=kc, tn=tn, tr0=tr0:
                                h.matmul(
                        pb7.h[:, hh * 128:hh * 128 + 128],
                        kf[:, 2 * hh + kc, tr0:tr0 + 128],
                        khatf[:, 2 * hh + kc, tn * 128:tn * 128 + 128],
                        start=(kc == 0), stop=(kc == 1)),
                        "t" if hh == NH - 1 and kc == 1 else None)
            pb7.end_write(tk)

        def pre_Aevac(tn):
            pb7.read_from("vector")
            for hh in range(NH):
                tr = P.emit("vector", lambda h, hh=hh, tn=tn: h.scalar_tensor_tensor(
                    dA[:, hh, :], pb7.h[:, hh * 128:hh * 128 + 128],
                    psc[:, tn, 32 + hh:33 + hh], negm[:, :], ALU.mult, ALU.mult), "v")
            pb7.add_reader(tr)
            CS["t_A"] = P.now("v")

        def pre_attnT(tn):
            tr0 = PAD + tn * 128
            pp = tn % 2
            pb[2].begin_write()
            P.wait("tensor", CS["t_A"])
            for hh in range(NH):
                for kc in range(2):
                    tk = P.emit("tensor", lambda h, hh=hh, kc=kc, tn=tn, tr0=tr0:
                                h.matmul(
                        pb[2].h[:, hh * 128:hh * 128 + 128],
                        khatf[:, 2 * hh + kc, tn * 128:tn * 128 + 128],
                        qf[:, 2 * hh + kc, tr0:tr0 + 128],
                        start=(kc == 0), stop=(kc == 1)),
                        "t" if hh == NH - 1 and kc == 1 else None)
            pb[2].end_write(tk)
            pb[2].read_from("vector")
            t_atn = P.emit("vector", lambda h, pp=pp: h.tensor_tensor(
                datn[:, pp, :, :],
                pb[2].h[:, 0:512].rearrange("p (a b) -> p a b", a=NH),
                cmT[:, :].unsqueeze(1).broadcast_to([128, NH, 128]), ALU.mult), "v")
            pb[2].add_reader(t_atn)

        grpbank = (pb7, pb[3])
        trbank = (pbT, pb[2])
        trview = (pbT.h[:, 0:256], pb[2].h[:, :].bitcast(BF16)[:, 0:256])

        def pre_C0(tn):
            CS["t_c"] = [None, None]
            CS["t_b"] = [None, None]
            CS["t_q"] = [None, None]
            CS["Qcur"], CS["Qnxt"] = dQa, dQb
            for g in range(2):
                bk = trbank[g]
                bk.begin_write()
                P.wait("tensor", CS["t_A"])
                for j in range(2):
                    hh = 2 * g + j
                    tk = P.emit("tensor", lambda h, hh=hh, j=j, g=g: h.transpose(
                        trview[g][:, j * 128:j * 128 + 128], dA[:, hh, :],
                        eyeb[:, :]), "t" if j == 1 else None)
                bk.end_write(tk)
            for g in range(2):
                bk = trbank[g]
                bk.read_from("scalar")
                CS["t_c"][g] = P.emit("scalar", lambda h, g=g: h.activation(
                    flat(dC[:, 2 * g:2 * g + 2, :]), trview[g][:, 0:256],
                    AF.Copy), "s")
                bk.add_reader(CS["t_c"][g])
            for g in range(2):
                P.wait("vector", CS["t_c"][g])
                CS["t_q"][g] = P.emit("vector", lambda h, g=g, Q=CS["Qcur"]:
                                      h.tensor_tensor(
                    Q[:, 2 * g:2 * g + 2, :], dC[:, 2 * g:2 * g + 2, :],
                    eyeb[:, :].unsqueeze(1).broadcast_to([128, 2, 128]),
                    ALU.add), "v")

        def ch_CMM(lev):
            for g in range(2):
                bk = grpbank[g]
                bk.begin_write()
                P.wait("tensor", CS["t_b"][g] if lev > 0 else CS["t_A"])
                P.wait("tensor", CS["t_c"][g])
                for j in range(2):
                    hh = 2 * g + j
                    src = dB if lev > 0 else dA
                    tk = P.emit("tensor", lambda h, hh=hh, j=j, g=g, src=src:
                                h.matmul(
                        grpbank[g].h[:, j * 128:j * 128 + 128], src[:, hh, :],
                        dC[:, hh, :], start=True, stop=True),
                        "t" if j == 1 else None)
                bk.end_write(tk)
            for g in range(2):
                bk = grpbank[g]
                bk.read_from("scalar")
                CS["t_c"][g] = P.emit("scalar", lambda h, g=g: h.activation(
                    flat(dC[:, 2 * g:2 * g + 2, :]),
                    grpbank[g].h[:, 0:256], AF.Copy), "s")
                bk.add_reader(CS["t_c"][g])

        def ch_TR(lev):
            for g in range(2):
                bk = trbank[g]
                bk.begin_write()
                P.wait("tensor", CS["t_c"][g])
                for j in range(2):
                    hh = 2 * g + j
                    tk = P.emit("tensor", lambda h, hh=hh, j=j, g=g: h.transpose(
                        trview[g][:, j * 128:j * 128 + 128], dC[:, hh, :],
                        eyeb[:, :]), "t" if j == 1 else None)
                bk.end_write(tk)
            for g in range(2):
                bk = trbank[g]
                bk.read_from("scalar")
                CS["t_b"][g] = P.emit("scalar", lambda h, g=g: h.activation(
                    flat(dB[:, 2 * g:2 * g + 2, :]), trview[g][:, 0:256],
                    AF.Copy), "s")
                bk.add_reader(CS["t_b"][g])

        def ch_QMM(lev):
            Qcur = CS["Qcur"]
            for g in range(2):
                bk = grpbank[g]
                bk.begin_write()
                P.wait("tensor", CS["t_b"][g])
                P.wait("tensor", CS["t_q"][g])
                for j in range(2):
                    hh = 2 * g + j
                    P.emit("tensor", lambda h, hh=hh, j=j, g=g, Q=Qcur: h.matmul(
                        grpbank[g].h[:, j * 128:j * 128 + 128], dB[:, hh, :],
                        Q[:, hh, :], start=True, stop=False), None)
                    tk = P.emit("tensor", lambda h, hh=hh, j=j, g=g, Q=Qcur:
                                h.matmul(
                        grpbank[g].h[:, j * 128:j * 128 + 128], eyeb[:, :],
                        Q[:, hh, :], start=False, stop=True),
                        "t" if j == 1 else None)
                bk.end_write(tk)
            for g in range(2):
                bk = grpbank[g]
                bk.read_from("scalar")
                CS["t_q"][g] = P.emit("scalar", lambda h, g=g, Q=CS["Qnxt"]:
                                      h.activation(
                    flat(Q[:, 2 * g:2 * g + 2, :]),
                    grpbank[g].h[:, 0:256], AF.Copy), "s")
                bk.add_reader(CS["t_q"][g])
            CS["Qcur"], CS["Qnxt"] = CS["Qnxt"], CS["Qcur"]

        def pre_uw(tn):
            dTT = CS["Qcur"]
            P.wait("tensor", CS["t_q"][0])
            P.wait("tensor", CS["t_q"][1])
            pb7.begin_write()
            pb[3].begin_write()
            for i in range(8):
                hh, dv = divmod(i, 2)
                bk = pb7 if i < 4 else pb[3]
                tk = P.emit("tensor", lambda h, hh=hh, dv=dv, bk=bk, i=i, dTT=dTT:
                            h.matmul(
                    bk.h[:, (i % 4) * 128:(i % 4) * 128 + 128],
                    dTT[:, hh, :], dvh[:, hh, dv * 128:dv * 128 + 128],
                    start=True, stop=True), "t" if i in (3, 7) else None)
                if i == 3:
                    pb7.end_write(tk)
            pb[3].end_write(tk)
            pb7.read_from("scalar")
            tu1 = P.emit("scalar", lambda h: h.activation(
                flat(du[:, 0:2, :]), pb7.h[:, 0:512], AF.Copy), "s")
            pb7.add_reader(tu1)
            pb[3].read_from("scalar")
            tu2 = P.emit("scalar", lambda h: h.activation(
                flat(du[:, 2:4, :]), pb[3].h[:, 0:512], AF.Copy), "s")
            pb[3].add_reader(tu2)

            pb[2].begin_write()
            pb7.begin_write()
            for i in range(8):
                hh, kc = divmod(i, 2)
                bk = pb[2] if i < 4 else pb7
                tk = P.emit("tensor", lambda h, hh=hh, kc=kc, bk=bk, i=i, dTT=dTT:
                            h.matmul(
                    bk.h[:, (i % 4) * 128:(i % 4) * 128 + 128],
                    dkb[:, hh, kc * 128:kc * 128 + 128], dTT[:, hh, :],
                    start=True, stop=True), "t" if i in (3, 7) else None)
                if i == 3:
                    pb[2].end_write(tk)
            pb7.end_write(tk)
            CS["t_uwmm"] = tk
            pb[2].read_from("scalar")
            tw1 = P.emit("scalar", lambda h: h.activation(
                flat(dwf[:, 0:4, :]), pb[2].h[:, 0:512], AF.Copy), "s")
            pb[2].add_reader(tw1)
            pb7.read_from("scalar")
            tw2 = P.emit("scalar", lambda h: h.activation(
                flat(dwf[:, 4:8, :]), pb7.h[:, 0:512], AF.Copy), "s")
            pb7.add_reader(tw2)
            CS["t_wf"] = P.now("s")
            CS["dTT"] = dTT

        def scan_wS(tc):
            pb[0].begin_write()
            pb[1].begin_write()
            P.wait("tensor", CS["t_wf"])
            P.wait("tensor", CS["t_scan_prev"])
            for hh in range(NH):
                bk = pb[0] if hh < 2 else pb[1]
                for kc in range(2):
                    tk = P.emit("tensor", lambda h, hh=hh, kc=kc, bk=bk: h.matmul(
                        bk.h[:, (hh % 2) * 256:(hh % 2) * 256 + 256],
                        dwf[:, 2 * hh + kc, :], Sbf[:, 2 * hh + kc, :],
                        start=(kc == 0), stop=(kc == 1)),
                        "t" if kc == 1 and hh in (1, 3) else None)
                if hh == 1:
                    pb[0].end_write(tk)
            pb[1].end_write(tk)

        def scan_uadj(tc):
            pb[0].read_from("vector")
            ta1 = P.emit("vector", lambda h: h.tensor_tensor(
                flat(dua[:, 0:2, :]), flat(du[:, 0:2, :]), pb[0].h[:, 0:512],
                ALU.subtract), "v")
            pb[0].add_reader(ta1)
            pb[1].read_from("vector")
            ta2 = P.emit("vector", lambda h: h.tensor_tensor(
                flat(dua[:, 2:4, :]), flat(du[:, 2:4, :]), pb[1].h[:, 0:512],
                ALU.subtract), "v")
            pb[1].add_reader(ta2)
            CS["t_ua"] = P.now("v")

        def scan_oMM(tc):
            tr0 = PAD + tc * 128
            pp = tc % 2
            pb[0].begin_write()
            pb[1].begin_write()
            P.wait("tensor", CS["t_ua"])
            for hh in range(NH):
                bk = pb[0] if hh < 2 else pb[1]
                for kc in range(2):
                    P.emit("tensor", lambda h, hh=hh, kc=kc, bk=bk, tc=tc, tr0=tr0:
                           h.matmul(
                        bk.h[:, (hh % 2) * 256:(hh % 2) * 256 + 256],
                        qf[:, 2 * hh + kc, tr0:tr0 + 128], Sbf[:, 2 * hh + kc, :],
                        start=(kc == 0), stop=False), None)
                tk = P.emit("tensor", lambda h, hh=hh, bk=bk, pp=pp: h.matmul(
                    bk.h[:, (hh % 2) * 256:(hh % 2) * 256 + 256],
                    datn[:, pp, hh, :], dua[:, hh, :], start=False, stop=True),
                    "t" if hh in (1, 3) else None)
                if hh == 1:
                    pb[0].end_write(tk)
            pb[1].end_write(tk)

        def scan_oevac(tc):
            for hh in range(NH):
                bk = pb[0] if hh < 2 else pb[1]
                bk.read_from("vector")
                tr = P.emit("vector", lambda h, hh=hh, bk=bk, tc=tc:
                            h.scalar_tensor_tensor(
                    ACC[:, tc, hh * 256:hh * 256 + 256],
                    bk.h[:, (hh % 2) * 256:(hh % 2) * 256 + 256],
                    psc[:, tc, 36 + hh:37 + hh],
                    ACC[:, tc, hh * 256:hh * 256 + 256], ALU.mult, ALU.add), "v")
                bk.add_reader(tr)

        def scan_dS(tc, rnd):
            h0, h1 = 2 * rnd, 2 * rnd + 1
            pb[4].begin_write()
            pb[5].begin_write()
            P.wait("tensor", CS["t_ua"])
            for hh, bk in ((h0, pb[4]), (h1, pb[5])):
                for kc in range(2):
                    tk = P.emit("tensor", lambda h, hh=hh, kc=kc, bk=bk, tc=tc:
                                h.matmul(
                        bk.h[:, kc * 256:kc * 256 + 256],
                        ktok[:, tc, hh * 256 + kc * 128:hh * 256 + kc * 128 + 128],
                        dua[:, hh, :], start=True, stop=True),
                        "t" if kc == 1 else None)
                if hh == h0:
                    pb[4].end_write(tk)
            pb[5].end_write(tk)
            for hh, bk in ((h0, pb[4]), (h1, pb[5])):
                bk.read_from("vector")
                ts1 = P.emit("vector", lambda h, hh=hh, bk=bk: h.tensor_tensor(
                    flat(S32[:, 2 * hh:2 * hh + 2, :]),
                    flat(S32[:, 2 * hh:2 * hh + 2, :]),
                    bk.h[:, 0:512], ALU.add), "v")
                bk.add_reader(ts1)
                P.wait("scalar", ts1)
                CS["t_scan_prev"] = P.emit("scalar", lambda h, hh=hh: h.activation(
                    flat(Sbf[:, 2 * hh:2 * hh + 2, :]),
                    flat(S32[:, 2 * hh:2 * hh + 2, :]), AF.Copy), "s")

        pre_scales(0)
        pre_Gmm(0)
        pre_Aevac(0)
        pre_attnT(0)
        pre_C0(0)
        for lev in range(NLEV):
            ch_CMM(lev)
            ch_TR(lev)
            ch_QMM(lev)
        pre_uw(0)

        if half == 0:
            for nm, buf in (("tt0", CS["dTT"]), ("u0", du),
                            ("wf0", dwf), ("atn0", datn[:, 0])):
                if nm in tapd:
                    P.wait("scalar", CS["t_wf"])
                    for a in range(NH):
                        P.emit("scalar", lambda h, nm=nm, a=a, buf=buf: h.dma_start(
                            out=tapd[nm].ap()[a], in_=buf[:, a, :]), "ds")

        for tc in range(NC):
            tn = tc + 1
            have = tn < NC
            if have:
                pre_scales(tn)
            scan_wS(tc)
            if have:
                pre_Gmm(tn)
            scan_uadj(tc)
            if have:
                pre_Aevac(tn)
                pre_attnT(tn)
                pre_C0(tn)
                ch_CMM(0)
            scan_oMM(tc)
            if have:
                ch_TR(0)
            scan_oevac(tc)
            scan_dS(tc, 0)
            if have:
                ch_QMM(0)
            scan_dS(tc, 1)
            if have:
                for lev in range(1, NLEV):
                    ch_CMM(lev)
                    ch_TR(lev)
                    ch_QMM(lev)
                pre_uw(tn)

            ckpt("delta" + str(half) + "_" + str(tc))
        t_scan_prev = CS["t_scan_prev"]
        P.barrier()
        ex.close()
        qx.close()
        ckpt("delta" + str(half))

        if "accfin" in tapd:
            P.wait("scalar", P.now("v"))
            for tc in range(NC):
                P.emit("scalar", lambda h, tc=tc: h.dma_start(
                    out=tapd["accfin"].ap()[half, tc], in_=ACC[:, tc, :]), "ds")

        fx = ExitStack()
        sbf_ = lambda n, sh, dt: fx.enter_context(
            nc.sbuf_tensor(f"{n}_{half}", sh, dt, side="left"))
        ot = sbf_("ot", [128, D], BF16)
        of = sbf_("of", [128, CH, TH], BF16)
        wof = sbf_("wof", [128, CH, TH], F32)
        xr = sbf_("xr", [128, D], F32)
        xnt = sbf_("xnt", [128, D], F32)
        sqs = sbf_("sqs", [128, D], BF16)

        t_delta = P.now("v")
        t_ot_free = None
        for tc in range(NC):
            P.wait("vector", t_delta)
            for hh in range(NH):
                P.emit("vector", lambda h, tc=tc, hh=hh: h.tensor_tensor(
                    scrap[:, 256 * hh:256 * hh + 256],
                    ACC[:, tc, hh * 256:hh * 256 + 256],
                    ACC[:, tc, hh * 256:hh * 256 + 256], ALU.mult), "v")
            for hh in range(NH):
                P.emit("vector", lambda h, tc=tc, hh=hh: h.tensor_reduce(
                    psc[:, tc, 40 + hh:41 + hh], scrap[:, 256 * hh:256 * hh + 256],
                    AXX, ALU.add), "v")
            P.wait("scalar", P.now("v"))
            t_r = P.emit("scalar", lambda h, tc=tc: h.activation(
                psc[:, tc, 40:44], psc[:, tc, 40:44], AF.Sqrt,
                scale=1.0 / 256, bias=ecst[:, 0:1]), "s")
            P.wait("vector", t_r)
            P.emit("vector", lambda h, tc=tc: h.reciprocal(
                psc[:, tc, 40:44], psc[:, tc, 40:44]), "v")
            P.vgap()
            P.wait("vector", t_ot_free)
            for hh in range(NH):
                P.emit("vector", lambda h, tc=tc, hh=hh: h.tensor_scalar(
                    ot[:, hh * 256:hh * 256 + 256], ACC[:, tc, hh * 256:hh * 256 + 256],
                    psc[:, tc, 40 + hh:41 + hh], None, ALU.mult), "v")
            t_otk = P.now("v")
            P.wait("tensor", t_otk)
            pbT.begin_write()
            for cc in range(CH):
                tk = P.emit("tensor", lambda h, cc=cc: h.transpose(
                    pbT.h[:, cc * 128:(cc + 1) * 128], ot[:, cc * 128:(cc + 1) * 128],
                    eyeb[:, :]), "t" if cc == CH - 1 else None)
            pbT.end_write(tk)
            t_ot_free = tk
            pbT.read_from("scalar")
            tev = P.emit("scalar", lambda h, tc=tc: h.activation(
                flat(of[:, :, tc * 128:tc * 128 + 128]), pbT.h[:, :], AF.Copy), "s")
            pbT.add_reader(tev)
        t_of = P.now("s")
        ckpt("of" + str(half))

        if "of" in tapd:
            P.wait("scalar", t_of)
            for cc in range(CH):
                P.emit("scalar", lambda h, cc=cc: h.dma_start(
                    out=tapd["of"].ap()[half, cc], in_=of[:, cc, :]), "ds")

        P.wait("tensor", t_of)
        P.wait("tensor", t_wofold if half == 0 else t_of)
        for tt in range(NT):
            for mc in range(CH):
                bk = pb[mc % 2]
                bk.begin_write()
                for kc in range(CH):
                    tk = P.emit("tensor", lambda h, mc=mc, kc=kc, tt=tt: h.matmul(
                        bk.h[:, 0:512], wo[:, kc, mc * 128:(mc + 1) * 128],
                        of[:, kc, tt * 512:tt * 512 + 512],
                        start=(kc == 0), stop=(kc == CH - 1)),
                        "t" if kc == CH - 1 else None)
                bk.end_write(tk)
                bk.read_from("scalar")
                tev = P.emit("scalar", lambda h, mc=mc, tt=tt: h.activation(
                    wof[:, mc, tt * 512:tt * 512 + 512], bk.h[:, 0:512], AF.Copy), "s")
                bk.add_reader(tev)
        t_wof = P.now("s")
        ckpt("wof" + str(half))

        for tc in range(NC):
            t0 = half * TH + tc * 128
            t_x = P.emit("sync", lambda h, t0=t0: h.dma_start(
                out=xr[:, :], in_=x_d.ap()[t0:t0 + 128, :]), "dl")
            pb[4].begin_write()
            pb[5].begin_write()
            P.wait("tensor", t_wof)
            for cc in range(CH):
                bk = pb[4] if cc < 4 else pb[5]
                tk = P.emit("tensor", lambda h, cc=cc, tc=tc, bk=bk: h.transpose(
                    bk.h[:, (cc % 4) * 128:(cc % 4) * 128 + 128],
                    wof[:, cc, tc * 128:tc * 128 + 128], eyef[:, :]),
                    "t" if cc in (3, 7) else None)
                if cc == 3:
                    pb[4].end_write(tk)
            pb[5].end_write(tk)
            P.wait("vector", t_x)
            for g, bk in ((0, pb[4]), (1, pb[5])):
                bk.read_from("vector")
                tr = P.emit("vector", lambda h, g=g, bk=bk: h.tensor_tensor(
                    xnt[:, g * 512:g * 512 + 512], xr[:, g * 512:g * 512 + 512],
                    bk.h[:, 0:512], ALU.add), "v")
                bk.add_reader(tr)
            t_xn = P.now("v")
            P.wait("scalar", t_xn)
            ci = half * NC + tc
            P.emit("scalar", lambda h, ci=ci: h.activation(
                sqs[:, :], xnt[:, :], AF.Copy,
                accum_out=stat[:, 64 + ci:65 + ci]), "s")
            P.emit("scalar", lambda h, ci=ci: h.activation(
                sqs[:, :], xnt[:, :], AF.Square,
                accum_out=stat[:, 80 + ci:81 + ci]), "s")
            t_st = P.emit("scalar", lambda h, t0=t0: h.dma_start(
                out=xnew_d.ap()[t0:t0 + 128, :], in_=xnt[:, :]), "ds")
            if "xnew" in tapd:
                P.emit("scalar", lambda h, t0=t0: h.dma_start(
                    out=tapd["xnew"].ap()[t0:t0 + 128, :], in_=xnt[:, :]), "ds")
            P.wait("sync", t_st)
            P.wait("vector", t_st)
        P.barrier()
        fx.close()
        hs.close()
        ckpt("resid" + str(half))

    P.barrier()
    att.close()
    ckpt("attn")

    fz = ExitStack()
    sbf2 = lambda n, sh, dt: fz.enter_context(
        nc.sbuf_tensor(n, sh, dt, side="left"))
    wgs = sbf2("wgs", [128, CH, HID], BF16)
    wus = sbf2("wus", [128, CH, HID], BF16)
    wds = sbf2("wds", [128, HC, D], BF16)
    hbuf = sbf2("hbuf", [128, HC, FT], BF16)
    ln2f = sbf2("ln2f", [128, 2, CH, FT], BF16)
    ga = sbf2("ga", [128, FT], BF16)
    lnt2 = sbf2("lnt2", [128, 2, D], BF16)
    xr2 = sbf2("xr2", [128, 2, 2, D], F32)
    dwn = sbf2("dwn", [128, CH, FT], F32)
    ont = sbf2("ont", [128, D], F32)

    t_wgc = {}
    t_wuc = {}
    t_wdc = {}
    for kc in range(CH):
        t_wgc[kc] = P.emit("gpsimd", lambda h, kc=kc: h.dma_start(
            out=wgs[:, kc, :], in_=wg_d.ap()[kc]), "dg")
        t_wuc[kc] = P.emit("gpsimd", lambda h, kc=kc: h.dma_start(
            out=wus[:, kc, :], in_=wu_d.ap()[kc]), "dg")
    for kc in range(HC):
        t_wdc[kc] = P.emit("gpsimd", lambda h, kc=kc: h.dma_start(
            out=wds[:, kc, :], in_=wd_d.ap()[kc]), "dg")

    NCH2 = L // 128
    P.wait("vector", P.now("s"))
    P.emit("vector", lambda h: h.tensor_scalar(
        stat[:, 64:64 + NCH2], stat[:, 64:64 + NCH2], 1.0 / D, None,
        ALU.mult), "v")
    P.emit("vector", lambda h: h.tensor_scalar(
        stat[:, 80:80 + NCH2], stat[:, 80:80 + NCH2], 1.0 / D, None,
        ALU.mult), "v")
    P.emit("vector", lambda h: h.tensor_tensor(
        stat[:, 96:96 + NCH2], stat[:, 64:64 + NCH2], stat[:, 64:64 + NCH2],
        ALU.mult), "v")
    P.vgap()
    P.emit("vector", lambda h: h.tensor_tensor(
        stat[:, 96:96 + NCH2], stat[:, 80:80 + NCH2], stat[:, 96:96 + NCH2],
        ALU.subtract), "v")
    P.wait("scalar", P.now("v"))
    t_rs2 = P.emit("scalar", lambda h: h.activation(
        stat[:, 96:96 + NCH2], stat[:, 96:96 + NCH2], AF.Sqrt,
        bias=ecst[:, 0:1]), "s")
    P.wait("vector", t_rs2)
    P.emit("vector", lambda h: h.reciprocal(
        stat[:, 80:80 + NCH2], stat[:, 96:96 + NCH2]), "v")
    P.vgap()
    t_chain2 = P.now("v")

    t_lnt2_free = [None, None]
    t_xr2_free = [None, None]
    t_tile_prev = None

    def emit_ln_tile(ft):
        p = ft % 2
        for tj in range(2):
            t0 = ft * FT + tj * 128
            ci = 2 * ft + tj
            P.wait("sync", t_xr2_free[p])
            t_x = P.emit("sync", lambda h, t0=t0, tj=tj, p=p: h.dma_start(
                out=xr2[:, p, tj, :], in_=xnew_d.ap()[t0:t0 + 128, :]), "dl")
            P.wait("vector", t_x)
            P.wait("vector", t_chain2 if ci == 0 else None)
            P.wait("vector", t_lnt2_free[ci % 2])
            t_ln = P.emit("vector", lambda h, tj=tj, ci=ci, p=p: h.tensor_scalar(
                lnt2[:, ci % 2, :], xr2[:, p, tj, :], stat[:, 64 + ci:65 + ci],
                stat[:, 80 + ci:81 + ci], ALU.subtract, ALU.mult), "v")
            P.wait("tensor", t_ln)
            pbT.begin_write()
            for cc in range(CH):
                tk = P.emit("tensor", lambda h, cc=cc, ci=ci: h.transpose(
                    pbT.h[:, cc * 128:(cc + 1) * 128],
                    lnt2[:, ci % 2, cc * 128:(cc + 1) * 128], eyeb[:, :]),
                    "t" if cc == CH - 1 else None)
            pbT.end_write(tk)
            t_lnt2_free[ci % 2] = tk
            pbT.read_from("scalar")
            tev = P.emit("scalar", lambda h, tj=tj, p=p: h.activation(
                flat(ln2f[:, p, :, tj * 128:tj * 128 + 128]), pbT.h[:, :],
                AF.Copy), "s")
            pbT.add_reader(tev)
        return P.now("s")

    t_ln2p = [None, None]
    t_ln2p[0] = emit_ln_tile(0)
    for ft in range(NFT):
        f0 = ft * FT
        p = ft % 2
        if "ln2f" in tapd:
            P.wait("scalar", t_ln2p[p])
            for cc in range(CH):
                P.emit("scalar", lambda h, cc=cc, ft=ft, p=p: h.dma_start(
                    out=tapd["ln2f"].ap()[ft, cc], in_=ln2f[:, p, cc, :]), "ds")

        P.wait("tensor", t_ln2p[p])
        P.wait("vector", t_tile_prev)
        for mc in range(HC):
            bg_ = pb[(mc % 2) * 2]
            bu_ = pb[(mc % 2) * 2 + 1]
            bg_.begin_write()
            bu_.begin_write()
            for kc in range(CH):
                P.wait("tensor", t_wgc[kc])
                P.wait("tensor", t_wuc[kc])
                tkg = P.emit("tensor", lambda h, mc=mc, kc=kc, bg_=bg_, p=p:
                             h.matmul(
                    bg_.h[:, 0:FT], wgs[:, kc, mc * 128:(mc + 1) * 128],
                    ln2f[:, p, kc, :], start=(kc == 0), stop=(kc == CH - 1)),
                    "t" if kc == CH - 1 else None)
                tk = P.emit("tensor", lambda h, mc=mc, kc=kc, bu_=bu_, p=p:
                            h.matmul(
                    bu_.h[:, 0:FT], wus[:, kc, mc * 128:(mc + 1) * 128],
                    ln2f[:, p, kc, :], start=(kc == 0), stop=(kc == CH - 1)),
                    "t" if kc == CH - 1 else None)
            bg_.end_write(tkg)
            bu_.end_write(tk)
            bg_.read_from("scalar")
            t_g = P.emit("scalar", lambda h, bg_=bg_: h.activation(
                ga[:, :], bg_.h[:, 0:FT], AF.Silu, scale=QS), "s")
            bg_.add_reader(t_g)
            bu_.read_from("vector")
            P.wait("vector", t_g)
            t_h = P.emit("vector", lambda h, mc=mc, bu_=bu_: h.scalar_tensor_tensor(
                hbuf[:, mc, :], bu_.h[:, 0:FT], QS, ga[:, :], ALU.mult, ALU.mult), "v")
            bu_.add_reader(t_h)
            P.wait("scalar", t_h)
        t_hb = P.now("v")
        ckpt("gup" + str(ft))
        if "hbuf" in tapd:
            P.wait("scalar", t_hb)
            for mc in range(HC):
                P.emit("scalar", lambda h, mc=mc, ft=ft: h.dma_start(
                    out=tapd["hbuf"].ap()[ft, mc], in_=hbuf[:, mc, :]), "ds")

        if ft + 1 < NFT:
            t_ln2p[(ft + 1) % 2] = emit_ln_tile(ft + 1)

        P.wait("tensor", t_hb)
        for kc in range(HC):
            P.wait("tensor", t_wdc[kc])
        for mc in range(CH):
            bk = pb[2 + mc % 4]
            bk.begin_write()
            for kc in range(HC):
                tk = P.emit("tensor", lambda h, kc=kc, mc=mc, bk=bk: h.matmul(
                    bk.h[:, 0:FT], wds[:, kc, mc * 128:(mc + 1) * 128],
                    hbuf[:, kc, :], start=(kc == 0), stop=(kc == HC - 1)),
                    "t" if kc == HC - 1 else None)
            bk.end_write(tk)
            bk.read_from("scalar")
            tev = P.emit("scalar", lambda h, mc=mc, bk=bk: h.activation(
                dwn[:, mc, :], bk.h[:, 0:FT], AF.Copy), "s")
            bk.add_reader(tev)
        t_dwn = P.now("s")
        ckpt("dwn" + str(ft))

        for tj in range(2):
            t0 = f0 + tj * 128
            pb[0].begin_write()
            pb[1].begin_write()
            P.wait("tensor", t_dwn)
            for cc in range(CH):
                bk = pb[0] if cc < 4 else pb[1]
                tk = P.emit("tensor", lambda h, cc=cc, tj=tj, bk=bk: h.transpose(
                    bk.h[:, (cc % 4) * 128:(cc % 4) * 128 + 128],
                    dwn[:, cc, tj * 128:tj * 128 + 128], eyef[:, :]),
                    "t" if cc in (3, 7) else None)
                if cc == 3:
                    pb[0].end_write(tk)
            pb[1].end_write(tk)
            for g, bk in ((0, pb[0]), (1, pb[1])):
                bk.read_from("vector")
                tr = P.emit("vector", lambda h, g=g, tj=tj, bk=bk, p=p:
                            h.scalar_tensor_tensor(
                    ont[:, g * 512:g * 512 + 512], bk.h[:, 0:512], QS,
                    xr2[:, p, tj, g * 512:g * 512 + 512], ALU.mult, ALU.add), "v")
                bk.add_reader(tr)
            t_o = P.now("v")
            P.wait("scalar", t_o)
            t_ost = P.emit("scalar", lambda h, t0=t0: h.dma_start(
                out=out_d.ap()[t0:t0 + 128, :], in_=ont[:, :]), "ds")
            P.wait("vector", t_ost)
        t_xr2_free[p] = t_o
        t_tile_prev = P.now("t")

    P.wait_queue("sync", "ds")
    P.wait_queue("sync", "dl")
    P.wait_queue("sync", "dg")

    assemble(nc, P)
    return nc, tapd


def assemble(nc, P):
    import bisect as _b
    keep = P.finalize_maps()
    sems = P.sems

    def runner(eng_name):
        def run(h):
            for rec in P.recs[eng_name]:
                kind = rec[0]
                if kind == "dma":
                    _, fn, key = rec
                    fn(h).then_inc(sems[key], 16)
                elif kind == "op":
                    _, fn, inc, pval = rec
                    ins = fn(h)
                    if inc is not None and pval in keep[inc][1]:
                        ins.then_inc(sems[inc], 1)
                else:
                    _, sem, val = rec
                    if sem in keep:
                        kept, m = keep[sem]
                        i = _b.bisect_left(kept, val)
                        if i >= len(kept):
                            i = len(kept) - 1
                        h.wait_ge(sems[sem], m[kept[i]])
                    else:
                        h.wait_ge(sems[sem], val)
        return run

    with nc.Block() as block:
        block.sync(runner("sync"))
        block.gpsimd(runner("gpsimd"))
        block.vector(runner("vector"))
        block.scalar(runner("scalar"))
        block.tensor(runner("tensor"))


def host_prep(inputs, L=2048):
    f32 = np.float32
    Wq = inputs["L0_attn_Wq"]
    Wk = inputs["L0_attn_Wk"]
    Wv = inputs["L0_attn_Wv"]
    Wo = inputs["L0_attn_Wo"]
    Wb = inputs["L0_attn_Wb"]
    gw = inputs["L0_attn_gate_w"].reshape(20, D)
    gb = inputs["L0_attn_gate_b"].reshape(20)
    conv_q = inputs["L0_attn_conv_q"]
    conv_k = inputs["L0_attn_conv_k"]
    conv_v = inputs["L0_attn_conv_v"]
    fir_s = inputs["L0_attn_fir_s"].reshape(D, 3)
    fir_m = inputs["L0_attn_fir_m"].reshape(D, 15)
    fir_l = inputs["L0_attn_fir_l"].reshape(D, 63)

    def wt(w):
        return np.ascontiguousarray(np.asarray(w, f32).T).reshape(CH, 128, D)

    gbw = np.concatenate([np.asarray(gw, f32), np.asarray(Wb, f32)], axis=0)
    wgbT = np.ascontiguousarray(gbw.T).reshape(CH, 128, 24)

    diag = np.zeros((CH, NTAP, 128, 128), dtype=f32)
    idx = np.arange(128)
    filts = [np.asarray(conv_q, f32), np.asarray(conv_k, f32),
             np.asarray(conv_v, f32), np.asarray(fir_s, f32),
             np.asarray(fir_m, f32), np.asarray(fir_l, f32)]
    for cc in range(CH):
        t = 0
        for fm in filts:
            K = fm.shape[1]
            for j in range(K):
                diag[cc, t, idx, idx] = fm[cc * 128:(cc + 1) * 128, j]
                t += 1
    assert t == NTAP

    m = {
        "wqT": wt(Wq), "wkT": wt(Wk), "wvT": wt(Wv), "woT": wt(Wo),
        "wgbT": wgbT, "diag_all": diag,
        "gb_exp": np.ascontiguousarray(np.broadcast_to(gb, (128, 20))).astype(f32),
        "log_temp": np.asarray(inputs["L0_attn_log_temp"], f32).reshape(4, 1),
        "eps_param": np.asarray(inputs["L0_attn_eps_param"], f32).reshape(4, 1),
        "nw": np.ascontiguousarray(
            np.asarray(inputs["L0_attn_norm_w"], f32).reshape(2, 128).T),
        "wgT": np.ascontiguousarray(
            np.asarray(inputs["L0_ffn_w_gate"]).T).reshape(CH, 128, HID),
        "wuT": np.ascontiguousarray(
            np.asarray(inputs["L0_ffn_w_up"]).T).reshape(CH, 128, HID),
        "wdT": np.ascontiguousarray(
            np.asarray(inputs["L0_ffn_w_down"]).T).reshape(HC, 128, D),
    }
    return m


def kernel(**inputs):
    x = np.asarray(inputs["L0_attn_hidden_states"], np.float32)
    B, L, _ = x.shape
    shared = host_prep(inputs, L)
    nc, _ = build(L)
    in_maps = []
    for b in range(B):
        im = dict(shared)
        im["x"] = np.ascontiguousarray(x[b])
        in_maps.append(im)
    res = run_bass_kernel_spmd(nc, in_maps, core_ids=list(range(B)))
    out = np.stack([np.asarray(res.results[b]["out"]) for b in range(B)], axis=0)
    return out.astype(np.float32)
